# version 05
# baseline (speedup 1.0000x reference)
import numpy as np
import ml_dtypes
import concourse.bass as bass
import concourse.mybir as mybir
from concourse.bass_utils import run_bass_kernel_spmd
from contextlib import ExitStack

F32 = mybir.dt.float32
BF16 = mybir.dt.bfloat16
I32 = mybir.dt.int32
OP = mybir.AluOpType
AF = mybir.ActivationFunctionType
AX = mybir.AxisListType

D = 1024
H = 16
MLP = 4096
N = 4096
TT = 512
NT = N // TT
MC = TT // 128
KC = D // 128
CC = MLP // 128
G = TT // 8
WIN = TT + 2
MAGIC = float(1.5 * 2 ** 23)
RSQRT_MAGIC = 0x5f3759df
NCORES = 8


class Emitter:
    def __init__(self):
        self.lists = {e: [] for e in ("sync", "tensor", "vector", "scalar", "gpsimd")}
        self.sems = {}
        self.cnt = {}

    def op(self, eng, fn, inc=None, n=1):
        if inc is not None:
            self.cnt[inc] += n
            sem = self.sems[inc]
            self.lists[eng].append(lambda e, fn=fn, sem=sem, n=n: fn(e).then_inc(sem, n))
        else:
            self.lists[eng].append(lambda e, fn=fn: fn(e))

    def wait(self, eng, name, val=None):
        v = self.cnt[name] if val is None else val
        sem = self.sems[name]
        self.lists[eng].append(lambda e, sem=sem, v=v: e.wait_ge(sem, v))


def build_program(nt_limit=NT, do_ffn=True, dump=(), skip=()):
    nc = bass.Bass("TRN2", target_bir_lowering=False)
    es = ExitStack()
    E = es.enter_context

    dr = {}

    def dt(name, shape, dtype, kind="Internal"):
        dr[name] = nc.dram_tensor(name, shape, dtype, kind=kind)
        return dr[name]

    dt("x", [N, D], F32, "ExternalInput")
    for nm, shp in [("wq", [D, D]), ("wk", [D, D]), ("wv", [D, D]), ("wo", [D, D]),
                    ("fcw", [D, MLP]), ("projw", [MLP, D]),
                    ("w1T", [D, MLP]), ("w2T", [MLP, D]),
                    ("ln1w", [D]), ("ln1b", [D]), ("ln2w", [D]), ("ln2b", [D]),
                    ("bq", [D]), ("bk", [D]), ("bv", [D]), ("bo", [D]),
                    ("fcb", [MLP]), ("projb", [D]),
                    ("convw0", [MLP]), ("convw1", [MLP]), ("convw2", [MLP]),
                    ("convb", [MLP])]:
        dt(nm, shp, F32, "ExternalInput")
    dt("identb", [128, 128], BF16, "ExternalInput")
    dt("identf", [128, 128], F32, "ExternalInput")
    dt("maskS", [128, 128], F32, "ExternalInput")
    dt("y", [N, D], F32, "ExternalOutput")

    dt("c_wq", [D, D], BF16)
    dt("c_wk", [D, D], BF16)
    dt("c_wv", [D, D], BF16)
    dt("c_wo", [D, D], BF16)
    dt("c_fcw", [KC * CC * 128 * 128], BF16)
    dt("c_projw", [MLP, D], BF16)
    dt("c_w1t", [KC * CC * 128 * 128], BF16)
    dt("c_w2t", [MLP, D], BF16)
    dt("b_fq", [TT, D], BF16)
    dt("b_fk", [TT, D], BF16)
    dt("b_v", [TT, D], BF16)
    dt("b_at", [128, 64 * G], BF16)
    dt("b_row", [D], F32)

    sb = {}

    def st(name, shape, dtype):
        sb[name] = E(nc.sbuf_tensor("s_" + name, shape, dtype))
        return sb[name]

    st("identb", [128, 128], BF16)
    st("identf", [128, 128], F32)
    st("maskS", [128, 128], F32)
    st("onesb", [128, 1], BF16)
    st("onesf", [128, 1], F32)
    st("onesrow", [1, 128], F32)
    st("magici", [128, 16], I32)
    st("lnw_cols", [128, 2 * KC], F32)
    st("ln1b_col", [128, KC], BF16)
    st("ln2b_col", [128, KC], BF16)
    st("bias_sc", [128, KC], F32)
    st("fcb_col", [128, CC], F32)
    st("convw_col", [128, 3 * CC], F32)
    st("convb_col", [128, CC], F32)
    st("mbar", [128, 4], F32)
    st("bq_rep", [128, D], BF16)
    st("bk_rep", [128, D], BF16)
    st("bv_rep", [128, D], BF16)
    st("bo_rep", [128, D], BF16)
    st("projb_rep", [128, D], BF16)
    st("wbuf", [128, KC * D], BF16)
    st("wslc", [128, 2 * D], BF16)
    st("wslc2", [128, 2 * D], BF16)
    st("x_tm", [128, MC * D], F32)
    st("xmid_tm", [128, MC * D], F32)
    st("x1a", [128, MC * D], F32)
    st("x1b", [128, MC * D], F32)
    st("tm_sc", [128, MC * D], BF16)
    st("fm_sc", [128, KC * TT], BF16)
    st("stats", [128, 16], F32)
    st("stats2", [128, 16], F32)
    st("sc_a", [128, 16], F32)
    st("sc_b", [128, 16], F32)
    st("sc_c", [128, 16], F32)
    st("qscale", [128, 8], F32)
    st("q2s", [128, 8], F32)
    st("dbgk", [128, 256], F32)
    st("junk", [128, D], F32)
    st("ju2", [128, 512], F32)
    st("ju3", [128, 512], F32)

    B = {}
    for i in range(8):
        B[i] = E(nc.psum_tensor(f"bank{i}", [128, 512], F32))

    em = Emitter()
    for s in ["ldc", "wdma", "xdma", "ydma", "bnc", "gth", "mm", "tr", "ev",
              "sv", "gp", "act"]:
        em.sems[s] = E(nc.semaphore(s))
        em.cnt[s] = 0

    def vop(fn, inc="ev"):
        em.op("vector", fn, inc)

    def vopd(fn, inc="ev"):
        em.op("vector", fn, None)
        em.op("vector", lambda e: e.drain(), inc)

    def sop(fn, inc="act"):
        em.op("scalar", fn, inc)

    def sopd(fn, inc="act"):
        em.op("scalar", fn, None)
        em.op("scalar", lambda e: e.drain(), inc)

    def gop(fn, inc="gp"):
        em.op("gpsimd", fn, inc)

    def top(fn, inc="mm"):
        em.op("tensor", fn, inc)

    def dma(eng, out_ap, in_ap, inc):
        em.op(eng, lambda e, o=out_ap, i=in_ap: e.dma_start(out=o, in_=i), inc, 16)

    def dmas(eng, out_ap, in_ap, inc):
        em.op(eng, lambda e, o=out_ap, i=in_ap: e.dma_start(
            out=o, in_=i, allow_slow_non_contiguous=True), inc, 16)

    pro_es = ExitStack()
    sb["wf32b"] = pro_es.enter_context(nc.sbuf_tensor("s_wf32b", [128, MLP], F32))
    sb["wbf"] = pro_es.enter_context(nc.sbuf_tensor("s_wbf", [128, 2 * MLP], BF16))

    for nm in ["identb", "identf", "maskS"]:
        dma("sync", sb[nm][:], dr[nm].ap(), "ldc")
    for j, nm in enumerate(["ln1w", "ln2w"]):
        dmas("sync", bass.AP(sb["lnw_cols"], j * KC, [[2 * KC, 128], [1, KC]]),
            bass.AP(dr[nm], 0, [[1, 128], [128, KC]]), "ldc")
    dmas("sync", bass.AP(sb["convb_col"], 0, [[CC, 128], [1, CC]]),
        bass.AP(dr["convb"], 0, [[1, 128], [128, CC]]), "ldc")
    dmas("gpsimd", sb["bo_rep"][:], bass.AP(dr["bo"], 0, [[0, 128], [1, D]]), "ldc")
    dmas("gpsimd", sb["projb_rep"][:], bass.AP(dr["projb"], 0, [[0, 128], [1, D]]), "ldc")
    dmas("gpsimd", sb["ln1b_col"][:], bass.AP(dr["ln1b"], 0, [[1, 128], [128, KC]]), "ldc")
    dmas("gpsimd", sb["ln2b_col"][:], bass.AP(dr["ln2b"], 0, [[1, 128], [128, KC]]), "ldc")
    em.wait("vector", "ldc")
    vop(lambda e: e.memset(sb["onesb"][:], 1.0), None)
    vop(lambda e: e.memset(sb["onesf"][:], 1.0), None)
    vop(lambda e: e.memset(sb["magici"][:], RSQRT_MAGIC), None)
    vop(lambda e: e.memset(sb["onesrow"][:], 1.0))

    def build_attn_cache(wname, cname, borig, brep):
        dmas("sync", bass.AP(sb["bias_sc"], 0, [[KC, 128], [1, KC]]),
             bass.AP(dr[borig], 0, [[1, 128], [128, KC]]), "wdma")
        em.wait("vector", "wdma")
        for k in range(KC):
            dma("sync", sb["wf32b"][:, 0:D], dr[wname].ap()[128 * k:128 * k + 128, :],
                "wdma")
            em.wait("vector", "wdma")
            lcol = sb["lnw_cols"][:, k:k + 1]
            vop(lambda e, lcol=lcol: e.tensor_scalar(
                sb["wbf"][:, 0:D], sb["wf32b"][:, 0:D], lcol, None, OP.mult), None)
            vopd(lambda e: e.tensor_copy(sb["wbf"][:, D:2 * D], sb["wf32b"][:, 0:D]))
            em.wait("sync", "ev")
            dma("sync", dr[cname].ap()[128 * k:128 * k + 128, :], sb["wbf"][:, 0:D],
                "wdma")
            em.wait("tensor", "ev")
            for c in range(KC):
                top(lambda t, k=k, c=c: t.matmul(
                    B[0][:, c:c + 1], sb["wbf"][:, D + 128 * c:D + 128 * c + 128],
                    sb["ln1b_col"][:, k:k + 1], start=True, stop=True))
            em.wait("vector", "mm")
            vopd(lambda e: e.tensor_tensor(sb["bias_sc"][:], B[0][:, 0:KC],
                                           sb["bias_sc"][:], OP.add))
        em.wait("sync", "ev")
        dmas("sync", bass.AP(dr["b_row"], 0, [[1, 128], [128, KC]]),
            bass.AP(sb["bias_sc"], 0, [[KC, 128], [1, KC]]), "wdma")
        em.wait("gpsimd", "wdma")
        dma("gpsimd", brep[:], bass.AP(dr["b_row"], 0, [[0, 128], [1, D]]), "wdma")

    if "attn" not in skip:
        build_attn_cache("wq", "c_wq", "bq", sb["bq_rep"])
        build_attn_cache("wk", "c_wk", "bk", sb["bk_rep"])
        build_attn_cache("wv", "c_wv", "bv", sb["bv_rep"])

    dmas("sync", bass.AP(sb["fcb_col"], 0, [[CC, 128], [1, CC]]),
         bass.AP(dr["fcb"], 0, [[1, 128], [128, CC]]), "wdma")
    em.wait("vector", "wdma")
    for k in range(KC):
        dma("sync", sb["wf32b"][:], dr["fcw"].ap()[128 * k:128 * k + 128, :], "wdma")
        em.wait("vector", "wdma")
        lcol = sb["lnw_cols"][:, KC + k:KC + k + 1]
        vop(lambda e, lcol=lcol: e.tensor_scalar(
            sb["wbf"][:, 0:MLP], sb["wf32b"][:], lcol, None, OP.mult), None)
        vopd(lambda e: e.tensor_copy(sb["wbf"][:, MLP:2 * MLP], sb["wf32b"][:]))
        em.wait("sync", "ev")
        dma("sync", bass.AP(dr["c_fcw"], k * 16384,
                            [[128, 128], [KC * 16384, CC], [1, 128]]),
            bass.AP(sb["wbf"], 0, [[2 * MLP, 128], [128, CC], [1, 128]]), "wdma")
        em.wait("tensor", "ev")
        for c in range(CC):
            top(lambda t, k=k, c=c: t.matmul(
                B[1][:, c:c + 1], sb["wbf"][:, MLP + 128 * c:MLP + 128 * c + 128],
                sb["ln2b_col"][:, k:k + 1], start=True, stop=True))
        em.wait("vector", "mm")
        vopd(lambda e: e.tensor_tensor(sb["fcb_col"][:], B[1][:, 0:CC],
                                       sb["fcb_col"][:], OP.add))

    if "casts" not in skip:
        for src_, dst_ in [("wo", "c_wo"), ("projw", "c_projw")]:
            nr = dr[src_].shape[0]
            for k in range(0, nr, 128):
                em.wait("sync", "ev")
                dma("sync", sb["wf32b"][:, 0:D],
                    dr[src_].ap()[k:k + 128, :], "wdma")
                em.wait("vector", "wdma")
                vopd(lambda e: e.tensor_copy(sb["wbf"][:, 0:D],
                                             sb["wf32b"][:, 0:D]))
                em.wait("sync", "ev")
                dma("sync", dr[dst_].ap()[k:k + 128, :], sb["wbf"][:, 0:D],
                    "wdma")

    def build_ternary(src, dst, mi, blocked):
        nchunk = dr[src].shape[0] // 128
        width = dr[src].shape[1]
        vop(lambda e: e.memset(sb["sc_a"][:, 0:1], 0.0), None)
        for k in range(nchunk):
            em.wait("sync", "ev")
            dma("sync", sb["wf32b"][:, 0:width],
                dr[src].ap()[128 * k:128 * k + 128, :], "wdma")
            em.wait("vector", "wdma")
            vop(lambda e, w=width: e.tensor_reduce(
                sb["sc_b"][:, 0:1], sb["wf32b"][:, 0:w], AX.X, OP.add,
                apply_absolute_value=True), None)
            vop(lambda e: e.drain(), None)
            vop(lambda e: e.tensor_tensor(sb["sc_a"][:, 0:1], sb["sc_a"][:, 0:1],
                                          sb["sc_b"][:, 0:1], OP.add), "ev")
        em.wait("tensor", "ev")
        top(lambda t: t.matmul(B[2][0:1, 0:1], sb["onesf"][:], sb["sc_a"][:, 0:1],
                               start=True, stop=True))
        em.wait("vector", "mm")
        tot = float(nchunk * 128 * width)
        vopd(lambda e, tot=tot: e.tensor_scalar(
            sb["sc_b"][0:1, 0:1], B[2][0:1, 0:1], 1.0 / tot, 1e-5,
            OP.mult, OP.max), "ev")
        vop(lambda e: e.drain(), None)
        vopd(lambda e: e.reciprocal(sb["sc_b"][0:1, 1:2], sb["sc_b"][0:1, 0:1]))
        em.wait("sync", "ev")
        dma("sync", bass.AP(dr["b_row"], 0, [[1, 1], [1, 2]]), sb["sc_b"][0:1, 0:2],
            "wdma")
        em.wait("gpsimd", "wdma")
        dma("gpsimd", sb["mbar"][:, mi:mi + 1],
            bass.AP(dr["b_row"], 0, [[0, 128], [1, 1]]), "wdma")
        dma("gpsimd", sb["mbar"][:, mi + 2:mi + 3],
            bass.AP(dr["b_row"], 1, [[0, 128], [1, 1]]), "wdma")
        em.wait("vector", "wdma")
        for k in range(nchunk):
            dma("sync", sb["wf32b"][:, 0:width],
                dr[src].ap()[128 * k:128 * k + 128, :], "wdma")
            em.wait("vector", "wdma")
            scol = sb["mbar"][:, mi + 2:mi + 3]
            vop(lambda e, w=width, scol=scol: e.tensor_scalar(
                sb["wf32b"][:, 0:w], sb["wf32b"][:, 0:w], scol, 1.0,
                OP.mult, OP.min), None)
            vop(lambda e, w=width: e.tensor_scalar(
                sb["wf32b"][:, 0:w], sb["wf32b"][:, 0:w], -1.0, MAGIC,
                OP.max, OP.add), None)
            vopd(lambda e, w=width: e.tensor_scalar(
                sb["wbf"][:, 0:w], sb["wf32b"][:, 0:w], MAGIC, None, OP.subtract))
            em.wait("sync", "ev")
            if blocked:
                ncb = width // 128
                dma("sync", bass.AP(dr[dst], k * 16384,
                                    [[128, 128], [KC * 16384, ncb], [1, 128]]),
                    bass.AP(sb["wbf"], 0, [[2 * MLP, 128], [128, ncb], [1, 128]]),
                    "wdma")
            else:
                dma("sync", dr[dst].ap()[128 * k:128 * k + 128, :],
                    sb["wbf"][:, 0:width], "wdma")
            em.wait("vector", "wdma")

    if "ternary" not in skip:
        build_ternary("w1T", "c_w1t", 0, True)
        build_ternary("w2T", "c_w2t", 1, False)

    for j, nm in enumerate(["convw0", "convw1", "convw2"]):
        dmas("sync", bass.AP(sb["convw_col"], j * CC, [[3 * CC, 128], [1, CC]]),
            bass.AP(dr[nm], 0, [[1, 128], [128, CC]]), "wdma")
    em.wait("vector", "wdma")
    vop(lambda e: e.tensor_scalar(sb["convw_col"][:], sb["convw_col"][:],
                                  sb["mbar"][:, 0:1], None, OP.mult))

    for eng in ("sync", "tensor", "vector", "scalar", "gpsimd"):
        for s in ["wdma", "ev", "mm", "gp", "act", "ldc"]:
            em.wait(eng, s)

    pro_es.close()
    HT = TT // 2
    st("fq_g2", [128, 64 * G], BF16)
    st("fk_g2", [128, 64 * G], BF16)
    st("v_g2", [128, 64 * G], BF16)
    st("fqT", [128, 2 * 128], BF16)
    st("fkT", [128, 2 * 128], BF16)
    st("S2", [128, 8 * 128], BF16)
    st("rc_c", [128, G], F32)
    st("attn_g2", [128, 64 * G], BF16)
    st("xq1w_a", [128, KC * WIN], BF16)
    st("xq1w_b", [128, KC * WIN], BF16)
    st("hsil", [128, CC * HT], BF16)
    st("hwin", [128, 260], BF16)
    st("cv1", [128, HT], BF16)
    st("cv2", [128, HT], BF16)
    st("rmax", [128, HT], BF16)
    st("rmin", [128, HT], BF16)
    st("hq_c", [128, 2 * HT], BF16)
    st("gch", [128, 2 * 256], BF16)
    st("s2bc", [128, HT], F32)
    st("s2row", [1, HT], F32)
    vop(lambda e: e.memset(sb["xq1w_a"][:], 0.0), None)
    vop(lambda e: e.memset(sb["xq1w_b"][:], 0.0), "ev")
    for eng in ("sync", "tensor", "scalar", "gpsimd"):
        em.wait(eng, "ev")

    def ln_finish(off):
        ssl = sb["stats"][:, off:off + MC]
        qsl = sb["stats2"][:, off:off + MC]
        a = sb["sc_a"][:, 0:MC]
        b = sb["sc_b"][:, 0:MC]
        vop(lambda e: e.drain(), None)
        vop(lambda e: e.tensor_scalar(ssl, ssl, 1.0 / D, None, OP.mult), None)
        vop(lambda e: e.drain(), None)
        vop(lambda e: e.tensor_tensor(a, ssl, ssl, OP.mult), None)
        vop(lambda e: e.tensor_scalar(qsl, qsl, 1.0 / D, None, OP.mult), None)
        vop(lambda e: e.drain(), None)
        vop(lambda e: e.tensor_tensor(qsl, qsl, a, OP.subtract), None)
        vop(lambda e: e.drain(), None)
        vop(lambda e: e.tensor_scalar(qsl, qsl, 1e-5, None, OP.add), "ev")
        em.wait("scalar", "ev")
        sop(lambda s: s.activation(a, qsl, AF.Ln), "act")
        sop(lambda s: s.activation(qsl, a, AF.Exp, scale=-0.5), "act")
        em.wait("vector", "act")

    def ln_stats(tile, off):
        em.wait("scalar", "ev")
        for m in range(MC):
            sl = tile[:, D * m:D * (m + 1)]
            vop(lambda e, sl=sl, m=m: e.tensor_reduce(
                sb["stats"][:, off + m:off + m + 1], sl, AX.X, OP.add), None)
            sop(lambda s, sl=sl, m=m: s.activation(
                sb["junk"][:], sl, AF.Square,
                accum_out=sb["stats2"][:, off + m:off + m + 1]), "act")
        em.wait("vector", "act")
        ln_finish(off)

    def ln_norm(src, dst, off):
        for m in range(MC):
            ssl = src[:, D * m:D * (m + 1)]
            dsl = dst[:, D * m:D * (m + 1)]
            mcol = sb["stats"][:, off + m:off + m + 1]
            rcol = sb["stats2"][:, off + m:off + m + 1]
            last = (m == MC - 1)
            (vopd if last else vop)(
                lambda e, ssl=ssl, dsl=dsl, mcol=mcol, rcol=rcol: e.tensor_scalar(
                    dsl, ssl, mcol, rcol, OP.subtract, OP.mult))

    def tm_to_fm(tm_tile, fm_tile, fstride=TT, foff=0):
        for idx in range(MC * KC):
            m, k = idx // KC, idx % KC
            bank = B[6 + (idx % 2)]
            pv = bank.ap().bitcast(BF16)[:, 0:128]
            em.wait("tensor", "sv")
            top(lambda t, m=m, k=k, pv=pv, tm=tm_tile: t.transpose(
                pv, tm[:, D * m + 128 * k:D * m + 128 * k + 128],
                sb["identb"][:]), "tr")
            dsl = fm_tile[:, fstride * k + foff + 128 * m:
                          fstride * k + foff + 128 * m + 128]
            if idx % 2 == 0:
                em.wait("vector", "tr")
                vopd(lambda e, dsl=dsl, pv=pv: e.tensor_copy(dsl, pv), "sv")
            else:
                em.wait("scalar", "tr")
                sopd(lambda s, dsl=dsl, pv=pv: s.copy(dsl, pv), "sv")

    x_d = dr["x"].ap()
    y_d = dr["y"].ap()
    FS = 64 * G

    def phase_attn(ti):
        x1cur = sb["x1a"] if ti % 2 == 0 else sb["x1b"]
        wcur = sb["xq1w_a"] if ti % 2 == 0 else sb["xq1w_b"]
        wprev = sb["xq1w_b"] if ti % 2 == 0 else sb["xq1w_a"]

        em.wait("sync", "ydma")
        em.wait("sync", "ev")
        for m in range(MC):
            dma("sync", sb["x_tm"][:, D * m:D * (m + 1)],
                x_d[ti * TT + 128 * m:ti * TT + 128 * (m + 1), :], "xdma")
        em.wait("vector", "xdma")
        em.wait("scalar", "xdma")

        ln_stats(sb["x_tm"], 0)
        ln_norm(sb["x_tm"], sb["tm_sc"], 0)
        em.wait("tensor", "ev")
        tm_to_fm(sb["tm_sc"], sb["fm_sc"])

        for cname, brep, do_elu, bname in [
                ("c_wq", sb["bq_rep"], True, "b_fq"),
                ("c_wk", sb["bk_rep"], True, "b_fk"),
                ("c_wv", sb["bv_rep"], False, "b_v")]:
            em.wait("sync", "mm")
            for k in range(KC):
                dma("sync", sb["wbuf"][:, D * k:D * (k + 1)],
                    dr[cname].ap()[128 * k:128 * k + 128, :], "wdma")
            em.wait("tensor", "wdma")
            em.wait("tensor", "sv")
            for m in range(MC):
                pb = (B[2 * (m % 2)], B[2 * (m % 2) + 1])
                em.wait("tensor", "ev")
                for k in range(KC):
                    lhsT = sb["fm_sc"][:, TT * k + 128 * m:TT * k + 128 * m + 128]
                    for h2 in range(2):
                        top(lambda t, lhsT=lhsT, k=k, h2=h2, pb=pb: t.matmul(
                            pb[h2][:, :], lhsT,
                            sb["wbuf"][:, D * k + 512 * h2:D * k + 512 * (h2 + 1)],
                            start=(k == 0), stop=(k == KC - 1)))
                em.wait("vector", "mm")
                for h2 in range(2):
                    psl = pb[h2][:, :]
                    osl = sb["tm_sc"][:, D * m + 512 * h2:D * m + 512 * (h2 + 1)]
                    rsl = brep[:, 512 * h2:512 * (h2 + 1)]
                    if do_elu:
                        vopd(lambda e, psl=psl, rsl=rsl: e.scalar_tensor_tensor(
                            sb["ju2"][:], psl, 1.0, rsl, OP.mult, OP.add), "ev")
                        em.wait("scalar", "ev")
                        sop(lambda s: s.activation(sb["junk"][:, 0:512],
                                                   sb["ju2"][:], AF.Exp), "act")
                        sopd(lambda s: s.activation(sb["ju3"][:],
                                                    sb["ju2"][:], AF.Relu), "act")
                        em.wait("vector", "act")
                        vopd(lambda e, osl=osl: e.scalar_tensor_tensor(
                            osl, sb["junk"][:, 0:512], 1.0, sb["ju3"][:],
                            OP.min, OP.add), "ev")
                    else:
                        vopd(lambda e, psl=psl, rsl=rsl, osl=osl:
                             e.scalar_tensor_tensor(osl, psl, 1.0, rsl,
                                                    OP.mult, OP.add), "ev")
            em.wait("scalar", "ev")
            for m in range(MC):
                dma("scalar", dr[bname].ap()[128 * m:128 * (m + 1), :],
                    sb["tm_sc"][:, D * m:D * (m + 1)], "bnc")
            em.wait("vector", "bnc")

        em.wait("scalar", "bnc")
        for tname, bname in [("fq_g2", "b_fq"), ("fk_g2", "b_fk"), ("v_g2", "b_v")]:
            for h in range(H):
                dma("scalar",
                    bass.AP(sb[tname], h * FS, [[16 * FS, 8], [64, G], [1, 64]]),
                    bass.AP(dr[bname], 64 * h, [[D, 8], [8 * D, G], [1, 64]]), "gth")

        em.wait("tensor", "gth")
        for b8 in range(G // 8):
            for jj in range(4):
                j = b8 * 4 + jj
                pv_q = B[6].ap().bitcast(BF16)[:, 0:128]
                pv_k = B[7].ap().bitcast(BF16)[:, 0:128]
                em.wait("tensor", "sv")
                top(lambda t, j=j, pv=pv_q: t.transpose(
                    pv, sb["fq_g2"][:, 128 * j:128 * j + 128], sb["identb"][:]), "tr")
                top(lambda t, j=j, pv=pv_k: t.transpose(
                    pv, sb["fk_g2"][:, 128 * j:128 * j + 128], sb["identb"][:]), "tr")
                qs = sb["fqT"][:, 128 * (jj % 2):128 * (jj % 2) + 128]
                ks = sb["fkT"][:, 128 * (jj % 2):128 * (jj % 2) + 128]
                em.wait("vector", "tr")
                vopd(lambda e, qs=qs, pv=pv_q: e.tensor_copy(qs, pv), "sv")
                em.wait("scalar", "tr")
                sopd(lambda s, ks=ks, pv=pv_k: s.copy(ks, pv), "sv")
                em.wait("tensor", "sv")
                for dd in range(2):
                    slot = jj
                    bank = B[4] if dd == 0 else B[5]
                    top(lambda t, qs=qs, ks=ks, dd=dd, bank=bank, slot=slot:
                        t.matmul(bank[:, 128 * slot:128 * slot + 128],
                                 ks[64 * dd:64 * dd + 64, :],
                                 qs[64 * dd:64 * dd + 64, :],
                                 start=True, stop=True))
            em.wait("vector", "mm")
            mb4 = bass.AP(sb["maskS"], 0, [[128, 128], [0, 4], [1, 128]])
            vop(lambda e, mb4=mb4: e.tensor_tensor(
                sb["S2"][:, 0:512].rearrange("p (g c) -> p g c", g=4),
                B[4][:].rearrange("p (g c) -> p g c", g=4), mb4, OP.mult), None)
            vopd(lambda e, mb4=mb4: e.tensor_tensor(
                sb["S2"][:, 512:1024].rearrange("p (g c) -> p g c", g=4),
                B[5][:].rearrange("p (g c) -> p g c", g=4), mb4, OP.mult), "sv")
            em.wait("tensor", "sv")
            for gg in range(8):
                jj, dd = gg // 2, gg % 2
                gi = b8 * 8 + 2 * jj + dd
                s2sl = sb["S2"][:, 512 * dd + 128 * jj:512 * dd + 128 * jj + 128]
                top(lambda t, s2sl=s2sl, gi=gi: t.matmul(
                    B[3][:, gi:gi + 1], s2sl, sb["onesb"][:], start=True, stop=True))
                top(lambda t, s2sl=s2sl, gi=gi, gg=gg: t.matmul(
                    B[2][:, 64 * gg:64 * gg + 64], s2sl,
                    sb["v_g2"][:, 64 * gi:64 * gi + 64], start=True, stop=True))
            em.wait("vector", "mm")
            rsl = sb["rc_c"][:, 8 * b8:8 * b8 + 8]
            vop(lambda e, b8=b8: e.tensor_scalar(
                sb["sc_c"][:, 0:8], B[3][:, 8 * b8:8 * b8 + 8], 1e-6, None,
                OP.add), None)
            vop(lambda e: e.drain(), None)
            vop(lambda e, rsl=rsl: e.reciprocal(rsl, sb["sc_c"][:, 0:8]), None)
            vop(lambda e: e.drain(), None)
            rbc = bass.AP(sb["rc_c"], 8 * b8, [[G, 128], [1, 8], [0, 64]])
            vopd(lambda e, b8=b8, rbc=rbc: e.tensor_tensor(
                sb["attn_g2"][:, 512 * b8:512 * (b8 + 1)].rearrange(
                    "p (g e) -> p g e", g=8),
                B[2][:].rearrange("p (g e) -> p g e", g=8), rbc, OP.mult), "sv")
            em.wait("tensor", "sv")

        em.wait("scalar", "sv")
        dma("scalar", dr["b_at"].ap(), sb["attn_g2"][:], "bnc")
        em.wait("scalar", "bnc")
        for gi in range(G):
            m = (8 * gi) // 128
            r0 = (8 * gi) % 128
            out_ap = bass.AP(sb["tm_sc"], r0 * (MC * D) + D * m,
                             [[MC * D, 8], [64, 16], [1, 64]])
            in_ap = bass.AP(dr["b_at"], 64 * gi, [[16 * FS, 8], [FS, 16], [1, 64]])
            dma("scalar", out_ap, in_ap, "gth")
        em.wait("tensor", "gth")
        em.wait("vector", "gth")
        em.wait("scalar", "gth")
        tm_to_fm(sb["tm_sc"], sb["fm_sc"])

        em.wait("sync", "mm")
        for k in range(KC):
            dma("sync", sb["wbuf"][:, D * k:D * (k + 1)],
                dr["c_wo"].ap()[128 * k:128 * k + 128, :], "wdma")
        em.wait("tensor", "wdma")
        em.wait("tensor", "sv")
        for m in range(MC):
            pb = (B[2 * (m % 2)], B[2 * (m % 2) + 1])
            em.wait("tensor", "ev")
            for k in range(KC):
                lhsT = sb["fm_sc"][:, TT * k + 128 * m:TT * k + 128 * m + 128]
                for h2 in range(2):
                    top(lambda t, lhsT=lhsT, k=k, h2=h2, pb=pb: t.matmul(
                        pb[h2][:, :], lhsT,
                        sb["wbuf"][:, D * k + 512 * h2:D * k + 512 * (h2 + 1)],
                        start=(k == 0), stop=(k == KC - 1)))
            em.wait("vector", "mm")
            for h2 in range(2):
                psl = pb[h2][:, :]
                bsl = sb["bo_rep"][:, 512 * h2:512 * (h2 + 1)]
                xsl = sb["x_tm"][:, D * m + 512 * h2:D * m + 512 * (h2 + 1)]
                osl = sb["xmid_tm"][:, D * m + 512 * h2:D * m + 512 * (h2 + 1)]
                vop(lambda e, psl=psl, bsl=bsl: e.scalar_tensor_tensor(
                    sb["ju2"][:], psl, 1.0, bsl, OP.mult, OP.add), None)
                vopd(lambda e, osl=osl, xsl=xsl: e.tensor_tensor(
                    osl, sb["ju2"][:], xsl, OP.add), "ev")

        ln_stats(sb["xmid_tm"], 4)
        ln_norm(sb["xmid_tm"], sb["tm_sc"], 4)
        em.wait("tensor", "ev")
        tm_to_fm(sb["tm_sc"], sb["fm_sc"])

        em.wait("tensor", "sv")
        for half in range(2):
            for c in range(CC):
                wsl = sb["wslc2"][:, D * (c % 2):D * (c % 2) + D]
                em.wait("sync", "mm")
                dma("sync", wsl,
                    bass.AP(dr["c_fcw"], c * KC * 16384,
                            [[128, 128], [16384, KC], [1, 128]]), "wdma")
                em.wait("tensor", "wdma")
                fcbank = B[4 + (c % 2)]
                em.wait("tensor", "act")
                em.wait("scalar", "mm")
                for k in range(KC):
                    rhs = sb["fm_sc"][:, TT * k + 256 * half:
                                      TT * k + 256 * half + 256]
                    top(lambda t, wsl=wsl, k=k, rhs=rhs, fcbank=fcbank: t.matmul(
                        fcbank[:, 0:256], wsl[:, 128 * k:128 * k + 128], rhs,
                        start=(k == 0), stop=(k == KC - 1)))
                em.wait("scalar", "mm")
                gsl = sb["gch"][:, 256 * (c % 2):256 * (c % 2) + 256]
                sopd(lambda s, gsl=gsl, fcbank=fcbank, c=c: s.activation(
                    gsl, fcbank[:, 0:256], AF.Gelu,
                    bias=sb["fcb_col"][:, c:c + 1], scale=1.0), "act")
                wpsl = sb["wslc"][:, D * (c % 2):D * (c % 2) + D]
                em.wait("sync", "mm")
                dma("sync", wpsl, dr["c_projw"].ap()[128 * c:128 * c + 128, :],
                    "wdma")
                em.wait("tensor", "wdma")
                em.wait("tensor", "act")
                for mloc in range(2):
                    for h2 in range(2):
                        top(lambda t, gsl=gsl, mloc=mloc, h2=h2, wpsl=wpsl, c=c:
                            t.matmul(B[2 * mloc + h2][:, :],
                                     gsl[:, 128 * mloc:128 * mloc + 128],
                                     wpsl[:, 512 * h2:512 * (h2 + 1)],
                                     start=(c == 0), stop=(c == CC - 1)))
            em.wait("vector", "mm")
            for mloc in range(2):
                m = 2 * half + mloc
                for h2 in range(2):
                    psl = B[2 * mloc + h2][:, :]
                    bsl = sb["projb_rep"][:, 512 * h2:512 * (h2 + 1)]
                    midsl = sb["xmid_tm"][:, D * m + 512 * h2:D * m + 512 * (h2 + 1)]
                    xsl = sb["x_tm"][:, D * m + 512 * h2:D * m + 512 * (h2 + 1)]
                    vop(lambda e, psl=psl, bsl=bsl: e.scalar_tensor_tensor(
                        sb["ju2"][:], psl, 1.0, bsl, OP.mult, OP.add), None)
                    vop(lambda e, midsl=midsl: e.tensor_tensor(
                        sb["ju3"][:], sb["ju2"][:], midsl, OP.add), None)
                    vopd(lambda e, xsl=xsl: e.tensor_tensor(
                        xsl, sb["ju3"][:], xsl, OP.add), "ev")
            em.wait("tensor", "ev")

        ln_stats(sb["x_tm"], 0)
        ln_norm(sb["x_tm"], x1cur, 0)

        for m in range(MC):
            sl = x1cur[:, D * m:D * (m + 1)]
            vop(lambda e, sl=sl, m=m: e.tensor_reduce(
                sb["qscale"][:, m:m + 1], sl, AX.X, OP.max,
                apply_absolute_value=True), None)
        vop(lambda e: e.drain(), None)
        vop(lambda e: e.tensor_scalar(sb["qscale"][:, 0:MC], sb["qscale"][:, 0:MC],
                                      1e-5, None, OP.max), None)
        vop(lambda e: e.drain(), None)
        vop(lambda e: e.reciprocal(sb["sc_c"][:, 0:MC], sb["qscale"][:, 0:MC]), None)
        vop(lambda e: e.tensor_scalar(sb["qscale"][:, 4:4 + MC],
                                      sb["qscale"][:, 0:MC], 1.0 / 127.0, None,
                                      OP.mult), None)
        vop(lambda e: e.drain(), None)
        vop(lambda e: e.tensor_scalar(sb["qscale"][:, 0:MC], sb["sc_c"][:, 0:MC],
                                      127.0, None, OP.mult), None)
        vop(lambda e: e.drain(), None)
        for m in range(MC):
            ssl = x1cur[:, D * m:D * (m + 1)]
            scol = sb["qscale"][:, m:m + 1]
            dcol = sb["qscale"][:, 4 + m:5 + m]
            osl = sb["tm_sc"][:, D * m:D * (m + 1)]
            vop(lambda e, ssl=ssl, scol=scol: e.tensor_scalar(
                sb["junk"][:], ssl, scol, MAGIC, OP.mult, OP.add), None)
            vopd(lambda e, osl=osl, dcol=dcol: e.tensor_scalar(
                osl, sb["junk"][:], MAGIC, dcol, OP.subtract, OP.mult), "ev")
        em.wait("tensor", "ev")
        tm_to_fm(sb["tm_sc"], wcur, fstride=WIN, foff=1)
        em.wait("vector", "sv")
        for k in range(KC):
            if ti > 0:
                vop(lambda e, k=k, wc=wcur, wp=wprev: e.tensor_copy(
                    wc[:, WIN * k:WIN * k + 1], wp[:, WIN * k + TT:WIN * k + TT + 1]),
                    None)
            (vopd if k == KC - 1 else vop)(
                lambda e, k=k, wc=wcur, wp=wprev: e.tensor_copy(
                    wp[:, WIN * k + TT + 1:WIN * k + TT + 2],
                    wc[:, WIN * k + 1:WIN * k + 2]), "ev")

    def phase_ffn(tj, last=False):
        x1p = sb["x1a"] if tj % 2 == 0 else sb["x1b"]
        wp = sb["xq1w_a"] if tj % 2 == 0 else sb["xq1w_b"]
        HT = TT // 2
        if last:
            for k in range(KC):
                (vopd if k == KC - 1 else vop)(lambda e, k=k, wp=wp: e.memset(
                    wp[:, WIN * k + TT + 1:WIN * k + TT + 2], 0.0), "ev")
            em.wait("tensor", "ev")
        for half in range(2):
            vop(lambda e: e.memset(sb["rmax"][:], 0.0), None)
            vop(lambda e: e.memset(sb["rmin"][:], 0.0), "ev")
            for c in range(CC):
                wsl = sb["wslc2"][:, D * (c % 2):D * (c % 2) + D]
                em.wait("sync", "mm")
                dma("sync", wsl,
                    bass.AP(dr["c_w1t"], c * KC * 16384,
                            [[128, 128], [16384, KC], [1, 128]]), "wdma")
                em.wait("tensor", "wdma")
                bA = B[4 + (c % 2)]
                em.wait("tensor", "ev")
                for k in range(KC):
                    lh = wsl[:, 128 * k:128 * k + 128]
                    top(lambda t, lh=lh, bA=bA, wp=wp, k=k, half=half: t.matmul(
                        bA[:, 0:258], lh,
                        wp[:, WIN * k + HT * half:WIN * k + HT * half + 258],
                        start=(k == 0), stop=(k == KC - 1)))
                em.wait("vector", "mm")
                vopd(lambda e, bA=bA: e.tensor_copy(sb["hwin"][:, 0:258],
                                                    bA[:, 0:258]), "ev")
                w0c = sb["convw_col"][:, c:c + 1]
                w1c = sb["convw_col"][:, CC + c:CC + c + 1]
                w2c = sb["convw_col"][:, 2 * CC + c:2 * CC + c + 1]
                vop(lambda e, w1c=w1c: e.tensor_scalar(
                    sb["cv1"][:], sb["hwin"][:, 1:257], w1c, None, OP.mult), None)
                vop(lambda e, w0c=w0c: e.scalar_tensor_tensor(
                    sb["cv2"][:], sb["hwin"][:, 0:256], w0c, sb["cv1"][:],
                    OP.mult, OP.add), None)
                vopd(lambda e, w2c=w2c: e.scalar_tensor_tensor(
                    sb["cv1"][:], sb["hwin"][:, 2:258], w2c, sb["cv2"][:],
                    OP.mult, OP.add), "ev")
                em.wait("scalar", "ev")
                hs = sb["hsil"][:, HT * c:HT * (c + 1)]
                sopd(lambda s, hs=hs, c=c: s.activation(
                    hs, sb["cv1"][:], AF.Silu, bias=sb["convb_col"][:, c:c + 1],
                    scale=1.0), "act")
                em.wait("gpsimd", "act")
                em.wait("vector", "act")
                vop(lambda e, hs=hs: e.tensor_tensor(sb["rmax"][:], sb["rmax"][:],
                                                     hs, OP.max), None)
                vop(lambda e, hs=hs: e.tensor_tensor(sb["rmin"][:], sb["rmin"][:],
                                                     hs, OP.min), "ev")

            for mloc in range(2):
                m = 2 * half + mloc
                pvA = B[6].ap().bitcast(BF16)[:, 0:128]
                pvB = B[6].ap().bitcast(BF16)[:, 128:256]
                top(lambda t, mloc=mloc, pvA=pvA: t.transpose(
                    pvA, sb["rmax"][:, 128 * mloc:128 * mloc + 128],
                    sb["identb"][:]), "tr")
                top(lambda t, mloc=mloc, pvB=pvB: t.transpose(
                    pvB, sb["rmin"][:, 128 * mloc:128 * mloc + 128],
                    sb["identb"][:]), "tr")
                em.wait("vector", "tr")
                vop(lambda e, pvA=pvA: e.tensor_reduce(
                    sb["sc_a"][:, 0:1], pvA, AX.X, OP.max,
                    apply_absolute_value=True), None)
                vopd(lambda e, pvB=pvB: e.tensor_reduce(
                    sb["sc_a"][:, 1:2], pvB, AX.X, OP.max,
                    apply_absolute_value=True), "ev")
                vopd(lambda e, m=m: e.tensor_tensor(
                    sb["q2s"][:, m:m + 1], sb["sc_a"][:, 0:1], sb["sc_a"][:, 1:2],
                    OP.max), "sv")
                em.wait("tensor", "sv")
            c0 = 2 * half
            vop(lambda e: e.drain(), None)
            vop(lambda e, c0=c0: e.tensor_scalar(
                sb["q2s"][:, c0:c0 + 2], sb["q2s"][:, c0:c0 + 2], 1e-5, None,
                OP.max), None)
            vop(lambda e: e.drain(), None)
            vop(lambda e, c0=c0: e.reciprocal(sb["sc_c"][:, 0:2],
                                              sb["q2s"][:, c0:c0 + 2]), None)
            vop(lambda e, c0=c0: e.tensor_scalar(
                sb["q2s"][:, 4 + c0:6 + c0], sb["q2s"][:, c0:c0 + 2],
                1.0 / 127.0, None, OP.mult), None)
            vop(lambda e: e.drain(), None)
            vop(lambda e, c0=c0: e.tensor_tensor(
                sb["q2s"][:, 4 + c0:6 + c0], sb["q2s"][:, 4 + c0:6 + c0],
                sb["mbar"][:, 1:2].broadcast_to([128, 2]), OP.mult), None)
            vopd(lambda e, c0=c0: e.tensor_scalar(
                sb["q2s"][:, c0:c0 + 2], sb["sc_c"][:, 0:2], 127.0, None,
                OP.mult), "ev")
            em.wait("tensor", "ev")
            for mloc in range(2):
                m = 2 * half + mloc
                top(lambda t, m=m, mloc=mloc: t.transpose(
                    B[7][0:1, 128 * mloc:128 * mloc + 128], sb["q2s"][:, m:m + 1],
                    sb["identf"][:]), "tr")
            em.wait("vector", "tr")
            vopd(lambda e: e.tensor_copy(sb["s2row"][:], B[7][0:1, 0:HT]), "sv")
            em.wait("tensor", "sv")
            top(lambda t: t.matmul(B[7][:, 0:HT], sb["onesrow"][:], sb["s2row"][:],
                                   start=True, stop=True))
            em.wait("vector", "mm")
            vopd(lambda e: e.tensor_copy(sb["s2bc"][:], B[7][:, 0:HT]), "ev")

            em.wait("tensor", "ev")
            for c in range(CC):
                hs = sb["hsil"][:, HT * c:HT * (c + 1)]
                hq = sb["hq_c"][:, HT * (c % 2):HT * (c % 2) + HT]
                em.wait("vector", "mm")
                vop(lambda e, hs=hs: e.tensor_tensor(
                    sb["ju2"][:, 0:HT], hs, sb["s2bc"][:], OP.mult), None)
                vopd(lambda e, hq=hq: e.tensor_scalar(
                    hq, sb["ju2"][:, 0:HT], MAGIC, MAGIC, OP.add, OP.subtract),
                    "ev")
                wsl = sb["wslc"][:, D * (c % 2):D * (c % 2) + D]
                em.wait("sync", "mm")
                dma("sync", wsl, dr["c_w2t"].ap()[128 * c:128 * c + 128, :], "wdma")
                em.wait("tensor", "wdma")
                em.wait("tensor", "ev")
                for mloc in range(2):
                    for h2 in range(2):
                        top(lambda t, hq=hq, mloc=mloc, h2=h2, wsl=wsl, c=c:
                            t.matmul(B[2 * mloc + h2][:, :],
                                     hq[:, 128 * mloc:128 * mloc + 128],
                                     wsl[:, 512 * h2:512 * (h2 + 1)],
                                     start=(c == 0), stop=(c == CC - 1)))
            em.wait("vector", "mm")
            for mloc in range(2):
                m = 2 * half + mloc
                for h2 in range(2):
                    psl = B[2 * mloc + h2][:, :]
                    x1sl = x1p[:, D * m + 512 * h2:D * m + 512 * (h2 + 1)]
                    osl = sb["xmid_tm"][:, D * m + 512 * h2:D * m + 512 * (h2 + 1)]
                    dcol = sb["q2s"][:, 4 + m:5 + m]
                    vopd(lambda e, psl=psl, x1sl=x1sl, osl=osl, dcol=dcol:
                         e.scalar_tensor_tensor(osl, psl, dcol, x1sl,
                                                OP.mult, OP.add), "ev")
            em.wait("tensor", "ev")

        ln_stats(sb["xmid_tm"], 8)
        ln_norm(sb["xmid_tm"], sb["x_tm"], 8)
        em.wait("sync", "ev")
        for m in range(MC):
            dma("sync", y_d[tj * TT + 128 * m:tj * TT + 128 * (m + 1), :],
                sb["x_tm"][:, D * m:D * (m + 1)], "ydma")

    with nc.Block() as block:
        for ti in range(nt_limit):
            phase_attn(ti)
            if ti > 0 and do_ffn:
                phase_ffn(ti - 1)
        if do_ffn:
            phase_ffn(nt_limit - 1, last=True)
        if dump:
            for eng in ("sync", "tensor", "vector", "scalar", "gpsimd"):
                for s in ["wdma", "xdma", "ydma", "bnc", "gth", "mm", "tr",
                          "ev", "sv", "gp", "act"]:
                    em.wait(eng, s)
            for nm in dump:
                t = sb[nm]
                shp = list(t.shape)
                dbg = nc.dram_tensor("dbg_" + nm, shp, t.dtype,
                                     kind="ExternalOutput")
                dma("sync", dbg.ap(), t[:], "ydma")
        em.wait("sync", "ydma")
        em.wait("gpsimd", "ydma")

        @block.sync
        def _(e):
            for fn in em.lists["sync"]:
                fn(e)

        @block.tensor
        def _(e):
            for fn in em.lists["tensor"]:
                fn(e)

        @block.vector
        def _(e):
            for fn in em.lists["vector"]:
                fn(e)

        @block.scalar
        def _(e):
            for fn in em.lists["scalar"]:
                fn(e)

        @block.gpsimd
        def _(e):
            for fn in em.lists["gpsimd"]:
                fn(e)

    es.close()
    return nc


_NC_CACHE = None


def _get_nc():
    global _NC_CACHE
    if _NC_CACHE is None:
        _NC_CACHE = build_program()
    return _NC_CACHE


def kernel(**inputs):
    ii = {k: np.asarray(v) for k, v in inputs.items()}
    x = np.ascontiguousarray(ii["L0_attn_q_x"].astype(np.float32, copy=False))
    shared = {
        "wq": ii["L0_attn_wq"], "wk": ii["L0_attn_wk"], "wv": ii["L0_attn_wv"],
        "wo": ii["L0_attn_wo"], "fcw": ii["L0_attn_fc_w"],
        "projw": ii["L0_attn_proj_w"],
        "w1T": np.ascontiguousarray(ii["L0_ffn_w1"].T),
        "w2T": np.ascontiguousarray(ii["L0_ffn_w2"].T),
        "ln1w": ii["L0_attn_ln1_w"], "ln1b": ii["L0_attn_ln1_b"],
        "ln2w": ii["L0_attn_ln2_w"], "ln2b": ii["L0_attn_ln2_b"],
        "bq": ii["L0_attn_bq"], "bk": ii["L0_attn_bk"], "bv": ii["L0_attn_bv"],
        "bo": ii["L0_attn_bo"], "fcb": ii["L0_attn_fc_b"],
        "projb": ii["L0_attn_proj_b"],
        "convw0": np.ascontiguousarray(ii["L0_ffn_conv_w"][:, 0, 0]),
        "convw1": np.ascontiguousarray(ii["L0_ffn_conv_w"][:, 0, 1]),
        "convw2": np.ascontiguousarray(ii["L0_ffn_conv_w"][:, 0, 2]),
        "identb": np.eye(128, dtype=ml_dtypes.bfloat16),
        "identf": np.eye(128, dtype=np.float32),
        "maskS": np.kron(np.eye(8, dtype=np.float32), np.ones((16, 16), np.float32)),
        "convb": ii["L0_ffn_conv_b"],
    }
    shared = {k: (np.ascontiguousarray(v.astype(np.float32, copy=False))
                  if v.dtype != ml_dtypes.bfloat16 else v)
              for k, v in shared.items()}
    nc = _get_nc()
    in_maps = [dict(shared, x=np.ascontiguousarray(x[c])) for c in range(NCORES)]
    res = run_bass_kernel_spmd(nc, in_maps, core_ids=list(range(NCORES)))
    out = np.stack([res.results[c]["y"] for c in range(NCORES)], axis=0)
    return out.astype(np.float32, copy=False)


if __name__ == "__main__":
    nc = build_program()
    print("built")


# version 06
# speedup vs baseline: 1.0397x; 1.0397x over previous
import numpy as np
import ml_dtypes
import concourse.bass as bass
import concourse.mybir as mybir
from concourse.bass_utils import run_bass_kernel_spmd
from contextlib import ExitStack

F32 = mybir.dt.float32
BF16 = mybir.dt.bfloat16
I32 = mybir.dt.int32
OP = mybir.AluOpType
AF = mybir.ActivationFunctionType
AX = mybir.AxisListType

D = 1024
H = 16
MLP = 4096
N = 4096
TT = 512
NT = N // TT
MC = TT // 128
KC = D // 128
CC = MLP // 128
G = TT // 8
WIN = TT + 2
MAGIC = float(1.5 * 2 ** 23)
RSQRT_MAGIC = 0x5f3759df
NCORES = 8


class Emitter:
    def __init__(self):
        self.lists = {e: [] for e in ("sync", "tensor", "vector", "scalar", "gpsimd")}
        self.sems = {}
        self.cnt = {}

    def op(self, eng, fn, inc=None, n=1):
        if inc is not None:
            self.cnt[inc] += n
            sem = self.sems[inc]
            self.lists[eng].append(lambda e, fn=fn, sem=sem, n=n: fn(e).then_inc(sem, n))
        else:
            self.lists[eng].append(lambda e, fn=fn: fn(e))

    def wait(self, eng, name, val=None):
        v = self.cnt[name] if val is None else val
        sem = self.sems[name]
        self.lists[eng].append(lambda e, sem=sem, v=v: e.wait_ge(sem, v))


def build_program(nt_limit=NT, do_ffn=True, dump=(), skip=()):
    nc = bass.Bass("TRN2", target_bir_lowering=False)
    es = ExitStack()
    E = es.enter_context

    dr = {}

    def dt(name, shape, dtype, kind="Internal"):
        dr[name] = nc.dram_tensor(name, shape, dtype, kind=kind)
        return dr[name]

    dt("x", [N, D], F32, "ExternalInput")
    for nm, shp in [("wq", [D, D]), ("wk", [D, D]), ("wv", [D, D]), ("wo", [D, D]),
                    ("fcw", [D, MLP]), ("projw", [MLP, D]),
                    ("w1T", [D, MLP]), ("w2T", [MLP, D]),
                    ("ln1w", [D]), ("ln1b", [D]), ("ln2w", [D]), ("ln2b", [D]),
                    ("bq", [D]), ("bk", [D]), ("bv", [D]), ("bo", [D]),
                    ("fcb", [MLP]), ("projb", [D]),
                    ("convw0", [MLP]), ("convw1", [MLP]), ("convw2", [MLP]),
                    ("convb", [MLP])]:
        dt(nm, shp, F32, "ExternalInput")
    dt("identb", [128, 128], BF16, "ExternalInput")
    dt("identf", [128, 128], F32, "ExternalInput")
    dt("maskS", [128, 128], F32, "ExternalInput")
    dt("y", [N, D], F32, "ExternalOutput")

    dt("c_wq", [D, D], BF16)
    dt("c_wk", [D, D], BF16)
    dt("c_wv", [D, D], BF16)
    dt("c_wo", [D, D], BF16)
    dt("c_fcw", [KC * CC * 128 * 128], BF16)
    dt("c_projw", [MLP, D], BF16)
    dt("c_w1t", [KC * CC * 128 * 128], BF16)
    dt("c_w2t", [MLP, D], BF16)
    dt("b_fq", [TT, D], BF16)
    dt("b_fk", [TT, D], BF16)
    dt("b_v", [TT, D], BF16)
    dt("b_at", [128, 64 * G], BF16)
    dt("b_row", [D], F32)

    sb = {}

    def st(name, shape, dtype):
        sb[name] = E(nc.sbuf_tensor("s_" + name, shape, dtype))
        return sb[name]

    st("identb", [128, 128], BF16)
    st("identf", [128, 128], F32)
    st("maskS", [128, 128], F32)
    st("onesb", [128, 1], BF16)
    st("onesf", [128, 1], F32)
    st("onesrow", [1, 128], F32)
    st("magici", [128, 16], I32)
    st("lnw_cols", [128, 2 * KC], F32)
    st("ln1b_col", [128, KC], BF16)
    st("ln2b_col", [128, KC], BF16)
    st("bias_sc", [128, KC], F32)
    st("fcb_col", [128, CC], F32)
    st("convw_col", [128, 3 * CC], F32)
    st("convb_col", [128, CC], F32)
    st("mbar", [128, 4], F32)
    st("bq_rep", [128, D], BF16)
    st("bk_rep", [128, D], BF16)
    st("bv_rep", [128, D], BF16)
    st("bo_rep", [128, D], BF16)
    st("projb_rep", [128, D], BF16)
    st("wbuf", [128, KC * D], BF16)
    st("wslc", [128, 2 * D], BF16)
    st("wslc2", [128, 2 * D], BF16)
    st("x_tm", [128, MC * D], F32)
    st("xmid_tm", [128, MC * D], F32)
    st("x1a", [128, MC * D], F32)
    st("x1b", [128, MC * D], F32)
    st("tm_sc", [128, MC * D], BF16)
    st("fm_sc", [128, KC * TT], BF16)
    st("stats", [128, 16], F32)
    st("stats2", [128, 16], F32)
    st("sc_a", [128, 16], F32)
    st("sc_b", [128, 16], F32)
    st("sc_c", [128, 16], F32)
    st("qscale", [128, 8], F32)
    st("q2s", [128, 8], F32)
    st("dbgk", [128, 256], F32)
    st("junk", [128, D], F32)
    st("ju2", [128, 1024], F32)
    st("ju3", [128, 1024], F32)

    B = {}
    for i in range(8):
        B[i] = E(nc.psum_tensor(f"bank{i}", [128, 512], F32))

    em = Emitter()
    for s in ["ldc", "wdma", "xdma", "ydma", "bnc", "gth", "mm", "tr", "ev",
              "sv", "gp", "act"]:
        em.sems[s] = E(nc.semaphore(s))
        em.cnt[s] = 0

    def vop(fn, inc="ev"):
        em.op("vector", fn, inc)

    def vopd(fn, inc="ev"):
        em.op("vector", fn, None)
        em.op("vector", lambda e: e.drain(), inc)

    def sop(fn, inc="act"):
        em.op("scalar", fn, inc)

    def sopd(fn, inc="act"):
        em.op("scalar", fn, None)
        em.op("scalar", lambda e: e.drain(), inc)

    def gop(fn, inc="gp"):
        em.op("gpsimd", fn, inc)

    def top(fn, inc="mm"):
        em.op("tensor", fn, inc)

    def dma(eng, out_ap, in_ap, inc):
        em.op(eng, lambda e, o=out_ap, i=in_ap: e.dma_start(out=o, in_=i), inc, 16)

    def dmas(eng, out_ap, in_ap, inc):
        em.op(eng, lambda e, o=out_ap, i=in_ap: e.dma_start(
            out=o, in_=i, allow_slow_non_contiguous=True), inc, 16)

    pro_es = ExitStack()
    sb["wf32b"] = pro_es.enter_context(nc.sbuf_tensor("s_wf32b", [128, MLP], F32))
    sb["wbf"] = pro_es.enter_context(nc.sbuf_tensor("s_wbf", [128, 2 * MLP], BF16))

    for nm in ["identb", "identf", "maskS"]:
        dma("sync", sb[nm][:], dr[nm].ap(), "ldc")
    for j, nm in enumerate(["ln1w", "ln2w"]):
        dmas("sync", bass.AP(sb["lnw_cols"], j * KC, [[2 * KC, 128], [1, KC]]),
            bass.AP(dr[nm], 0, [[1, 128], [128, KC]]), "ldc")
    dmas("sync", bass.AP(sb["convb_col"], 0, [[CC, 128], [1, CC]]),
        bass.AP(dr["convb"], 0, [[1, 128], [128, CC]]), "ldc")
    dmas("gpsimd", sb["bo_rep"][:], bass.AP(dr["bo"], 0, [[0, 128], [1, D]]), "ldc")
    dmas("gpsimd", sb["projb_rep"][:], bass.AP(dr["projb"], 0, [[0, 128], [1, D]]), "ldc")
    dmas("gpsimd", sb["ln1b_col"][:], bass.AP(dr["ln1b"], 0, [[1, 128], [128, KC]]), "ldc")
    dmas("gpsimd", sb["ln2b_col"][:], bass.AP(dr["ln2b"], 0, [[1, 128], [128, KC]]), "ldc")
    em.wait("vector", "ldc")
    vop(lambda e: e.memset(sb["onesb"][:], 1.0), None)
    vop(lambda e: e.memset(sb["onesf"][:], 1.0), None)
    vop(lambda e: e.memset(sb["magici"][:], RSQRT_MAGIC), None)
    vop(lambda e: e.memset(sb["onesrow"][:], 1.0))

    def build_attn_cache(wname, cname, borig, brep):
        dmas("sync", bass.AP(sb["bias_sc"], 0, [[KC, 128], [1, KC]]),
             bass.AP(dr[borig], 0, [[1, 128], [128, KC]]), "wdma")
        em.wait("vector", "wdma")
        for k in range(KC):
            dma("sync", sb["wf32b"][:, 0:D], dr[wname].ap()[128 * k:128 * k + 128, :],
                "wdma")
            em.wait("vector", "wdma")
            lcol = sb["lnw_cols"][:, k:k + 1]
            vop(lambda e, lcol=lcol: e.tensor_scalar(
                sb["wbf"][:, 0:D], sb["wf32b"][:, 0:D], lcol, None, OP.mult), None)
            vopd(lambda e: e.tensor_copy(sb["wbf"][:, D:2 * D], sb["wf32b"][:, 0:D]))
            em.wait("sync", "ev")
            dma("sync", dr[cname].ap()[128 * k:128 * k + 128, :], sb["wbf"][:, 0:D],
                "wdma")
            em.wait("tensor", "ev")
            for c in range(KC):
                top(lambda t, k=k, c=c: t.matmul(
                    B[0][:, c:c + 1], sb["wbf"][:, D + 128 * c:D + 128 * c + 128],
                    sb["ln1b_col"][:, k:k + 1], start=True, stop=True))
            em.wait("vector", "mm")
            vopd(lambda e: e.tensor_tensor(sb["bias_sc"][:], B[0][:, 0:KC],
                                           sb["bias_sc"][:], OP.add))
        em.wait("sync", "ev")
        dmas("sync", bass.AP(dr["b_row"], 0, [[1, 128], [128, KC]]),
            bass.AP(sb["bias_sc"], 0, [[KC, 128], [1, KC]]), "wdma")
        em.wait("gpsimd", "wdma")
        dma("gpsimd", brep[:], bass.AP(dr["b_row"], 0, [[0, 128], [1, D]]), "wdma")

    if "attn" not in skip:
        build_attn_cache("wq", "c_wq", "bq", sb["bq_rep"])
        build_attn_cache("wk", "c_wk", "bk", sb["bk_rep"])
        build_attn_cache("wv", "c_wv", "bv", sb["bv_rep"])

    dmas("sync", bass.AP(sb["fcb_col"], 0, [[CC, 128], [1, CC]]),
         bass.AP(dr["fcb"], 0, [[1, 128], [128, CC]]), "wdma")
    em.wait("vector", "wdma")
    for k in range(KC):
        dma("sync", sb["wf32b"][:], dr["fcw"].ap()[128 * k:128 * k + 128, :], "wdma")
        em.wait("vector", "wdma")
        lcol = sb["lnw_cols"][:, KC + k:KC + k + 1]
        vop(lambda e, lcol=lcol: e.tensor_scalar(
            sb["wbf"][:, 0:MLP], sb["wf32b"][:], lcol, None, OP.mult), None)
        vopd(lambda e: e.tensor_copy(sb["wbf"][:, MLP:2 * MLP], sb["wf32b"][:]))
        em.wait("sync", "ev")
        dma("sync", bass.AP(dr["c_fcw"], k * 16384,
                            [[128, 128], [KC * 16384, CC], [1, 128]]),
            bass.AP(sb["wbf"], 0, [[2 * MLP, 128], [128, CC], [1, 128]]), "wdma")
        em.wait("tensor", "ev")
        for c in range(CC):
            top(lambda t, k=k, c=c: t.matmul(
                B[1][:, c:c + 1], sb["wbf"][:, MLP + 128 * c:MLP + 128 * c + 128],
                sb["ln2b_col"][:, k:k + 1], start=True, stop=True))
        em.wait("vector", "mm")
        vopd(lambda e: e.tensor_tensor(sb["fcb_col"][:], B[1][:, 0:CC],
                                       sb["fcb_col"][:], OP.add))

    if "casts" not in skip:
        for src_, dst_ in [("wo", "c_wo"), ("projw", "c_projw")]:
            nr = dr[src_].shape[0]
            for k in range(0, nr, 128):
                em.wait("sync", "ev")
                dma("sync", sb["wf32b"][:, 0:D],
                    dr[src_].ap()[k:k + 128, :], "wdma")
                em.wait("vector", "wdma")
                vopd(lambda e: e.tensor_copy(sb["wbf"][:, 0:D],
                                             sb["wf32b"][:, 0:D]))
                em.wait("sync", "ev")
                dma("sync", dr[dst_].ap()[k:k + 128, :], sb["wbf"][:, 0:D],
                    "wdma")

    def build_ternary(src, dst, mi, blocked):
        nchunk = dr[src].shape[0] // 128
        width = dr[src].shape[1]
        vop(lambda e: e.memset(sb["sc_a"][:, 0:1], 0.0), None)
        for k in range(nchunk):
            em.wait("sync", "ev")
            dma("sync", sb["wf32b"][:, 0:width],
                dr[src].ap()[128 * k:128 * k + 128, :], "wdma")
            em.wait("vector", "wdma")
            vop(lambda e, w=width: e.tensor_reduce(
                sb["sc_b"][:, 0:1], sb["wf32b"][:, 0:w], AX.X, OP.add,
                apply_absolute_value=True), None)
            vop(lambda e: e.drain(), None)
            vop(lambda e: e.tensor_tensor(sb["sc_a"][:, 0:1], sb["sc_a"][:, 0:1],
                                          sb["sc_b"][:, 0:1], OP.add), "ev")
        em.wait("tensor", "ev")
        top(lambda t: t.matmul(B[2][0:1, 0:1], sb["onesf"][:], sb["sc_a"][:, 0:1],
                               start=True, stop=True))
        em.wait("vector", "mm")
        tot = float(nchunk * 128 * width)
        vopd(lambda e, tot=tot: e.tensor_scalar(
            sb["sc_b"][0:1, 0:1], B[2][0:1, 0:1], 1.0 / tot, 1e-5,
            OP.mult, OP.max), "ev")
        vop(lambda e: e.drain(), None)
        vopd(lambda e: e.reciprocal(sb["sc_b"][0:1, 1:2], sb["sc_b"][0:1, 0:1]))
        em.wait("sync", "ev")
        dma("sync", bass.AP(dr["b_row"], 0, [[1, 1], [1, 2]]), sb["sc_b"][0:1, 0:2],
            "wdma")
        em.wait("gpsimd", "wdma")
        dma("gpsimd", sb["mbar"][:, mi:mi + 1],
            bass.AP(dr["b_row"], 0, [[0, 128], [1, 1]]), "wdma")
        dma("gpsimd", sb["mbar"][:, mi + 2:mi + 3],
            bass.AP(dr["b_row"], 1, [[0, 128], [1, 1]]), "wdma")
        em.wait("vector", "wdma")
        for k in range(nchunk):
            dma("sync", sb["wf32b"][:, 0:width],
                dr[src].ap()[128 * k:128 * k + 128, :], "wdma")
            em.wait("vector", "wdma")
            scol = sb["mbar"][:, mi + 2:mi + 3]
            vop(lambda e, w=width, scol=scol: e.tensor_scalar(
                sb["wf32b"][:, 0:w], sb["wf32b"][:, 0:w], scol, 1.0,
                OP.mult, OP.min), None)
            vop(lambda e, w=width: e.tensor_scalar(
                sb["wf32b"][:, 0:w], sb["wf32b"][:, 0:w], -1.0, MAGIC,
                OP.max, OP.add), None)
            vopd(lambda e, w=width: e.tensor_scalar(
                sb["wbf"][:, 0:w], sb["wf32b"][:, 0:w], MAGIC, None, OP.subtract))
            em.wait("sync", "ev")
            if blocked:
                ncb = width // 128
                dma("sync", bass.AP(dr[dst], k * 16384,
                                    [[128, 128], [KC * 16384, ncb], [1, 128]]),
                    bass.AP(sb["wbf"], 0, [[2 * MLP, 128], [128, ncb], [1, 128]]),
                    "wdma")
            else:
                dma("sync", dr[dst].ap()[128 * k:128 * k + 128, :],
                    sb["wbf"][:, 0:width], "wdma")
            em.wait("vector", "wdma")

    if "ternary" not in skip:
        build_ternary("w1T", "c_w1t", 0, True)
        build_ternary("w2T", "c_w2t", 1, False)

    for j, nm in enumerate(["convw0", "convw1", "convw2"]):
        dmas("sync", bass.AP(sb["convw_col"], j * CC, [[3 * CC, 128], [1, CC]]),
            bass.AP(dr[nm], 0, [[1, 128], [128, CC]]), "wdma")
    em.wait("vector", "wdma")
    vop(lambda e: e.tensor_scalar(sb["convw_col"][:], sb["convw_col"][:],
                                  sb["mbar"][:, 0:1], None, OP.mult))

    for eng in ("sync", "tensor", "vector", "scalar", "gpsimd"):
        for s in ["wdma", "ev", "mm", "gp", "act", "ldc"]:
            em.wait(eng, s)

    pro_es.close()
    HT = TT // 2
    st("fq_g2", [128, 64 * G], BF16)
    st("fk_g2", [128, 64 * G], BF16)
    st("v_g2", [128, 64 * G], BF16)
    st("fqT", [128, 2 * 128], BF16)
    st("fkT", [128, 2 * 128], BF16)
    st("S2", [128, 8 * 128], BF16)
    st("rc_c", [128, G], F32)
    st("attn_g2", [128, 64 * G], BF16)
    st("xq1w_a", [128, KC * WIN], BF16)
    st("xq1w_b", [128, KC * WIN], BF16)
    st("hsil", [128, CC * HT], BF16)
    st("hwin", [128, 260], BF16)
    st("cv1", [128, HT], BF16)
    st("cv2", [128, HT], BF16)
    st("rmax", [128, HT], BF16)
    st("rmin", [128, HT], BF16)
    st("hq_c", [128, 2 * HT], BF16)
    st("gch", [128, 2 * 256], BF16)
    st("s2bc", [128, HT], F32)
    st("s2row", [1, HT], F32)
    vop(lambda e: e.memset(sb["xq1w_a"][:], 0.0), None)
    vop(lambda e: e.memset(sb["xq1w_b"][:], 0.0), "ev")
    for eng in ("sync", "tensor", "scalar", "gpsimd"):
        em.wait(eng, "ev")

    def ln_finish(off):
        ssl = sb["stats"][:, off:off + MC]
        qsl = sb["stats2"][:, off:off + MC]
        a = sb["sc_a"][:, 0:MC]
        b = sb["sc_b"][:, 0:MC]
        vop(lambda e: e.drain(), None)
        vop(lambda e: e.tensor_scalar(ssl, ssl, 1.0 / D, None, OP.mult), None)
        vop(lambda e: e.drain(), None)
        vop(lambda e: e.tensor_tensor(a, ssl, ssl, OP.mult), None)
        vop(lambda e: e.tensor_scalar(qsl, qsl, 1.0 / D, None, OP.mult), None)
        vop(lambda e: e.drain(), None)
        vop(lambda e: e.tensor_tensor(qsl, qsl, a, OP.subtract), None)
        vop(lambda e: e.drain(), None)
        vop(lambda e: e.tensor_scalar(qsl, qsl, 1e-5, None, OP.add), "ev")
        em.wait("scalar", "ev")
        sop(lambda s: s.activation(a, qsl, AF.Ln), "act")
        sop(lambda s: s.activation(qsl, a, AF.Exp, scale=-0.5), "act")
        em.wait("vector", "act")

    def ln_stats(tile, off):
        em.wait("scalar", "ev")
        for m in range(MC):
            sl = tile[:, D * m:D * (m + 1)]
            vop(lambda e, sl=sl, m=m: e.tensor_reduce(
                sb["stats"][:, off + m:off + m + 1], sl, AX.X, OP.add), None)
            sop(lambda s, sl=sl, m=m: s.activation(
                sb["junk"][:], sl, AF.Square,
                accum_out=sb["stats2"][:, off + m:off + m + 1]), "act")
        em.wait("vector", "act")
        ln_finish(off)

    def ln_norm(src, dst, off):
        for m in range(MC):
            ssl = src[:, D * m:D * (m + 1)]
            dsl = dst[:, D * m:D * (m + 1)]
            mcol = sb["stats"][:, off + m:off + m + 1]
            rcol = sb["stats2"][:, off + m:off + m + 1]
            last = (m == MC - 1)
            (vopd if last else vop)(
                lambda e, ssl=ssl, dsl=dsl, mcol=mcol, rcol=rcol: e.tensor_scalar(
                    dsl, ssl, mcol, rcol, OP.subtract, OP.mult))

    def tm_to_fm(tm_tile, fm_tile, fstride=TT, foff=0):
        evmark = [em.cnt["sv"], em.cnt["sv"]]
        for idx in range(MC * KC):
            m, k = idx // KC, idx % KC
            bank = B[6 + (idx % 2)]
            pv = bank.ap().bitcast(BF16)[:, 0:128]
            em.wait("tensor", "sv", evmark[idx % 2])
            top(lambda t, m=m, k=k, pv=pv, tm=tm_tile: t.transpose(
                pv, tm[:, D * m + 128 * k:D * m + 128 * k + 128],
                sb["identb"][:]), "tr")
            trmark = em.cnt["tr"]
            dsl = fm_tile[:, fstride * k + foff + 128 * m:
                          fstride * k + foff + 128 * m + 128]
            if idx % 2 == 0:
                em.wait("vector", "tr", trmark)
                vopd(lambda e, dsl=dsl, pv=pv: e.tensor_copy(dsl, pv), "sv")
            else:
                em.wait("scalar", "tr", trmark)
                sopd(lambda s, dsl=dsl, pv=pv: s.copy(dsl, pv), "sv")
            evmark[idx % 2] = em.cnt["sv"]

    x_d = dr["x"].ap()
    y_d = dr["y"].ap()
    FS = 64 * G

    def phase_attn(ti):
        x1cur = sb["x1a"] if ti % 2 == 0 else sb["x1b"]
        wcur = sb["xq1w_a"] if ti % 2 == 0 else sb["xq1w_b"]
        wprev = sb["xq1w_b"] if ti % 2 == 0 else sb["xq1w_a"]

        em.wait("sync", "ydma")
        em.wait("sync", "ev")
        for m in range(MC):
            dma("sync", sb["x_tm"][:, D * m:D * (m + 1)],
                x_d[ti * TT + 128 * m:ti * TT + 128 * (m + 1), :], "xdma")
        em.wait("vector", "xdma")
        em.wait("scalar", "xdma")

        ln_stats(sb["x_tm"], 0)
        ln_norm(sb["x_tm"], sb["tm_sc"], 0)
        em.wait("tensor", "ev")
        tm_to_fm(sb["tm_sc"], sb["fm_sc"])

        slot_mark = [0] * 8
        bank_mark = [em.cnt["ev"]] * 4
        tmsc_mark = [em.cnt["bnc"]] * MC
        gath_marks = {}
        for pi, (cname, brep, do_elu, bname) in enumerate([
                ("c_wq", sb["bq_rep"], True, "b_fq"),
                ("c_wk", sb["bk_rep"], True, "b_fk"),
                ("c_wv", sb["bv_rep"], False, "b_v")]):
            ld_mark = []
            for k in range(KC):
                g = pi * KC + k
                em.wait("sync", "mm", slot_mark[g % 8])
                dma("sync", sb["wbuf"][:, D * (g % 8):D * (g % 8 + 1)],
                    dr[cname].ap()[128 * k:128 * k + 128, :], "wdma")
                ld_mark.append(em.cnt["wdma"])
            for k in range(KC):
                g = pi * KC + k
                em.wait("tensor", "wdma", ld_mark[k])
                if k == 0:
                    for b in range(4):
                        em.wait("tensor", "ev", bank_mark[b])
                    if pi == 0:
                        em.wait("tensor", "sv")
                for m in range(MC):
                    for h2 in range(2):
                        lhsT = sb["fm_sc"][:, TT * k + 128 * m:
                                           TT * k + 128 * m + 128]
                        top(lambda t, lhsT=lhsT, m=m, h2=h2, g=g, k=k: t.matmul(
                            B[2 * m + h2][:, :], lhsT,
                            sb["wbuf"][:, D * (g % 8) + 512 * h2:
                                       D * (g % 8) + 512 * (h2 + 1)],
                            start=(k == 0), stop=(k == KC - 1)))
            mm_done = em.cnt["mm"]
            for k in range(KC):
                slot_mark[(pi * KC + k) % 8] = mm_done
            for m in range(MC):
                em.wait("vector", "mm", mm_done)
                em.wait("vector", "bnc", tmsc_mark[m])
                for h2 in range(2):
                    psl = B[2 * m + h2][:, :]
                    osl = sb["tm_sc"][:, D * m + 512 * h2:D * m + 512 * (h2 + 1)]
                    rsl = brep[:, 512 * h2:512 * (h2 + 1)]
                    ju2 = sb["ju2"][:, 512 * h2:512 * h2 + 512]
                    ju3 = sb["ju3"][:, 512 * h2:512 * h2 + 512]
                    jnk = sb["junk"][:, 512 * h2:512 * h2 + 512]
                    if do_elu:
                        vopd(lambda e, psl=psl, rsl=rsl, ju2=ju2:
                             e.scalar_tensor_tensor(ju2, psl, 1.0, rsl,
                                                    OP.mult, OP.add), "ev")
                        em.wait("scalar", "ev", em.cnt["ev"])
                        sop(lambda s, ju2=ju2, jnk=jnk: s.activation(
                            jnk, ju2, AF.Exp), "act")
                        sopd(lambda s, ju2=ju2, ju3=ju3: s.activation(
                            ju3, ju2, AF.Relu), "act")
                        em.wait("vector", "act", em.cnt["act"])
                        vopd(lambda e, osl=osl, jnk=jnk, ju3=ju3:
                             e.scalar_tensor_tensor(osl, jnk, 1.0, ju3,
                                                    OP.min, OP.add), "ev")
                    else:
                        vopd(lambda e, psl=psl, rsl=rsl, osl=osl:
                             e.scalar_tensor_tensor(osl, psl, 1.0, rsl,
                                                    OP.mult, OP.add), "ev")
                bank_mark[m] = em.cnt["ev"]
                em.wait("scalar", "ev", em.cnt["ev"])
                dma("scalar", dr[bname].ap()[128 * m:128 * (m + 1), :],
                    sb["tm_sc"][:, D * m:D * (m + 1)], "bnc")
                tmsc_mark[m] = em.cnt["bnc"]
            em.wait("scalar", "bnc", em.cnt["bnc"])
            tname = {"b_fq": "fq_g2", "b_fk": "fk_g2", "b_v": "v_g2"}[bname]
            for h in range(H):
                dma("scalar",
                    bass.AP(sb[tname], h * FS, [[16 * FS, 8], [64, G], [1, 64]]),
                    bass.AP(dr[bname], 64 * h, [[D, 8], [8 * D, G], [1, 64]]),
                    "gth")
            gath_marks[tname] = em.cnt["gth"]

        em.wait("tensor", "gth")
        for b8 in range(G // 8):
            for jj in range(4):
                j = b8 * 4 + jj
                pv_q = B[6].ap().bitcast(BF16)[:, 0:128]
                pv_k = B[7].ap().bitcast(BF16)[:, 0:128]
                em.wait("tensor", "sv")
                top(lambda t, j=j, pv=pv_q: t.transpose(
                    pv, sb["fq_g2"][:, 128 * j:128 * j + 128], sb["identb"][:]), "tr")
                top(lambda t, j=j, pv=pv_k: t.transpose(
                    pv, sb["fk_g2"][:, 128 * j:128 * j + 128], sb["identb"][:]), "tr")
                qs = sb["fqT"][:, 128 * (jj % 2):128 * (jj % 2) + 128]
                ks = sb["fkT"][:, 128 * (jj % 2):128 * (jj % 2) + 128]
                em.wait("vector", "tr")
                vopd(lambda e, qs=qs, pv=pv_q: e.tensor_copy(qs, pv), "sv")
                em.wait("scalar", "tr")
                sopd(lambda s, ks=ks, pv=pv_k: s.copy(ks, pv), "sv")
                em.wait("tensor", "sv")
                for dd in range(2):
                    slot = jj
                    bank = B[4] if dd == 0 else B[5]
                    top(lambda t, qs=qs, ks=ks, dd=dd, bank=bank, slot=slot:
                        t.matmul(bank[:, 128 * slot:128 * slot + 128],
                                 ks[64 * dd:64 * dd + 64, :],
                                 qs[64 * dd:64 * dd + 64, :],
                                 start=True, stop=True))
            em.wait("vector", "mm")
            mb4 = bass.AP(sb["maskS"], 0, [[128, 128], [0, 4], [1, 128]])
            vop(lambda e, mb4=mb4: e.tensor_tensor(
                sb["S2"][:, 0:512].rearrange("p (g c) -> p g c", g=4),
                B[4][:].rearrange("p (g c) -> p g c", g=4), mb4, OP.mult), None)
            vopd(lambda e, mb4=mb4: e.tensor_tensor(
                sb["S2"][:, 512:1024].rearrange("p (g c) -> p g c", g=4),
                B[5][:].rearrange("p (g c) -> p g c", g=4), mb4, OP.mult), "sv")
            em.wait("tensor", "sv")
            for gg in range(8):
                jj, dd = gg // 2, gg % 2
                gi = b8 * 8 + 2 * jj + dd
                s2sl = sb["S2"][:, 512 * dd + 128 * jj:512 * dd + 128 * jj + 128]
                top(lambda t, s2sl=s2sl, gi=gi: t.matmul(
                    B[3][:, gi:gi + 1], s2sl, sb["onesb"][:], start=True, stop=True))
                top(lambda t, s2sl=s2sl, gi=gi, gg=gg: t.matmul(
                    B[2][:, 64 * gg:64 * gg + 64], s2sl,
                    sb["v_g2"][:, 64 * gi:64 * gi + 64], start=True, stop=True))
            em.wait("vector", "mm")
            rsl = sb["rc_c"][:, 8 * b8:8 * b8 + 8]
            vop(lambda e, b8=b8: e.tensor_scalar(
                sb["sc_c"][:, 0:8], B[3][:, 8 * b8:8 * b8 + 8], 1e-6, None,
                OP.add), None)
            vop(lambda e: e.drain(), None)
            vop(lambda e, rsl=rsl: e.reciprocal(rsl, sb["sc_c"][:, 0:8]), None)
            vop(lambda e: e.drain(), None)
            rbc = bass.AP(sb["rc_c"], 8 * b8, [[G, 128], [1, 8], [0, 64]])
            vopd(lambda e, b8=b8, rbc=rbc: e.tensor_tensor(
                sb["attn_g2"][:, 512 * b8:512 * (b8 + 1)].rearrange(
                    "p (g e) -> p g e", g=8),
                B[2][:].rearrange("p (g e) -> p g e", g=8), rbc, OP.mult), "sv")
            em.wait("tensor", "sv")

        em.wait("scalar", "sv")
        dma("scalar", dr["b_at"].ap(), sb["attn_g2"][:], "bnc")
        em.wait("scalar", "bnc")
        for gi in range(G):
            m = (8 * gi) // 128
            r0 = (8 * gi) % 128
            out_ap = bass.AP(sb["tm_sc"], r0 * (MC * D) + D * m,
                             [[MC * D, 8], [64, 16], [1, 64]])
            in_ap = bass.AP(dr["b_at"], 64 * gi, [[16 * FS, 8], [FS, 16], [1, 64]])
            dma("scalar", out_ap, in_ap, "gth")
        em.wait("tensor", "gth")
        em.wait("vector", "gth")
        em.wait("scalar", "gth")
        tm_to_fm(sb["tm_sc"], sb["fm_sc"])

        em.wait("sync", "mm")
        for k in range(KC):
            dma("sync", sb["wbuf"][:, D * k:D * (k + 1)],
                dr["c_wo"].ap()[128 * k:128 * k + 128, :], "wdma")
        em.wait("tensor", "wdma")
        em.wait("tensor", "sv")
        for m in range(MC):
            pb = (B[2 * (m % 2)], B[2 * (m % 2) + 1])
            em.wait("tensor", "ev")
            for k in range(KC):
                lhsT = sb["fm_sc"][:, TT * k + 128 * m:TT * k + 128 * m + 128]
                for h2 in range(2):
                    top(lambda t, lhsT=lhsT, k=k, h2=h2, pb=pb: t.matmul(
                        pb[h2][:, :], lhsT,
                        sb["wbuf"][:, D * k + 512 * h2:D * k + 512 * (h2 + 1)],
                        start=(k == 0), stop=(k == KC - 1)))
            em.wait("vector", "mm")
            for h2 in range(2):
                psl = pb[h2][:, :]
                bsl = sb["bo_rep"][:, 512 * h2:512 * (h2 + 1)]
                xsl = sb["x_tm"][:, D * m + 512 * h2:D * m + 512 * (h2 + 1)]
                osl = sb["xmid_tm"][:, D * m + 512 * h2:D * m + 512 * (h2 + 1)]
                vop(lambda e, psl=psl, bsl=bsl: e.scalar_tensor_tensor(
                    sb["ju2"][:, 0:512], psl, 1.0, bsl, OP.mult, OP.add), None)
                vopd(lambda e, osl=osl, xsl=xsl: e.tensor_tensor(
                    osl, sb["ju2"][:, 0:512], xsl, OP.add), "ev")

        ln_stats(sb["xmid_tm"], 4)
        ln_norm(sb["xmid_tm"], sb["tm_sc"], 4)
        em.wait("tensor", "ev")
        tm_to_fm(sb["tm_sc"], sb["fm_sc"])

        em.wait("tensor", "sv")
        for half in range(2):
            for c in range(CC):
                wsl = sb["wslc2"][:, D * (c % 2):D * (c % 2) + D]
                em.wait("sync", "mm")
                dma("sync", wsl,
                    bass.AP(dr["c_fcw"], c * KC * 16384,
                            [[128, 128], [16384, KC], [1, 128]]), "wdma")
                em.wait("tensor", "wdma")
                fcbank = B[4 + (c % 2)]
                em.wait("tensor", "act")
                em.wait("scalar", "mm")
                for k in range(KC):
                    rhs = sb["fm_sc"][:, TT * k + 256 * half:
                                      TT * k + 256 * half + 256]
                    top(lambda t, wsl=wsl, k=k, rhs=rhs, fcbank=fcbank: t.matmul(
                        fcbank[:, 0:256], wsl[:, 128 * k:128 * k + 128], rhs,
                        start=(k == 0), stop=(k == KC - 1)))
                em.wait("scalar", "mm")
                gsl = sb["gch"][:, 256 * (c % 2):256 * (c % 2) + 256]
                sopd(lambda s, gsl=gsl, fcbank=fcbank, c=c: s.activation(
                    gsl, fcbank[:, 0:256], AF.Gelu,
                    bias=sb["fcb_col"][:, c:c + 1], scale=1.0), "act")
                wpsl = sb["wslc"][:, D * (c % 2):D * (c % 2) + D]
                em.wait("sync", "mm")
                dma("sync", wpsl, dr["c_projw"].ap()[128 * c:128 * c + 128, :],
                    "wdma")
                em.wait("tensor", "wdma")
                em.wait("tensor", "act")
                for mloc in range(2):
                    for h2 in range(2):
                        top(lambda t, gsl=gsl, mloc=mloc, h2=h2, wpsl=wpsl, c=c:
                            t.matmul(B[2 * mloc + h2][:, :],
                                     gsl[:, 128 * mloc:128 * mloc + 128],
                                     wpsl[:, 512 * h2:512 * (h2 + 1)],
                                     start=(c == 0), stop=(c == CC - 1)))
            em.wait("vector", "mm")
            for mloc in range(2):
                m = 2 * half + mloc
                for h2 in range(2):
                    psl = B[2 * mloc + h2][:, :]
                    bsl = sb["projb_rep"][:, 512 * h2:512 * (h2 + 1)]
                    midsl = sb["xmid_tm"][:, D * m + 512 * h2:D * m + 512 * (h2 + 1)]
                    xsl = sb["x_tm"][:, D * m + 512 * h2:D * m + 512 * (h2 + 1)]
                    vop(lambda e, psl=psl, bsl=bsl: e.scalar_tensor_tensor(
                        sb["ju2"][:, 0:512], psl, 1.0, bsl, OP.mult, OP.add), None)
                    vop(lambda e, midsl=midsl: e.tensor_tensor(
                        sb["ju3"][:, 0:512], sb["ju2"][:, 0:512], midsl, OP.add),
                        None)
                    vopd(lambda e, xsl=xsl: e.tensor_tensor(
                        xsl, sb["ju3"][:, 0:512], xsl, OP.add), "ev")
            em.wait("tensor", "ev")

        ln_stats(sb["x_tm"], 0)
        ln_norm(sb["x_tm"], x1cur, 0)

        for m in range(MC):
            sl = x1cur[:, D * m:D * (m + 1)]
            vop(lambda e, sl=sl, m=m: e.tensor_reduce(
                sb["qscale"][:, m:m + 1], sl, AX.X, OP.max,
                apply_absolute_value=True), None)
        vop(lambda e: e.drain(), None)
        vop(lambda e: e.tensor_scalar(sb["qscale"][:, 0:MC], sb["qscale"][:, 0:MC],
                                      1e-5, None, OP.max), None)
        vop(lambda e: e.drain(), None)
        vop(lambda e: e.reciprocal(sb["sc_c"][:, 0:MC], sb["qscale"][:, 0:MC]), None)
        vop(lambda e: e.tensor_scalar(sb["qscale"][:, 4:4 + MC],
                                      sb["qscale"][:, 0:MC], 1.0 / 127.0, None,
                                      OP.mult), None)
        vop(lambda e: e.drain(), None)
        vop(lambda e: e.tensor_scalar(sb["qscale"][:, 0:MC], sb["sc_c"][:, 0:MC],
                                      127.0, None, OP.mult), None)
        vop(lambda e: e.drain(), None)
        for m in range(MC):
            ssl = x1cur[:, D * m:D * (m + 1)]
            scol = sb["qscale"][:, m:m + 1]
            dcol = sb["qscale"][:, 4 + m:5 + m]
            osl = sb["tm_sc"][:, D * m:D * (m + 1)]
            vop(lambda e, ssl=ssl, scol=scol: e.tensor_scalar(
                sb["junk"][:], ssl, scol, MAGIC, OP.mult, OP.add), None)
            vopd(lambda e, osl=osl, dcol=dcol: e.tensor_scalar(
                osl, sb["junk"][:], MAGIC, dcol, OP.subtract, OP.mult), "ev")
        em.wait("tensor", "ev")
        tm_to_fm(sb["tm_sc"], wcur, fstride=WIN, foff=1)
        em.wait("vector", "sv")
        for k in range(KC):
            if ti > 0:
                vop(lambda e, k=k, wc=wcur, wp=wprev: e.tensor_copy(
                    wc[:, WIN * k:WIN * k + 1], wp[:, WIN * k + TT:WIN * k + TT + 1]),
                    None)
            (vopd if k == KC - 1 else vop)(
                lambda e, k=k, wc=wcur, wp=wprev: e.tensor_copy(
                    wp[:, WIN * k + TT + 1:WIN * k + TT + 2],
                    wc[:, WIN * k + 1:WIN * k + 2]), "ev")

    def phase_ffn(tj, last=False):
        x1p = sb["x1a"] if tj % 2 == 0 else sb["x1b"]
        wp = sb["xq1w_a"] if tj % 2 == 0 else sb["xq1w_b"]
        HT = TT // 2
        if last:
            for k in range(KC):
                (vopd if k == KC - 1 else vop)(lambda e, k=k, wp=wp: e.memset(
                    wp[:, WIN * k + TT + 1:WIN * k + TT + 2], 0.0), "ev")
            em.wait("tensor", "ev")
        for half in range(2):
            vop(lambda e: e.memset(sb["rmax"][:], 0.0), None)
            vop(lambda e: e.memset(sb["rmin"][:], 0.0), "ev")
            for c in range(CC):
                wsl = sb["wslc2"][:, D * (c % 2):D * (c % 2) + D]
                em.wait("sync", "mm")
                dma("sync", wsl,
                    bass.AP(dr["c_w1t"], c * KC * 16384,
                            [[128, 128], [16384, KC], [1, 128]]), "wdma")
                em.wait("tensor", "wdma")
                bA = B[4 + (c % 2)]
                em.wait("tensor", "ev")
                for k in range(KC):
                    lh = wsl[:, 128 * k:128 * k + 128]
                    top(lambda t, lh=lh, bA=bA, wp=wp, k=k, half=half: t.matmul(
                        bA[:, 0:258], lh,
                        wp[:, WIN * k + HT * half:WIN * k + HT * half + 258],
                        start=(k == 0), stop=(k == KC - 1)))
                em.wait("vector", "mm")
                vopd(lambda e, bA=bA: e.tensor_copy(sb["hwin"][:, 0:258],
                                                    bA[:, 0:258]), "ev")
                w0c = sb["convw_col"][:, c:c + 1]
                w1c = sb["convw_col"][:, CC + c:CC + c + 1]
                w2c = sb["convw_col"][:, 2 * CC + c:2 * CC + c + 1]
                vop(lambda e, w1c=w1c: e.tensor_scalar(
                    sb["cv1"][:], sb["hwin"][:, 1:257], w1c, None, OP.mult), None)
                vop(lambda e, w0c=w0c: e.scalar_tensor_tensor(
                    sb["cv2"][:], sb["hwin"][:, 0:256], w0c, sb["cv1"][:],
                    OP.mult, OP.add), None)
                vopd(lambda e, w2c=w2c: e.scalar_tensor_tensor(
                    sb["cv1"][:], sb["hwin"][:, 2:258], w2c, sb["cv2"][:],
                    OP.mult, OP.add), "ev")
                em.wait("scalar", "ev")
                hs = sb["hsil"][:, HT * c:HT * (c + 1)]
                sopd(lambda s, hs=hs, c=c: s.activation(
                    hs, sb["cv1"][:], AF.Silu, bias=sb["convb_col"][:, c:c + 1],
                    scale=1.0), "act")
                em.wait("gpsimd", "act")
                em.wait("vector", "act")
                vop(lambda e, hs=hs: e.tensor_tensor(sb["rmax"][:], sb["rmax"][:],
                                                     hs, OP.max), None)
                vop(lambda e, hs=hs: e.tensor_tensor(sb["rmin"][:], sb["rmin"][:],
                                                     hs, OP.min), "ev")

            for mloc in range(2):
                m = 2 * half + mloc
                pvA = B[6].ap().bitcast(BF16)[:, 0:128]
                pvB = B[6].ap().bitcast(BF16)[:, 128:256]
                top(lambda t, mloc=mloc, pvA=pvA: t.transpose(
                    pvA, sb["rmax"][:, 128 * mloc:128 * mloc + 128],
                    sb["identb"][:]), "tr")
                top(lambda t, mloc=mloc, pvB=pvB: t.transpose(
                    pvB, sb["rmin"][:, 128 * mloc:128 * mloc + 128],
                    sb["identb"][:]), "tr")
                em.wait("vector", "tr")
                vop(lambda e, pvA=pvA: e.tensor_reduce(
                    sb["sc_a"][:, 0:1], pvA, AX.X, OP.max,
                    apply_absolute_value=True), None)
                vopd(lambda e, pvB=pvB: e.tensor_reduce(
                    sb["sc_a"][:, 1:2], pvB, AX.X, OP.max,
                    apply_absolute_value=True), "ev")
                vopd(lambda e, m=m: e.tensor_tensor(
                    sb["q2s"][:, m:m + 1], sb["sc_a"][:, 0:1], sb["sc_a"][:, 1:2],
                    OP.max), "sv")
                em.wait("tensor", "sv")
            c0 = 2 * half
            vop(lambda e: e.drain(), None)
            vop(lambda e, c0=c0: e.tensor_scalar(
                sb["q2s"][:, c0:c0 + 2], sb["q2s"][:, c0:c0 + 2], 1e-5, None,
                OP.max), None)
            vop(lambda e: e.drain(), None)
            vop(lambda e, c0=c0: e.reciprocal(sb["sc_c"][:, 0:2],
                                              sb["q2s"][:, c0:c0 + 2]), None)
            vop(lambda e, c0=c0: e.tensor_scalar(
                sb["q2s"][:, 4 + c0:6 + c0], sb["q2s"][:, c0:c0 + 2],
                1.0 / 127.0, None, OP.mult), None)
            vop(lambda e: e.drain(), None)
            vop(lambda e, c0=c0: e.tensor_tensor(
                sb["q2s"][:, 4 + c0:6 + c0], sb["q2s"][:, 4 + c0:6 + c0],
                sb["mbar"][:, 1:2].broadcast_to([128, 2]), OP.mult), None)
            vopd(lambda e, c0=c0: e.tensor_scalar(
                sb["q2s"][:, c0:c0 + 2], sb["sc_c"][:, 0:2], 127.0, None,
                OP.mult), "ev")
            em.wait("tensor", "ev")
            for mloc in range(2):
                m = 2 * half + mloc
                top(lambda t, m=m, mloc=mloc: t.transpose(
                    B[7][0:1, 128 * mloc:128 * mloc + 128], sb["q2s"][:, m:m + 1],
                    sb["identf"][:]), "tr")
            em.wait("vector", "tr")
            vopd(lambda e: e.tensor_copy(sb["s2row"][:], B[7][0:1, 0:HT]), "sv")
            em.wait("tensor", "sv")
            top(lambda t: t.matmul(B[7][:, 0:HT], sb["onesrow"][:], sb["s2row"][:],
                                   start=True, stop=True))
            em.wait("vector", "mm")
            vopd(lambda e: e.tensor_copy(sb["s2bc"][:], B[7][:, 0:HT]), "ev")

            em.wait("tensor", "ev")
            for c in range(CC):
                hs = sb["hsil"][:, HT * c:HT * (c + 1)]
                hq = sb["hq_c"][:, HT * (c % 2):HT * (c % 2) + HT]
                em.wait("vector", "mm")
                vop(lambda e, hs=hs: e.tensor_tensor(
                    sb["ju2"][:, 0:HT], hs, sb["s2bc"][:], OP.mult), None)
                vopd(lambda e, hq=hq: e.tensor_scalar(
                    hq, sb["ju2"][:, 0:HT], MAGIC, MAGIC, OP.add, OP.subtract),
                    "ev")
                wsl = sb["wslc"][:, D * (c % 2):D * (c % 2) + D]
                em.wait("sync", "mm")
                dma("sync", wsl, dr["c_w2t"].ap()[128 * c:128 * c + 128, :], "wdma")
                em.wait("tensor", "wdma")
                em.wait("tensor", "ev")
                for mloc in range(2):
                    for h2 in range(2):
                        top(lambda t, hq=hq, mloc=mloc, h2=h2, wsl=wsl, c=c:
                            t.matmul(B[2 * mloc + h2][:, :],
                                     hq[:, 128 * mloc:128 * mloc + 128],
                                     wsl[:, 512 * h2:512 * (h2 + 1)],
                                     start=(c == 0), stop=(c == CC - 1)))
            em.wait("vector", "mm")
            for mloc in range(2):
                m = 2 * half + mloc
                for h2 in range(2):
                    psl = B[2 * mloc + h2][:, :]
                    x1sl = x1p[:, D * m + 512 * h2:D * m + 512 * (h2 + 1)]
                    osl = sb["xmid_tm"][:, D * m + 512 * h2:D * m + 512 * (h2 + 1)]
                    dcol = sb["q2s"][:, 4 + m:5 + m]
                    vopd(lambda e, psl=psl, x1sl=x1sl, osl=osl, dcol=dcol:
                         e.scalar_tensor_tensor(osl, psl, dcol, x1sl,
                                                OP.mult, OP.add), "ev")
            em.wait("tensor", "ev")

        ln_stats(sb["xmid_tm"], 8)
        ln_norm(sb["xmid_tm"], sb["x_tm"], 8)
        em.wait("sync", "ev")
        for m in range(MC):
            dma("sync", y_d[tj * TT + 128 * m:tj * TT + 128 * (m + 1), :],
                sb["x_tm"][:, D * m:D * (m + 1)], "ydma")

    with nc.Block() as block:
        for ti in range(nt_limit):
            phase_attn(ti)
            if ti > 0 and do_ffn:
                phase_ffn(ti - 1)
        if do_ffn:
            phase_ffn(nt_limit - 1, last=True)
        if dump:
            for eng in ("sync", "tensor", "vector", "scalar", "gpsimd"):
                for s in ["wdma", "xdma", "ydma", "bnc", "gth", "mm", "tr",
                          "ev", "sv", "gp", "act"]:
                    em.wait(eng, s)
            for nm in dump:
                t = sb[nm]
                shp = list(t.shape)
                dbg = nc.dram_tensor("dbg_" + nm, shp, t.dtype,
                                     kind="ExternalOutput")
                dma("sync", dbg.ap(), t[:], "ydma")
        em.wait("sync", "ydma")
        em.wait("gpsimd", "ydma")

        @block.sync
        def _(e):
            for fn in em.lists["sync"]:
                fn(e)

        @block.tensor
        def _(e):
            for fn in em.lists["tensor"]:
                fn(e)

        @block.vector
        def _(e):
            for fn in em.lists["vector"]:
                fn(e)

        @block.scalar
        def _(e):
            for fn in em.lists["scalar"]:
                fn(e)

        @block.gpsimd
        def _(e):
            for fn in em.lists["gpsimd"]:
                fn(e)

    es.close()
    return nc


_NC_CACHE = None


def _get_nc():
    global _NC_CACHE
    if _NC_CACHE is None:
        _NC_CACHE = build_program()
    return _NC_CACHE


def kernel(**inputs):
    ii = {k: np.asarray(v) for k, v in inputs.items()}
    x = np.ascontiguousarray(ii["L0_attn_q_x"].astype(np.float32, copy=False))
    shared = {
        "wq": ii["L0_attn_wq"], "wk": ii["L0_attn_wk"], "wv": ii["L0_attn_wv"],
        "wo": ii["L0_attn_wo"], "fcw": ii["L0_attn_fc_w"],
        "projw": ii["L0_attn_proj_w"],
        "w1T": np.ascontiguousarray(ii["L0_ffn_w1"].T),
        "w2T": np.ascontiguousarray(ii["L0_ffn_w2"].T),
        "ln1w": ii["L0_attn_ln1_w"], "ln1b": ii["L0_attn_ln1_b"],
        "ln2w": ii["L0_attn_ln2_w"], "ln2b": ii["L0_attn_ln2_b"],
        "bq": ii["L0_attn_bq"], "bk": ii["L0_attn_bk"], "bv": ii["L0_attn_bv"],
        "bo": ii["L0_attn_bo"], "fcb": ii["L0_attn_fc_b"],
        "projb": ii["L0_attn_proj_b"],
        "convw0": np.ascontiguousarray(ii["L0_ffn_conv_w"][:, 0, 0]),
        "convw1": np.ascontiguousarray(ii["L0_ffn_conv_w"][:, 0, 1]),
        "convw2": np.ascontiguousarray(ii["L0_ffn_conv_w"][:, 0, 2]),
        "identb": np.eye(128, dtype=ml_dtypes.bfloat16),
        "identf": np.eye(128, dtype=np.float32),
        "maskS": np.kron(np.eye(8, dtype=np.float32), np.ones((16, 16), np.float32)),
        "convb": ii["L0_ffn_conv_b"],
    }
    shared = {k: (np.ascontiguousarray(v.astype(np.float32, copy=False))
                  if v.dtype != ml_dtypes.bfloat16 else v)
              for k, v in shared.items()}
    nc = _get_nc()
    in_maps = [dict(shared, x=np.ascontiguousarray(x[c])) for c in range(NCORES)]
    res = run_bass_kernel_spmd(nc, in_maps, core_ids=list(range(NCORES)))
    out = np.stack([res.results[c]["y"] for c in range(NCORES)], axis=0)
    return out.astype(np.float32, copy=False)


if __name__ == "__main__":
    nc = build_program()
    print("built")


# version 14
# speedup vs baseline: 1.5166x; 1.4587x over previous
SERIAL_FC = False
"""Fused megakernel for mk_seq_postln_d1 (linear-attention block + BitNet FFN).

One raw-Bass program per NeuronCore, launched exactly once via
run_bass_kernel_spmd.  Batch (8) sharded across the 8 cores; each core runs
the entire forward for its own 4096-token sequence.

Structure:
  PROLOGUE  - build per-core bf16 weight caches in HBM (LN gains folded into
              weights, biases pre-folded via on-device matmuls, ternary
              quantization of w1/w2 done in fp32), load constants.
  MAIN LOOP - 8 token tiles of 512.  Per tile: LN1 -> q/k/v (token-major) ->
              elu+1 -> DRAM bounce -> grouped gathers -> 8-token-group
              block-diagonal attention matmuls -> wo -> LN2 -> fc/gelu/proj
              -> post-LN -> act-quant -> transposed window.  The BitNet FFN
              for tile i runs during iteration i+1 (the depthwise conv needs
              one token of lookahead).

Layouts: TM = [128 tokens x feat] (chunk-major), FM = [128 feat x tokens],
g2 = [128 = 16n+h x 64d] 8-token attention groups.
"""
import numpy as np
import ml_dtypes
import concourse.bass as bass
import concourse.mybir as mybir
from concourse.bass_utils import run_bass_kernel_spmd
from contextlib import ExitStack

F32 = mybir.dt.float32
BF16 = mybir.dt.bfloat16
I32 = mybir.dt.int32
OP = mybir.AluOpType
AF = mybir.ActivationFunctionType
AX = mybir.AxisListType

D = 1024
H = 16
MLP = 4096
N = 4096
TT = 512
NT = N // TT
MC = TT // 128
KC = D // 128
CC = MLP // 128
G = TT // 8
WIN = TT + 2
MAGIC = float(1.5 * 2 ** 23)
RSQRT_MAGIC = 0x5f3759df
NCORES = 8


class Emitter:
    def __init__(self):
        self.lists = {e: [] for e in ("sync", "tensor", "vector", "scalar", "gpsimd")}
        self.sems = {}
        self.cnt = {}

    def op(self, eng, fn, inc=None, n=1):
        if inc is not None:
            self.cnt[inc] += n
            sem = self.sems[inc]
            self.lists[eng].append(lambda e, fn=fn, sem=sem, n=n: fn(e).then_inc(sem, n))
        else:
            self.lists[eng].append(lambda e, fn=fn: fn(e))

    def wait(self, eng, name, val=None):
        v = self.cnt[name] if val is None else val
        sem = self.sems[name]
        self.lists[eng].append(lambda e, sem=sem, v=v: e.wait_ge(sem, v))


def build_program(nt_limit=NT, do_ffn=True, dump=(), skip=(), serial_fc=False,
                  barrier=()):
    global SERIAL_FC
    SERIAL_FC = int(serial_fc)
    nc = bass.Bass("TRN2", target_bir_lowering=False)
    es = ExitStack()
    E = es.enter_context

    dr = {}

    def dt(name, shape, dtype, kind="Internal"):
        dr[name] = nc.dram_tensor(name, shape, dtype, kind=kind)
        return dr[name]

    dt("x", [N, D], F32, "ExternalInput")
    for nm, shp in [("wq", [D, D]), ("wk", [D, D]), ("wv", [D, D]), ("wo", [D, D]),
                    ("fcw", [D, MLP]), ("projw", [MLP, D]),
                    ("w1T", [D, MLP]), ("w2T", [MLP, D]),
                    ("ln1w", [D]), ("ln1b", [D]), ("ln2w", [D]), ("ln2b", [D]),
                    ("bq", [D]), ("bk", [D]), ("bv", [D]), ("bo", [D]),
                    ("fcb", [MLP]), ("projb", [D]),
                    ("convw0", [MLP]), ("convw1", [MLP]), ("convw2", [MLP]),
                    ("convb", [MLP])]:
        dt(nm, shp, F32, "ExternalInput")
    dt("identb", [128, 128], BF16, "ExternalInput")
    dt("identf", [128, 128], F32, "ExternalInput")
    dt("maskS", [128, 128], F32, "ExternalInput")
    dt("y", [N, D], F32, "ExternalOutput")

    dt("c_wq", [D, D], BF16)
    dt("c_wk", [D, D], BF16)
    dt("c_wv", [D, D], BF16)
    dt("c_wo", [D, D], BF16)
    dt("c_fcw", [KC * CC * 128 * 128], BF16)
    dt("c_projw", [MLP, D], BF16)
    dt("c_w1t", [KC * CC * 128 * 128], BF16)
    dt("c_w2t", [MLP, D], BF16)
    dt("b_fq", [TT, D], BF16)
    dt("b_fk", [TT, D], BF16)
    dt("b_v", [TT, D], BF16)
    dt("b_at", [128, 64 * G], BF16)
    dt("b_row", [D], F32)

    sb = {}

    def st(name, shape, dtype):
        sb[name] = E(nc.sbuf_tensor("s_" + name, shape, dtype))
        return sb[name]

    st("identb", [128, 128], BF16)
    st("identf", [128, 128], F32)
    st("maskS", [128, 128], F32)
    st("onesb", [128, 1], BF16)
    st("onesf", [128, 1], F32)
    st("onesrow", [1, 128], F32)
    st("magici", [128, 16], I32)
    st("lnw_cols", [128, 2 * KC], F32)
    st("ln1b_col", [128, KC], BF16)
    st("ln2b_col", [128, KC], BF16)
    st("bias_sc", [128, KC], F32)
    st("fcb_col", [128, CC], F32)
    st("convw_col", [128, 3 * CC], F32)
    st("convb_col", [128, CC], F32)
    st("mbar", [128, 4], F32)
    st("bq_rep", [128, D], BF16)
    st("bk_rep", [128, D], BF16)
    st("bv_rep", [128, D], BF16)
    st("bo_rep", [128, D], BF16)
    st("projb_rep", [128, D], BF16)
    st("wbuf", [128, KC * D], BF16)
    st("wslc", [128, 2 * D], BF16)
    st("wslc2", [128, 2 * D], BF16)
    st("x_tm", [128, MC * D], F32)
    st("xmid_tm", [128, MC * D], F32)
    st("x1a", [128, MC * D], F32)
    st("x1b", [128, MC * D], F32)
    st("tm_sc", [128, MC * D], BF16)
    st("fm_sc", [128, KC * TT], BF16)
    st("stats", [128, 16], F32)
    st("stats2", [128, 16], F32)
    st("sc_a", [128, 16], F32)
    st("sc_b", [128, 16], F32)
    st("sc_c", [128, 16], F32)
    st("qscale", [128, 8], F32)
    st("q2s", [128, 8], F32)
    st("dbgk", [128, 256], F32)
    st("junk", [128, D], F32)
    st("ju2", [128, 1024], F32)
    st("ju3", [128, 1024], F32)

    B = {}
    for i in range(8):
        B[i] = E(nc.psum_tensor(f"bank{i}", [128, 512], F32))

    em = Emitter()
    for s in ["ldc", "wdma", "xdma", "ydma", "bnc", "gth", "mm", "tr", "ev",
              "sv", "gp", "act"]:
        em.sems[s] = E(nc.semaphore(s))
        em.cnt[s] = 0

    def vop(fn, inc="ev"):
        em.op("vector", fn, inc)

    def vopd(fn, inc="ev"):
        em.op("vector", fn, None)
        em.op("vector", lambda e: e.drain(), inc)

    def sop(fn, inc="act"):
        em.op("scalar", fn, inc)

    def sopd(fn, inc="act"):
        em.op("scalar", fn, None)
        em.op("scalar", lambda e: e.drain(), inc)

    def gop(fn, inc="gp"):
        em.op("gpsimd", fn, inc)

    def top(fn, inc="mm"):
        em.op("tensor", fn, inc)

    def dma(eng, out_ap, in_ap, inc):
        em.op(eng, lambda e, o=out_ap, i=in_ap: e.dma_start(out=o, in_=i), inc, 16)

    def dmas(eng, out_ap, in_ap, inc):
        em.op(eng, lambda e, o=out_ap, i=in_ap: e.dma_start(
            out=o, in_=i, allow_slow_non_contiguous=True), inc, 16)

    pro_es = ExitStack()
    sb["wf32b"] = pro_es.enter_context(nc.sbuf_tensor("s_wf32b", [128, MLP], F32))
    sb["wbf"] = pro_es.enter_context(nc.sbuf_tensor("s_wbf", [128, 2 * MLP], BF16))

    for nm in ["identb", "identf", "maskS"]:
        dma("sync", sb[nm][:], dr[nm].ap(), "ldc")
    for j, nm in enumerate(["ln1w", "ln2w"]):
        dmas("sync", bass.AP(sb["lnw_cols"], j * KC, [[2 * KC, 128], [1, KC]]),
            bass.AP(dr[nm], 0, [[1, 128], [128, KC]]), "ldc")
    dmas("sync", bass.AP(sb["convb_col"], 0, [[CC, 128], [1, CC]]),
        bass.AP(dr["convb"], 0, [[1, 128], [128, CC]]), "ldc")
    dmas("gpsimd", sb["bo_rep"][:], bass.AP(dr["bo"], 0, [[0, 128], [1, D]]), "ldc")
    dmas("gpsimd", sb["projb_rep"][:], bass.AP(dr["projb"], 0, [[0, 128], [1, D]]), "ldc")
    dmas("gpsimd", sb["ln1b_col"][:], bass.AP(dr["ln1b"], 0, [[1, 128], [128, KC]]), "ldc")
    dmas("gpsimd", sb["ln2b_col"][:], bass.AP(dr["ln2b"], 0, [[1, 128], [128, KC]]), "ldc")
    em.wait("vector", "ldc")
    vop(lambda e: e.memset(sb["onesb"][:], 1.0), None)
    vop(lambda e: e.memset(sb["onesf"][:], 1.0), None)
    vop(lambda e: e.memset(sb["magici"][:], RSQRT_MAGIC), None)
    vop(lambda e: e.memset(sb["onesrow"][:], 1.0))

    def build_attn_cache(wname, cname, borig, brep):
        dmas("sync", bass.AP(sb["bias_sc"], 0, [[KC, 128], [1, KC]]),
             bass.AP(dr[borig], 0, [[1, 128], [128, KC]]), "wdma")
        em.wait("vector", "wdma")
        for k in range(KC):
            dma("sync", sb["wf32b"][:, 0:D], dr[wname].ap()[128 * k:128 * k + 128, :],
                "wdma")
            em.wait("vector", "wdma")
            lcol = sb["lnw_cols"][:, k:k + 1]
            vop(lambda e, lcol=lcol: e.tensor_scalar(
                sb["wbf"][:, 0:D], sb["wf32b"][:, 0:D], lcol, None, OP.mult), None)
            vopd(lambda e: e.tensor_copy(sb["wbf"][:, D:2 * D], sb["wf32b"][:, 0:D]))
            em.wait("sync", "ev")
            dma("sync", dr[cname].ap()[128 * k:128 * k + 128, :], sb["wbf"][:, 0:D],
                "wdma")
            em.wait("tensor", "ev")
            for c in range(KC):
                top(lambda t, k=k, c=c: t.matmul(
                    B[0][:, c:c + 1], sb["wbf"][:, D + 128 * c:D + 128 * c + 128],
                    sb["ln1b_col"][:, k:k + 1], start=True, stop=True))
            em.wait("vector", "mm")
            vopd(lambda e: e.tensor_tensor(sb["bias_sc"][:], B[0][:, 0:KC],
                                           sb["bias_sc"][:], OP.add))
        em.wait("sync", "ev")
        dmas("sync", bass.AP(dr["b_row"], 0, [[1, 128], [128, KC]]),
            bass.AP(sb["bias_sc"], 0, [[KC, 128], [1, KC]]), "wdma")
        em.wait("gpsimd", "wdma")
        dma("gpsimd", brep[:], bass.AP(dr["b_row"], 0, [[0, 128], [1, D]]), "wdma")

    if "attn" not in skip:
        build_attn_cache("wq", "c_wq", "bq", sb["bq_rep"])
        build_attn_cache("wk", "c_wk", "bk", sb["bk_rep"])
        build_attn_cache("wv", "c_wv", "bv", sb["bv_rep"])

    dmas("sync", bass.AP(sb["fcb_col"], 0, [[CC, 128], [1, CC]]),
         bass.AP(dr["fcb"], 0, [[1, 128], [128, CC]]), "wdma")
    em.wait("vector", "wdma")
    for k in range(KC):
        dma("sync", sb["wf32b"][:], dr["fcw"].ap()[128 * k:128 * k + 128, :], "wdma")
        em.wait("vector", "wdma")
        lcol = sb["lnw_cols"][:, KC + k:KC + k + 1]
        vop(lambda e, lcol=lcol: e.tensor_scalar(
            sb["wbf"][:, 0:MLP], sb["wf32b"][:], lcol, None, OP.mult), None)
        vopd(lambda e: e.tensor_copy(sb["wbf"][:, MLP:2 * MLP], sb["wf32b"][:]))
        em.wait("sync", "ev")
        dma("sync", bass.AP(dr["c_fcw"], k * 16384,
                            [[128, 128], [KC * 16384, CC], [1, 128]]),
            bass.AP(sb["wbf"], 0, [[2 * MLP, 128], [128, CC], [1, 128]]), "wdma")
        em.wait("tensor", "ev")
        for c in range(CC):
            top(lambda t, k=k, c=c: t.matmul(
                B[1][:, c:c + 1], sb["wbf"][:, MLP + 128 * c:MLP + 128 * c + 128],
                sb["ln2b_col"][:, k:k + 1], start=True, stop=True))
        em.wait("vector", "mm")
        vopd(lambda e: e.tensor_tensor(sb["fcb_col"][:], B[1][:, 0:CC],
                                       sb["fcb_col"][:], OP.add))

    if "casts" not in skip:
        for src_, dst_ in [("wo", "c_wo"), ("projw", "c_projw")]:
            nr = dr[src_].shape[0]
            for k in range(0, nr, 128):
                em.wait("sync", "ev")
                dma("sync", sb["wf32b"][:, 0:D],
                    dr[src_].ap()[k:k + 128, :], "wdma")
                em.wait("vector", "wdma")
                vopd(lambda e: e.tensor_copy(sb["wbf"][:, 0:D],
                                             sb["wf32b"][:, 0:D]))
                em.wait("sync", "ev")
                dma("sync", dr[dst_].ap()[k:k + 128, :], sb["wbf"][:, 0:D],
                    "wdma")

    def build_ternary(src, dst, mi, blocked):
        nchunk = dr[src].shape[0] // 128
        width = dr[src].shape[1]
        vop(lambda e: e.memset(sb["sc_a"][:, 0:1], 0.0), None)
        for k in range(nchunk):
            em.wait("sync", "ev")
            dma("sync", sb["wf32b"][:, 0:width],
                dr[src].ap()[128 * k:128 * k + 128, :], "wdma")
            em.wait("vector", "wdma")
            vop(lambda e, w=width: e.tensor_reduce(
                sb["sc_b"][:, 0:1], sb["wf32b"][:, 0:w], AX.X, OP.add,
                apply_absolute_value=True), None)
            vop(lambda e: e.drain(), None)
            vop(lambda e: e.tensor_tensor(sb["sc_a"][:, 0:1], sb["sc_a"][:, 0:1],
                                          sb["sc_b"][:, 0:1], OP.add), "ev")
        em.wait("tensor", "ev")
        top(lambda t: t.matmul(B[2][0:1, 0:1], sb["onesf"][:], sb["sc_a"][:, 0:1],
                               start=True, stop=True))
        em.wait("vector", "mm")
        tot = float(nchunk * 128 * width)
        vopd(lambda e, tot=tot: e.tensor_scalar(
            sb["sc_b"][0:1, 0:1], B[2][0:1, 0:1], 1.0 / tot, 1e-5,
            OP.mult, OP.max), "ev")
        vop(lambda e: e.drain(), None)
        vopd(lambda e: e.reciprocal(sb["sc_b"][0:1, 1:2], sb["sc_b"][0:1, 0:1]))
        em.wait("sync", "ev")
        dma("sync", bass.AP(dr["b_row"], 0, [[1, 1], [1, 2]]), sb["sc_b"][0:1, 0:2],
            "wdma")
        em.wait("gpsimd", "wdma")
        dma("gpsimd", sb["mbar"][:, mi:mi + 1],
            bass.AP(dr["b_row"], 0, [[0, 128], [1, 1]]), "wdma")
        dma("gpsimd", sb["mbar"][:, mi + 2:mi + 3],
            bass.AP(dr["b_row"], 1, [[0, 128], [1, 1]]), "wdma")
        em.wait("vector", "wdma")
        for k in range(nchunk):
            dma("sync", sb["wf32b"][:, 0:width],
                dr[src].ap()[128 * k:128 * k + 128, :], "wdma")
            em.wait("vector", "wdma")
            scol = sb["mbar"][:, mi + 2:mi + 3]
            vop(lambda e, w=width, scol=scol: e.tensor_scalar(
                sb["wf32b"][:, 0:w], sb["wf32b"][:, 0:w], scol, 1.0,
                OP.mult, OP.min), None)
            vop(lambda e, w=width: e.tensor_scalar(
                sb["wf32b"][:, 0:w], sb["wf32b"][:, 0:w], -1.0, MAGIC,
                OP.max, OP.add), None)
            vopd(lambda e, w=width: e.tensor_scalar(
                sb["wbf"][:, 0:w], sb["wf32b"][:, 0:w], MAGIC, None, OP.subtract))
            em.wait("sync", "ev")
            if blocked:
                ncb = width // 128
                dma("sync", bass.AP(dr[dst], k * 16384,
                                    [[128, 128], [KC * 16384, ncb], [1, 128]]),
                    bass.AP(sb["wbf"], 0, [[2 * MLP, 128], [128, ncb], [1, 128]]),
                    "wdma")
            else:
                dma("sync", dr[dst].ap()[128 * k:128 * k + 128, :],
                    sb["wbf"][:, 0:width], "wdma")
            em.wait("vector", "wdma")

    if "ternary" not in skip:
        build_ternary("w1T", "c_w1t", 0, True)
        build_ternary("w2T", "c_w2t", 1, False)

    for j, nm in enumerate(["convw0", "convw1", "convw2"]):
        dmas("sync", bass.AP(sb["convw_col"], j * CC, [[3 * CC, 128], [1, CC]]),
            bass.AP(dr[nm], 0, [[1, 128], [128, CC]]), "wdma")
    em.wait("vector", "wdma")
    vop(lambda e: e.tensor_scalar(sb["convw_col"][:], sb["convw_col"][:],
                                  sb["mbar"][:, 0:1], None, OP.mult))

    for eng in ("sync", "tensor", "vector", "scalar", "gpsimd"):
        for s in ["wdma", "ev", "mm", "gp", "act", "ldc"]:
            em.wait(eng, s)

    pro_es.close()
    HT = TT // 2
    st("fq_g2", [128, 64 * G], BF16)
    st("fk_g2", [128, 64 * G], BF16)
    st("v_g2", [128, 64 * G], BF16)
    st("fqT", [128, 4 * 128], BF16)
    st("fkT", [128, 4 * 128], BF16)
    st("S2", [128, 8 * 128], BF16)
    st("rc_c", [128, G], F32)
    st("attn_g2", [128, 64 * G], BF16)
    st("xq1w_a", [128, KC * WIN], BF16)
    st("xq1w_b", [128, KC * WIN], BF16)
    st("hsil", [128, CC * HT], BF16)
    st("hwin", [128, 2 * 260], BF16)
    st("cv1", [128, 2 * HT], BF16)
    st("cv2", [128, 2 * HT], BF16)
    st("rmax", [128, HT], BF16)
    st("rmin", [128, HT], BF16)
    st("hq_c", [128, 2 * HT], BF16)
    st("gch", [128, 2 * 256], BF16)
    st("s2bc", [128, HT], F32)
    st("s2row", [1, HT], F32)
    vop(lambda e: e.memset(sb["xq1w_a"][:], 0.0), None)
    vop(lambda e: e.memset(sb["xq1w_b"][:], 0.0), "ev")
    for eng in ("sync", "tensor", "scalar", "gpsimd"):
        em.wait(eng, "ev")

    def ln_finish(off):
        ssl = sb["stats"][:, off:off + MC]
        qsl = sb["stats2"][:, off:off + MC]
        a = sb["sc_a"][:, 0:MC]
        b = sb["sc_b"][:, 0:MC]
        vop(lambda e: e.drain(), None)
        vop(lambda e: e.tensor_scalar(ssl, ssl, 1.0 / D, None, OP.mult), None)
        vop(lambda e: e.drain(), None)
        vop(lambda e: e.tensor_tensor(a, ssl, ssl, OP.mult), None)
        vop(lambda e: e.tensor_scalar(qsl, qsl, 1.0 / D, None, OP.mult), None)
        vop(lambda e: e.drain(), None)
        vop(lambda e: e.tensor_tensor(qsl, qsl, a, OP.subtract), None)
        vop(lambda e: e.drain(), None)
        vop(lambda e: e.tensor_scalar(qsl, qsl, 1e-5, None, OP.add), "ev")
        em.wait("scalar", "ev")
        sop(lambda s: s.activation(a, qsl, AF.Ln), "act")
        sop(lambda s: s.activation(qsl, a, AF.Exp, scale=-0.5), "act")
        em.wait("vector", "act")

    def ln_stats(tile, off):
        em.wait("scalar", "ev")
        for m in range(MC):
            sl = tile[:, D * m:D * (m + 1)]
            vop(lambda e, sl=sl, m=m: e.tensor_reduce(
                sb["stats"][:, off + m:off + m + 1], sl, AX.X, OP.add), None)
            sop(lambda s, sl=sl, m=m: s.activation(
                sb["junk"][:], sl, AF.Square,
                accum_out=sb["stats2"][:, off + m:off + m + 1]), "act")
        em.wait("vector", "act")
        ln_finish(off)

    def ln_norm(src, dst, off):
        for m in range(MC):
            ssl = src[:, D * m:D * (m + 1)]
            dsl = dst[:, D * m:D * (m + 1)]
            mcol = sb["stats"][:, off + m:off + m + 1]
            rcol = sb["stats2"][:, off + m:off + m + 1]
            last = (m == MC - 1)
            (vopd if last else vop)(
                lambda e, ssl=ssl, dsl=dsl, mcol=mcol, rcol=rcol: e.tensor_scalar(
                    dsl, ssl, mcol, rcol, OP.subtract, OP.mult))

    def tm_to_fm(tm_tile, fm_tile, fstride=TT, foff=0):
        em.wait("tensor", "ev", em.cnt["ev"])
        evmark = [em.cnt["sv"]] * (MC * KC)
        for idx in range(MC * KC):
            m, k = idx // KC, idx % KC
            bank = B[4 + (idx % 4)]
            pv = bank.ap().bitcast(BF16)[:, 0:128]
            em.wait("tensor", "sv", evmark[idx - 2] if idx >= 2 else evmark[0])
            top(lambda t, m=m, k=k, pv=pv, tm=tm_tile: t.transpose(
                pv, tm[:, D * m + 128 * k:D * m + 128 * k + 128],
                sb["identb"][:]), "tr")
            trmark = em.cnt["tr"]
            dsl = fm_tile[:, fstride * k + foff + 128 * m:
                          fstride * k + foff + 128 * m + 128]
            em.wait("vector", "tr", trmark)
            vopd(lambda e, dsl=dsl, pv=pv: e.tensor_copy(dsl, pv), "sv")
            evmark[idx] = em.cnt["sv"]

    x_d = dr["x"].ap()
    y_d = dr["y"].ap()
    FS = 64 * G

    def full_barrier():
        for eng in ("sync", "tensor", "vector", "scalar", "gpsimd"):
            for s in ["wdma", "xdma", "ydma", "bnc", "gth", "mm", "tr",
                      "ev", "sv", "gp", "act"]:
                em.wait(eng, s)

    def phase_attn(ti):
        if "attn" in barrier:
            full_barrier()
        x1cur = sb["x1a"] if ti % 2 == 0 else sb["x1b"]
        wcur = sb["xq1w_a"] if ti % 2 == 0 else sb["xq1w_b"]
        wprev = sb["xq1w_b"] if ti % 2 == 0 else sb["xq1w_a"]

        em.wait("sync", "ydma")
        em.wait("sync", "ev")
        for m in range(MC):
            dma("sync", sb["x_tm"][:, D * m:D * (m + 1)],
                x_d[ti * TT + 128 * m:ti * TT + 128 * (m + 1), :], "xdma")
        em.wait("vector", "xdma")
        em.wait("scalar", "xdma")

        ln_stats(sb["x_tm"], 0)
        ln_norm(sb["x_tm"], sb["tm_sc"], 0)
        em.wait("tensor", "ev")
        tm_to_fm(sb["tm_sc"], sb["fm_sc"])

        slot_mark = [em.cnt["mm"]] * 8
        bank_mark = [em.cnt["ev"]] * 4
        tmsc_mark = [em.cnt["bnc"]] * MC
        gath_marks = {}
        for pi, (cname, brep, do_elu, bname) in enumerate([
                ("c_wq", sb["bq_rep"], True, "b_fq"),
                ("c_wk", sb["bk_rep"], True, "b_fk"),
                ("c_wv", sb["bv_rep"], False, "b_v")]):
            ld_mark = []
            for k in range(KC):
                g = pi * KC + k
                em.wait("sync", "mm", slot_mark[g % 8])
                dma("sync", sb["wbuf"][:, D * (g % 8):D * (g % 8 + 1)],
                    dr[cname].ap()[128 * k:128 * k + 128, :], "wdma")
                ld_mark.append(em.cnt["wdma"])
            for k in range(KC):
                g = pi * KC + k
                em.wait("tensor", "wdma", ld_mark[k])
                if k == 0:
                    for b in range(4):
                        em.wait("tensor", "ev", bank_mark[b])
                    if pi == 0:
                        em.wait("tensor", "sv")
                for m in range(MC):
                    for h2 in range(2):
                        lhsT = sb["fm_sc"][:, TT * k + 128 * m:
                                           TT * k + 128 * m + 128]
                        top(lambda t, lhsT=lhsT, m=m, h2=h2, g=g, k=k: t.matmul(
                            B[2 * m + h2][:, :], lhsT,
                            sb["wbuf"][:, D * (g % 8) + 512 * h2:
                                       D * (g % 8) + 512 * (h2 + 1)],
                            start=(k == 0), stop=(k == KC - 1)))
            mm_done = em.cnt["mm"]
            for k in range(KC):
                slot_mark[(pi * KC + k) % 8] = mm_done
            for m in range(MC):
                em.wait("vector", "mm", mm_done)
                em.wait("vector", "bnc", tmsc_mark[m])
                for h2 in range(2):
                    psl = B[2 * m + h2][:, :]
                    osl = sb["tm_sc"][:, D * m + 512 * h2:D * m + 512 * (h2 + 1)]
                    rsl = brep[:, 512 * h2:512 * (h2 + 1)]
                    ju2 = sb["ju2"][:, 512 * h2:512 * h2 + 512]
                    ju3 = sb["ju3"][:, 512 * h2:512 * h2 + 512]
                    jnk = sb["junk"][:, 512 * h2:512 * h2 + 512]
                    if do_elu:
                        vopd(lambda e, psl=psl, rsl=rsl, ju2=ju2:
                             e.scalar_tensor_tensor(ju2, psl, 1.0, rsl,
                                                    OP.mult, OP.add), "ev")
                        em.wait("scalar", "ev", em.cnt["ev"])
                        sop(lambda s, ju2=ju2, jnk=jnk: s.activation(
                            jnk, ju2, AF.Exp), "act")
                        sopd(lambda s, ju2=ju2, ju3=ju3: s.activation(
                            ju3, ju2, AF.Relu), "act")
                        em.wait("vector", "act", em.cnt["act"])
                        vopd(lambda e, osl=osl, jnk=jnk, ju3=ju3:
                             e.scalar_tensor_tensor(osl, jnk, 1.0, ju3,
                                                    OP.min, OP.add), "ev")
                    else:
                        vopd(lambda e, psl=psl, rsl=rsl, osl=osl:
                             e.scalar_tensor_tensor(osl, psl, 1.0, rsl,
                                                    OP.mult, OP.add), "ev")
                bank_mark[m] = em.cnt["ev"]
                em.wait("scalar", "ev", em.cnt["ev"])
                if m == 0:
                    em.wait("scalar", "gth", em.cnt["gth"])
                dma("scalar", dr[bname].ap()[128 * m:128 * (m + 1), :],
                    sb["tm_sc"][:, D * m:D * (m + 1)], "bnc")
                tmsc_mark[m] = em.cnt["bnc"]
            em.wait("scalar", "bnc", em.cnt["bnc"])
            em.wait("scalar", "mm", em.cnt["mm"])
            tname = {"b_fq": "fq_g2", "b_fk": "fk_g2", "b_v": "v_g2"}[bname]
            for h in range(H):
                dma("scalar",
                    bass.AP(sb[tname], h * FS, [[16 * FS, 8], [64, G], [1, 64]]),
                    bass.AP(dr[bname], 64 * h, [[D, 8], [8 * D, G], [1, 64]]),
                    "gth")
            gath_marks[tname] = em.cnt["gth"]

        em.wait("tensor", "gth")
        b01_mark = em.cnt["sv"]
        s2e_mark = [em.cnt["sv"]] * 2
        st1_mark = em.cnt["mm"]
        ave_mark = em.cnt["ev"]
        mb4 = bass.AP(sb["maskS"], 0, [[128, 128], [0, 4], [1, 128]])
        for b8 in range(G // 8):
            sA = B[4] if b8 % 2 == 0 else B[6]
            sB = B[5] if b8 % 2 == 0 else B[7]
            pvq = B[0].ap().bitcast(BF16)
            pvk = B[1].ap().bitcast(BF16)
            em.wait("tensor", "sv", b01_mark)
            for jj in range(4):
                j = b8 * 4 + jj
                top(lambda t, j=j, jj=jj, pvq=pvq: t.transpose(
                    pvq[:, 128 * jj:128 * jj + 128],
                    sb["fq_g2"][:, 128 * j:128 * j + 128], sb["identb"][:]), "tr")
                top(lambda t, j=j, jj=jj, pvk=pvk: t.transpose(
                    pvk[:, 128 * jj:128 * jj + 128],
                    sb["fk_g2"][:, 128 * j:128 * j + 128], sb["identb"][:]), "tr")
            trm = em.cnt["tr"]
            em.wait("vector", "tr", trm)
            em.wait("vector", "mm", st1_mark)
            vopd(lambda e, pvq=pvq: e.tensor_copy(sb["fqT"][:, 0:512],
                                                  pvq[:, 0:512]), "sv")
            vopd(lambda e, pvk=pvk: e.tensor_copy(sb["fkT"][:, 0:512],
                                                  pvk[:, 0:512]), "sv")
            b01_mark = em.cnt["sv"]
            em.wait("tensor", "sv", b01_mark)
            em.wait("tensor", "sv", s2e_mark[b8 % 2])
            for jj in range(4):
                qs = sb["fqT"][:, 128 * jj:128 * jj + 128]
                ks = sb["fkT"][:, 128 * jj:128 * jj + 128]
                for dd in range(2):
                    bank = sA if dd == 0 else sB
                    top(lambda t, qs=qs, ks=ks, dd=dd, bank=bank, jj=jj:
                        t.matmul(bank[:, 128 * jj:128 * jj + 128],
                                 ks[64 * dd:64 * dd + 64, :],
                                 qs[64 * dd:64 * dd + 64, :],
                                 start=True, stop=True))
            st1_mark = em.cnt["mm"]
            em.wait("vector", "mm", st1_mark)
            vop(lambda e, sA=sA: e.tensor_tensor(
                sb["S2"][:, 0:512].rearrange("p (g c) -> p g c", g=4),
                sA[:].rearrange("p (g c) -> p g c", g=4), mb4, OP.mult), None)
            vopd(lambda e, sB=sB: e.tensor_tensor(
                sb["S2"][:, 512:1024].rearrange("p (g c) -> p g c", g=4),
                sB[:].rearrange("p (g c) -> p g c", g=4), mb4, OP.mult), "sv")
            s2e_mark[b8 % 2] = em.cnt["sv"]
            em.wait("tensor", "sv", s2e_mark[b8 % 2])
            em.wait("tensor", "ev", ave_mark)
            for gg in range(8):
                jj, dd = gg // 2, gg % 2
                gi = b8 * 8 + 2 * jj + dd
                s2sl = sb["S2"][:, 512 * dd + 128 * jj:512 * dd + 128 * jj + 128]
                top(lambda t, s2sl=s2sl, gi=gi: t.matmul(
                    B[3][:, gi:gi + 1], s2sl, sb["onesb"][:],
                    start=True, stop=True))
                top(lambda t, s2sl=s2sl, gi=gi, gg=gg: t.matmul(
                    B[2][:, 64 * gg:64 * gg + 64], s2sl,
                    sb["v_g2"][:, 64 * gi:64 * gi + 64], start=True, stop=True))
            st2m = em.cnt["mm"]
            em.wait("vector", "mm", st2m)
            rsl = sb["rc_c"][:, 8 * b8:8 * b8 + 8]
            vop(lambda e, b8=b8: e.tensor_scalar(
                sb["sc_c"][:, 0:8], B[3][:, 8 * b8:8 * b8 + 8], 1e-6, None,
                OP.add), None)
            vop(lambda e: e.drain(), None)
            vop(lambda e, rsl=rsl: e.reciprocal(rsl, sb["sc_c"][:, 0:8]), None)
            vop(lambda e: e.drain(), None)
            rbc = bass.AP(sb["rc_c"], 8 * b8, [[G, 128], [1, 8], [0, 64]])
            vopd(lambda e, b8=b8, rbc=rbc: e.tensor_tensor(
                sb["attn_g2"][:, 512 * b8:512 * (b8 + 1)].rearrange(
                    "p (g e) -> p g e", g=8),
                B[2][:].rearrange("p (g e) -> p g e", g=8), rbc, OP.mult), "ev")
            ave_mark = em.cnt["ev"]

        em.wait("scalar", "ev")
        em.wait("scalar", "gth", em.cnt["gth"])
        dma("scalar", dr["b_at"].ap(), sb["attn_g2"][:], "bnc")
        em.wait("scalar", "bnc")
        for gi in range(G):
            m = (8 * gi) // 128
            r0 = (8 * gi) % 128
            out_ap = bass.AP(sb["tm_sc"], r0 * (MC * D) + D * m,
                             [[MC * D, 8], [64, 16], [1, 64]])
            in_ap = bass.AP(dr["b_at"], 64 * gi, [[16 * FS, 8], [FS, 16], [1, 64]])
            dma("scalar", out_ap, in_ap, "gth")
        em.wait("tensor", "gth")
        em.wait("vector", "gth")
        em.wait("scalar", "gth")
        tm_to_fm(sb["tm_sc"], sb["fm_sc"])

        em.wait("sync", "mm")
        for k in range(KC):
            dma("sync", sb["wbuf"][:, D * k:D * (k + 1)],
                dr["c_wo"].ap()[128 * k:128 * k + 128, :], "wdma")
        em.wait("tensor", "wdma")
        em.wait("tensor", "sv")
        for m in range(MC):
            pb = (B[2 * (m % 2)], B[2 * (m % 2) + 1])
            em.wait("tensor", "ev")
            for k in range(KC):
                lhsT = sb["fm_sc"][:, TT * k + 128 * m:TT * k + 128 * m + 128]
                for h2 in range(2):
                    top(lambda t, lhsT=lhsT, k=k, h2=h2, pb=pb: t.matmul(
                        pb[h2][:, :], lhsT,
                        sb["wbuf"][:, D * k + 512 * h2:D * k + 512 * (h2 + 1)],
                        start=(k == 0), stop=(k == KC - 1)))
            em.wait("vector", "mm")
            for h2 in range(2):
                psl = pb[h2][:, :]
                bsl = sb["bo_rep"][:, 512 * h2:512 * (h2 + 1)]
                xsl = sb["x_tm"][:, D * m + 512 * h2:D * m + 512 * (h2 + 1)]
                osl = sb["xmid_tm"][:, D * m + 512 * h2:D * m + 512 * (h2 + 1)]
                vop(lambda e, psl=psl, bsl=bsl: e.scalar_tensor_tensor(
                    sb["ju2"][:, 0:512], psl, 1.0, bsl, OP.mult, OP.add), None)
                vopd(lambda e, osl=osl, xsl=xsl: e.tensor_tensor(
                    osl, sb["ju2"][:, 0:512], xsl, OP.add), "ev")

        ln_stats(sb["xmid_tm"], 4)
        ln_norm(sb["xmid_tm"], sb["tm_sc"], 4)
        em.wait("tensor", "ev")
        tm_to_fm(sb["tm_sc"], sb["fm_sc"])

        em.wait("tensor", "sv")
        gelu_mark = [em.cnt["act"]] * 2
        fcslice_mark = [em.cnt["mm"]] * 2
        pjslice_mark = [em.cnt["mm"]] * 2
        for half in range(2):
            projmm_marks = []
            for c in range(CC):
                wsl = sb["wslc2"][:, D * (c % 2):D * (c % 2) + D]
                em.wait("sync", "mm", fcslice_mark[c % 2])
                dma("sync", wsl,
                    bass.AP(dr["c_fcw"], c * KC * 16384,
                            [[128, 128], [16384, KC], [1, 128]]), "wdma")
                fcw_ld = em.cnt["wdma"]
                em.wait("sync", "wdma", fcw_ld)
                wpsl = sb["wslc"][:, D * (c % 2):D * (c % 2) + D]
                em.wait("sync", "mm", pjslice_mark[c % 2])
                dma("sync", wpsl, dr["c_projw"].ap()[128 * c:128 * c + 128, :],
                    "wdma")
                em.wait("sync", "wdma", em.cnt["wdma"])
                pjw_ld = em.cnt["wdma"]
                fcbank = B[4 + (c % 2)]
                em.wait("tensor", "wdma", fcw_ld)
                em.wait("tensor", "act", max(gelu_mark))
                for k in range(KC):
                    rhs = sb["fm_sc"][:, TT * k + 256 * half:
                                      TT * k + 256 * half + 256]
                    top(lambda t, wsl=wsl, k=k, rhs=rhs, fcbank=fcbank: t.matmul(
                        fcbank[:, 0:256], wsl[:, 128 * k:128 * k + 128], rhs,
                        start=(k == 0), stop=(k == KC - 1)))
                fcmm = em.cnt["mm"]
                fcslice_mark[c % 2] = fcmm
                gsl = sb["gch"][:, 256 * (c % 2):256 * (c % 2) + 256]
                em.wait("scalar", "mm", fcmm)
                if c >= 2:
                    em.wait("scalar", "mm", projmm_marks[c - 2])
                sop(lambda s, gsl=gsl, fcbank=fcbank, c=c: s.activation(
                    gsl, fcbank[:, 0:256], AF.Gelu,
                    bias=sb["fcb_col"][:, c:c + 1], scale=1.0), None)
                sopd(lambda s: s.copy(sb["junk"][0:1, 0:1],
                                      sb["junk"][0:1, 1:2]), "act")
                gelu_mark[c % 2] = em.cnt["act"]
                em.wait("tensor", "wdma", pjw_ld)
                em.wait("tensor", "act", gelu_mark[c % 2])
                for mloc in range(2):
                    for h2 in range(2):
                        top(lambda t, gsl=gsl, mloc=mloc, h2=h2, wpsl=wpsl, c=c:
                            t.matmul(B[2 * mloc + h2][:, :],
                                     gsl[:, 128 * mloc:128 * mloc + 128],
                                     wpsl[:, 512 * h2:512 * (h2 + 1)],
                                     start=(c == 0), stop=(c == CC - 1)))
                projmm_marks.append(em.cnt["mm"])
                pjslice_mark[c % 2] = em.cnt["mm"]
            em.wait("vector", "mm", em.cnt["mm"])
            for mloc in range(2):
                m = 2 * half + mloc
                for h2 in range(2):
                    psl = B[2 * mloc + h2][:, :]
                    bsl = sb["projb_rep"][:, 512 * h2:512 * (h2 + 1)]
                    midsl = sb["xmid_tm"][:, D * m + 512 * h2:D * m + 512 * (h2 + 1)]
                    xsl = sb["x_tm"][:, D * m + 512 * h2:D * m + 512 * (h2 + 1)]
                    vop(lambda e, psl=psl, bsl=bsl: e.scalar_tensor_tensor(
                        sb["ju2"][:, 0:512], psl, 1.0, bsl, OP.mult, OP.add), None)
                    vop(lambda e, midsl=midsl: e.tensor_tensor(
                        sb["ju3"][:, 0:512], sb["ju2"][:, 0:512], midsl, OP.add),
                        None)
                    vopd(lambda e, xsl=xsl: e.tensor_tensor(
                        xsl, sb["ju3"][:, 0:512], xsl, OP.add), "ev")
            em.wait("tensor", "ev", em.cnt["ev"])

        ln_stats(sb["x_tm"], 0)
        ln_norm(sb["x_tm"], x1cur, 0)

        for m in range(MC):
            sl = x1cur[:, D * m:D * (m + 1)]
            vop(lambda e, sl=sl, m=m: e.tensor_reduce(
                sb["qscale"][:, m:m + 1], sl, AX.X, OP.max,
                apply_absolute_value=True), None)
        vop(lambda e: e.drain(), None)
        vop(lambda e: e.tensor_scalar(sb["qscale"][:, 0:MC], sb["qscale"][:, 0:MC],
                                      1e-5, None, OP.max), None)
        vop(lambda e: e.drain(), None)
        vop(lambda e: e.reciprocal(sb["sc_c"][:, 0:MC], sb["qscale"][:, 0:MC]), None)
        vop(lambda e: e.tensor_scalar(sb["qscale"][:, 4:4 + MC],
                                      sb["qscale"][:, 0:MC], 1.0 / 127.0, None,
                                      OP.mult), None)
        vop(lambda e: e.drain(), None)
        vop(lambda e: e.tensor_scalar(sb["qscale"][:, 0:MC], sb["sc_c"][:, 0:MC],
                                      127.0, None, OP.mult), None)
        vop(lambda e: e.drain(), None)
        for m in range(MC):
            ssl = x1cur[:, D * m:D * (m + 1)]
            scol = sb["qscale"][:, m:m + 1]
            dcol = sb["qscale"][:, 4 + m:5 + m]
            osl = sb["tm_sc"][:, D * m:D * (m + 1)]
            vop(lambda e, ssl=ssl, scol=scol: e.tensor_scalar(
                sb["junk"][:], ssl, scol, MAGIC, OP.mult, OP.add), None)
            vopd(lambda e, osl=osl, dcol=dcol: e.tensor_scalar(
                osl, sb["junk"][:], MAGIC, dcol, OP.subtract, OP.mult), "ev")
        em.wait("tensor", "ev")
        tm_to_fm(sb["tm_sc"], wcur, fstride=WIN, foff=1)
        em.wait("vector", "sv")
        for k in range(KC):
            if ti > 0:
                vop(lambda e, k=k, wc=wcur, wp=wprev: e.tensor_copy(
                    wc[:, WIN * k:WIN * k + 1], wp[:, WIN * k + TT:WIN * k + TT + 1]),
                    None)
            (vopd if k == KC - 1 else vop)(
                lambda e, k=k, wc=wcur, wp=wprev: e.tensor_copy(
                    wp[:, WIN * k + TT + 1:WIN * k + TT + 2],
                    wc[:, WIN * k + 1:WIN * k + 2]), "ev")

    def phase_ffn(tj, last=False):
        if "ffn" in barrier:
            full_barrier()
        x1p = sb["x1a"] if tj % 2 == 0 else sb["x1b"]
        wp = sb["xq1w_a"] if tj % 2 == 0 else sb["xq1w_b"]
        HT = TT // 2
        if last:
            for k in range(KC):
                (vopd if k == KC - 1 else vop)(lambda e, k=k, wp=wp: e.memset(
                    wp[:, WIN * k + TT + 1:WIN * k + TT + 2], 0.0), "ev")
            em.wait("tensor", "ev")
        em.wait("tensor", "sv")
        for half in range(2):
            vop(lambda e: e.memset(sb["rmax"][:], 0.0), None)
            vop(lambda e: e.memset(sb["rmin"][:], 0.0), "ev")
            w1slice_mark = [em.cnt["mm"]] * 2
            hbank_mark = [em.cnt["ev"]] * 2
            convslot_mark = [em.cnt["act"]] * 2
            w1mm_marks = []
            for c in range(CC):
                wsl = sb["wslc2"][:, D * (c % 2):D * (c % 2) + D]
                em.wait("sync", "mm", w1slice_mark[c % 2])
                dma("sync", wsl,
                    bass.AP(dr["c_w1t"], c * KC * 16384,
                            [[128, 128], [16384, KC], [1, 128]]), "wdma")
                ldm = em.cnt["wdma"]
                bA = B[4 + (c % 2)]
                em.wait("tensor", "wdma", ldm)
                em.wait("tensor", "ev", hbank_mark[c % 2])
                for k in range(KC):
                    lh = wsl[:, 128 * k:128 * k + 128]
                    top(lambda t, lh=lh, bA=bA, wp=wp, k=k, half=half: t.matmul(
                        bA[:, 0:258], lh,
                        wp[:, WIN * k + HT * half:WIN * k + HT * half + 258],
                        start=(k == 0), stop=(k == KC - 1)))
                w1mm_marks.append(em.cnt["mm"])
                w1slice_mark[c % 2] = em.cnt["mm"]
                hw = sb["hwin"][:, 260 * (c % 2):260 * (c % 2) + 258]
                cva = sb["cv1"][:, 256 * (c % 2):256 * (c % 2) + 256]
                cvb = sb["cv2"][:, 256 * (c % 2):256 * (c % 2) + 256]
                em.wait("vector", "mm", w1mm_marks[-1])
                em.wait("vector", "act", convslot_mark[c % 2])
                vopd(lambda e, bA=bA, hw=hw: e.tensor_copy(hw, bA[:, 0:258]),
                     "ev")
                hbank_mark[c % 2] = em.cnt["ev"]
                w0c = sb["convw_col"][:, c:c + 1]
                w1c = sb["convw_col"][:, CC + c:CC + c + 1]
                w2c = sb["convw_col"][:, 2 * CC + c:2 * CC + c + 1]
                vop(lambda e, w1c=w1c, hw=hw, cva=cva: e.tensor_scalar(
                    cva, hw[:, 1:257], w1c, None, OP.mult), None)
                vop(lambda e, w0c=w0c, hw=hw, cva=cva, cvb=cvb:
                    e.scalar_tensor_tensor(cvb, hw[:, 0:256], w0c, cva,
                                           OP.mult, OP.add), None)
                vopd(lambda e, w2c=w2c, hw=hw, cva=cva, cvb=cvb:
                     e.scalar_tensor_tensor(cva, hw[:, 2:258], w2c, cvb,
                                            OP.mult, OP.add), "ev")
                cvm = em.cnt["ev"]
                em.wait("scalar", "ev", cvm)
                hs = sb["hsil"][:, HT * c:HT * (c + 1)]
                sopd(lambda s, hs=hs, c=c, cva=cva: s.activation(
                    hs, cva, AF.Silu, bias=sb["convb_col"][:, c:c + 1],
                    scale=1.0), "act")
                convslot_mark[c % 2] = em.cnt["act"]
                em.wait("vector", "act", em.cnt["act"])
                vop(lambda e, hs=hs: e.tensor_tensor(sb["rmax"][:], sb["rmax"][:],
                                                     hs, OP.max), None)
                vop(lambda e, hs=hs: e.tensor_tensor(sb["rmin"][:], sb["rmin"][:],
                                                     hs, OP.min), "ev")

            em.wait("tensor", "ev", em.cnt["ev"])
            for mloc in range(2):
                m = 2 * half + mloc
                pvA = B[6].ap().bitcast(BF16)[:, 0:128]
                pvB = B[6].ap().bitcast(BF16)[:, 128:256]
                top(lambda t, mloc=mloc, pvA=pvA: t.transpose(
                    pvA, sb["rmax"][:, 128 * mloc:128 * mloc + 128],
                    sb["identb"][:]), "tr")
                top(lambda t, mloc=mloc, pvB=pvB: t.transpose(
                    pvB, sb["rmin"][:, 128 * mloc:128 * mloc + 128],
                    sb["identb"][:]), "tr")
                em.wait("vector", "tr")
                vop(lambda e, pvA=pvA: e.tensor_reduce(
                    sb["sc_a"][:, 0:1], pvA, AX.X, OP.max,
                    apply_absolute_value=True), None)
                vopd(lambda e, pvB=pvB: e.tensor_reduce(
                    sb["sc_a"][:, 1:2], pvB, AX.X, OP.max,
                    apply_absolute_value=True), "ev")
                vopd(lambda e, m=m: e.tensor_tensor(
                    sb["q2s"][:, m:m + 1], sb["sc_a"][:, 0:1], sb["sc_a"][:, 1:2],
                    OP.max), "sv")
                em.wait("tensor", "sv")
            c0 = 2 * half
            vop(lambda e: e.drain(), None)
            vop(lambda e, c0=c0: e.tensor_scalar(
                sb["q2s"][:, c0:c0 + 2], sb["q2s"][:, c0:c0 + 2], 1e-5, None,
                OP.max), None)
            vop(lambda e: e.drain(), None)
            vop(lambda e, c0=c0: e.reciprocal(sb["sc_c"][:, 0:2],
                                              sb["q2s"][:, c0:c0 + 2]), None)
            vop(lambda e, c0=c0: e.tensor_scalar(
                sb["q2s"][:, 4 + c0:6 + c0], sb["q2s"][:, c0:c0 + 2],
                1.0 / 127.0, None, OP.mult), None)
            vop(lambda e: e.drain(), None)
            vop(lambda e, c0=c0: e.tensor_tensor(
                sb["q2s"][:, 4 + c0:6 + c0], sb["q2s"][:, 4 + c0:6 + c0],
                sb["mbar"][:, 1:2].broadcast_to([128, 2]), OP.mult), None)
            vopd(lambda e, c0=c0: e.tensor_scalar(
                sb["q2s"][:, c0:c0 + 2], sb["sc_c"][:, 0:2], 127.0, None,
                OP.mult), "ev")
            em.wait("tensor", "ev")
            for mloc in range(2):
                m = 2 * half + mloc
                top(lambda t, m=m, mloc=mloc: t.transpose(
                    B[7][0:1, 128 * mloc:128 * mloc + 128], sb["q2s"][:, m:m + 1],
                    sb["identf"][:]), "tr")
            em.wait("vector", "tr")
            vopd(lambda e: e.tensor_copy(sb["s2row"][:], B[7][0:1, 0:HT]), "sv")
            em.wait("tensor", "sv")
            top(lambda t: t.matmul(B[7][:, 0:HT], sb["onesrow"][:], sb["s2row"][:],
                                   start=True, stop=True))
            em.wait("vector", "mm")
            vopd(lambda e: e.tensor_copy(sb["s2bc"][:], B[7][:, 0:HT]), "ev")

            em.wait("tensor", "ev", em.cnt["ev"])
            w2slice_mark = [em.cnt["mm"]] * 2
            hq_mark = [em.cnt["mm"]] * 2
            for c in range(CC):
                hs = sb["hsil"][:, HT * c:HT * (c + 1)]
                hq = sb["hq_c"][:, HT * (c % 2):HT * (c % 2) + HT]
                em.wait("vector", "mm", hq_mark[c % 2])
                vop(lambda e, hs=hs: e.tensor_tensor(
                    sb["ju2"][:, 0:HT], hs, sb["s2bc"][:], OP.mult), None)
                vopd(lambda e, hq=hq: e.tensor_scalar(
                    hq, sb["ju2"][:, 0:HT], MAGIC, MAGIC, OP.add, OP.subtract),
                    "ev")
                rndm = em.cnt["ev"]
                wsl = sb["wslc"][:, D * (c % 2):D * (c % 2) + D]
                em.wait("sync", "mm", w2slice_mark[c % 2])
                dma("sync", wsl, dr["c_w2t"].ap()[128 * c:128 * c + 128, :], "wdma")
                ldm = em.cnt["wdma"]
                em.wait("tensor", "wdma", ldm)
                em.wait("tensor", "ev", rndm)
                for mloc in range(2):
                    for h2 in range(2):
                        top(lambda t, hq=hq, mloc=mloc, h2=h2, wsl=wsl, c=c:
                            t.matmul(B[2 * mloc + h2][:, :],
                                     hq[:, 128 * mloc:128 * mloc + 128],
                                     wsl[:, 512 * h2:512 * (h2 + 1)],
                                     start=(c == 0), stop=(c == CC - 1)))
                hq_mark[c % 2] = em.cnt["mm"]
                w2slice_mark[c % 2] = em.cnt["mm"]
            em.wait("vector", "mm")
            for mloc in range(2):
                m = 2 * half + mloc
                for h2 in range(2):
                    psl = B[2 * mloc + h2][:, :]
                    x1sl = x1p[:, D * m + 512 * h2:D * m + 512 * (h2 + 1)]
                    osl = sb["xmid_tm"][:, D * m + 512 * h2:D * m + 512 * (h2 + 1)]
                    dcol = sb["q2s"][:, 4 + m:5 + m]
                    vopd(lambda e, psl=psl, x1sl=x1sl, osl=osl, dcol=dcol:
                         e.scalar_tensor_tensor(osl, psl, dcol, x1sl,
                                                OP.mult, OP.add), "ev")
            em.wait("tensor", "ev")

        ln_stats(sb["xmid_tm"], 8)
        ln_norm(sb["xmid_tm"], sb["x_tm"], 8)
        em.wait("sync", "ev")
        for m in range(MC):
            dma("sync", y_d[tj * TT + 128 * m:tj * TT + 128 * (m + 1), :],
                sb["x_tm"][:, D * m:D * (m + 1)], "ydma")

    with nc.Block() as block:
        for ti in range(nt_limit):
            phase_attn(ti)
            if ti > 0 and do_ffn:
                phase_ffn(ti - 1)
        if do_ffn:
            phase_ffn(nt_limit - 1, last=True)
        if dump:
            for eng in ("sync", "tensor", "vector", "scalar", "gpsimd"):
                for s in ["wdma", "xdma", "ydma", "bnc", "gth", "mm", "tr",
                          "ev", "sv", "gp", "act"]:
                    em.wait(eng, s)
            for nm in dump:
                t = sb[nm]
                shp = list(t.shape)
                dbg = nc.dram_tensor("dbg_" + nm, shp, t.dtype,
                                     kind="ExternalOutput")
                dma("sync", dbg.ap(), t[:], "ydma")
        em.wait("sync", "ydma")
        em.wait("gpsimd", "ydma")

        @block.sync
        def _(e):
            for fn in em.lists["sync"]:
                fn(e)

        @block.tensor
        def _(e):
            for fn in em.lists["tensor"]:
                fn(e)

        @block.vector
        def _(e):
            for fn in em.lists["vector"]:
                fn(e)

        @block.scalar
        def _(e):
            for fn in em.lists["scalar"]:
                fn(e)

        @block.gpsimd
        def _(e):
            for fn in em.lists["gpsimd"]:
                fn(e)

    es.close()
    return nc


_NC_CACHE = None


def _get_nc():
    global _NC_CACHE
    if _NC_CACHE is None:
        _NC_CACHE = build_program()
    return _NC_CACHE


def kernel(**inputs):
    ii = {k: np.asarray(v) for k, v in inputs.items()}
    x = np.ascontiguousarray(ii["L0_attn_q_x"].astype(np.float32, copy=False))
    shared = {
        "wq": ii["L0_attn_wq"], "wk": ii["L0_attn_wk"], "wv": ii["L0_attn_wv"],
        "wo": ii["L0_attn_wo"], "fcw": ii["L0_attn_fc_w"],
        "projw": ii["L0_attn_proj_w"],
        "w1T": np.ascontiguousarray(ii["L0_ffn_w1"].T),
        "w2T": np.ascontiguousarray(ii["L0_ffn_w2"].T),
        "ln1w": ii["L0_attn_ln1_w"], "ln1b": ii["L0_attn_ln1_b"],
        "ln2w": ii["L0_attn_ln2_w"], "ln2b": ii["L0_attn_ln2_b"],
        "bq": ii["L0_attn_bq"], "bk": ii["L0_attn_bk"], "bv": ii["L0_attn_bv"],
        "bo": ii["L0_attn_bo"], "fcb": ii["L0_attn_fc_b"],
        "projb": ii["L0_attn_proj_b"],
        "convw0": np.ascontiguousarray(ii["L0_ffn_conv_w"][:, 0, 0]),
        "convw1": np.ascontiguousarray(ii["L0_ffn_conv_w"][:, 0, 1]),
        "convw2": np.ascontiguousarray(ii["L0_ffn_conv_w"][:, 0, 2]),
        "identb": np.eye(128, dtype=ml_dtypes.bfloat16),
        "identf": np.eye(128, dtype=np.float32),
        "maskS": np.kron(np.eye(8, dtype=np.float32), np.ones((16, 16), np.float32)),
        "convb": ii["L0_ffn_conv_b"],
    }
    shared = {k: (np.ascontiguousarray(v.astype(np.float32, copy=False))
                  if v.dtype != ml_dtypes.bfloat16 else v)
              for k, v in shared.items()}
    nc = _get_nc()
    in_maps = [dict(shared, x=np.ascontiguousarray(x[c])) for c in range(NCORES)]
    res = run_bass_kernel_spmd(nc, in_maps, core_ids=list(range(NCORES)))
    out = np.stack([res.results[c]["y"] for c in range(NCORES)], axis=0)
    return out.astype(np.float32, copy=False)


if __name__ == "__main__":
    nc = build_program()
    print("built")


# version 15
# speedup vs baseline: 1.5305x; 1.0091x over previous
SERIAL_FC = False
"""Fused megakernel for mk_seq_postln_d1 (linear-attention block + BitNet FFN).

One raw-Bass program per NeuronCore, launched exactly once via
run_bass_kernel_spmd.  Batch (8) sharded across the 8 cores; each core runs
the entire forward for its own 4096-token sequence.

Structure:
  PROLOGUE  - build per-core bf16 weight caches in HBM (LN gains folded into
              weights, biases pre-folded via on-device matmuls, ternary
              quantization of w1/w2 done in fp32), load constants.
  MAIN LOOP - 8 token tiles of 512.  Per tile: LN1 -> q/k/v (token-major) ->
              elu+1 -> DRAM bounce -> grouped gathers -> 8-token-group
              block-diagonal attention matmuls -> wo -> LN2 -> fc/gelu/proj
              -> post-LN -> act-quant -> transposed window.  The BitNet FFN
              for tile i runs during iteration i+1 (the depthwise conv needs
              one token of lookahead).

Layouts: TM = [128 tokens x feat] (chunk-major), FM = [128 feat x tokens],
g2 = [128 = 16n+h x 64d] 8-token attention groups.
"""
import numpy as np
import ml_dtypes
import concourse.bass as bass
import concourse.mybir as mybir
from concourse.bass_utils import run_bass_kernel_spmd
from contextlib import ExitStack

F32 = mybir.dt.float32
BF16 = mybir.dt.bfloat16
I32 = mybir.dt.int32
OP = mybir.AluOpType
AF = mybir.ActivationFunctionType
AX = mybir.AxisListType

D = 1024
H = 16
MLP = 4096
N = 4096
TT = 512
NT = N // TT
MC = TT // 128
KC = D // 128
CC = MLP // 128
G = TT // 8
WIN = TT + 2
MAGIC = float(1.5 * 2 ** 23)
RSQRT_MAGIC = 0x5f3759df
NCORES = 8


class Emitter:
    def __init__(self):
        self.lists = {e: [] for e in ("sync", "tensor", "vector", "scalar", "gpsimd")}
        self.sems = {}
        self.cnt = {}

    def op(self, eng, fn, inc=None, n=1):
        if inc is not None:
            self.cnt[inc] += n
            sem = self.sems[inc]
            self.lists[eng].append(lambda e, fn=fn, sem=sem, n=n: fn(e).then_inc(sem, n))
        else:
            self.lists[eng].append(lambda e, fn=fn: fn(e))

    def wait(self, eng, name, val=None):
        v = self.cnt[name] if val is None else val
        sem = self.sems[name]
        self.lists[eng].append(lambda e, sem=sem, v=v: e.wait_ge(sem, v))


def build_program(nt_limit=NT, do_ffn=True, dump=(), skip=(), serial_fc=False,
                  barrier=()):
    global SERIAL_FC
    SERIAL_FC = int(serial_fc)
    nc = bass.Bass("TRN2", target_bir_lowering=False)
    es = ExitStack()
    E = es.enter_context

    dr = {}

    def dt(name, shape, dtype, kind="Internal"):
        dr[name] = nc.dram_tensor(name, shape, dtype, kind=kind)
        return dr[name]

    dt("x", [N, D], F32, "ExternalInput")
    for nm, shp in [("wq", [D, D]), ("wk", [D, D]), ("wv", [D, D]), ("wo", [D, D]),
                    ("fcw", [D, MLP]), ("projw", [MLP, D]),
                    ("w1T", [D, MLP]), ("w2T", [MLP, D]),
                    ("ln1w", [D]), ("ln1b", [D]), ("ln2w", [D]), ("ln2b", [D]),
                    ("bq", [D]), ("bk", [D]), ("bv", [D]), ("bo", [D]),
                    ("fcb", [MLP]), ("projb", [D]),
                    ("convw0", [MLP]), ("convw1", [MLP]), ("convw2", [MLP]),
                    ("convb", [MLP])]:
        dt(nm, shp, F32, "ExternalInput")
    dt("identb", [128, 128], BF16, "ExternalInput")
    dt("identf", [128, 128], F32, "ExternalInput")
    dt("maskS", [128, 128], F32, "ExternalInput")
    dt("y", [N, D], F32, "ExternalOutput")

    dt("c_wq", [D, D], BF16)
    dt("c_wk", [D, D], BF16)
    dt("c_wv", [D, D], BF16)
    dt("c_wo", [D, D], BF16)
    dt("c_fcw", [KC * CC * 128 * 128], BF16)
    dt("c_projw", [MLP, D], BF16)
    dt("c_w1t", [KC * CC * 128 * 128], BF16)
    dt("c_w2t", [MLP, D], BF16)
    dt("b_fq", [TT, D], BF16)
    dt("b_fk", [TT, D], BF16)
    dt("b_v", [TT, D], BF16)
    dt("b_at", [128, 64 * G], BF16)
    dt("b_row", [D], F32)

    sb = {}

    def st(name, shape, dtype):
        sb[name] = E(nc.sbuf_tensor("s_" + name, shape, dtype))
        return sb[name]

    st("identb", [128, 128], BF16)
    st("identf", [128, 128], F32)
    st("maskS", [128, 128], F32)
    st("onesb", [128, 1], BF16)
    st("onesf", [128, 1], F32)
    st("onesrow", [1, 128], F32)
    st("magici", [128, 16], I32)
    st("lnw_cols", [128, 2 * KC], F32)
    st("ln1b_col", [128, KC], BF16)
    st("ln2b_col", [128, KC], BF16)
    st("bias_sc", [128, KC], F32)
    st("fcb_col", [128, CC], F32)
    st("convw_col", [128, 3 * CC], F32)
    st("convb_col", [128, CC], F32)
    st("mbar", [128, 4], F32)
    st("bq_rep", [128, D], BF16)
    st("bk_rep", [128, D], BF16)
    st("bv_rep", [128, D], BF16)
    st("bo_rep", [128, D], BF16)
    st("projb_rep", [128, D], BF16)
    st("wbuf", [128, KC * D], BF16)
    st("wslc", [128, 2 * D], BF16)
    st("wslc2", [128, 2 * D], BF16)
    st("x_tm", [128, MC * D], F32)
    st("xmid_tm", [128, MC * D], F32)
    st("x1a", [128, MC * D], F32)
    st("x1b", [128, MC * D], F32)
    st("tm_sc", [128, MC * D], BF16)
    st("fm_sc", [128, KC * TT], BF16)
    st("stats", [128, 16], F32)
    st("stats2", [128, 16], F32)
    st("sc_a", [128, 16], F32)
    st("sc_b", [128, 16], F32)
    st("sc_c", [128, 16], F32)
    st("qscale", [128, 8], F32)
    st("q2s", [128, 8], F32)
    st("dbgk", [128, 256], F32)
    st("junk", [128, D], F32)
    st("ju2", [128, 1024], F32)
    st("ju3", [128, 1024], F32)

    B = {}
    for i in range(8):
        B[i] = E(nc.psum_tensor(f"bank{i}", [128, 512], F32))

    em = Emitter()
    for s in ["ldc", "wdma", "xdma", "ydma", "bnc", "gth", "mm", "tr", "ev",
              "sv", "gp", "act"]:
        em.sems[s] = E(nc.semaphore(s))
        em.cnt[s] = 0

    def vop(fn, inc="ev"):
        em.op("vector", fn, inc)

    def vopd(fn, inc="ev"):
        em.op("vector", fn, None)
        em.op("vector", lambda e: e.drain(), inc)

    def sop(fn, inc="act"):
        em.op("scalar", fn, inc)

    def sopd(fn, inc="act"):
        em.op("scalar", fn, None)
        em.op("scalar", lambda e: e.drain(), inc)

    def gop(fn, inc="gp"):
        em.op("gpsimd", fn, inc)

    def top(fn, inc="mm"):
        em.op("tensor", fn, inc)

    def dma(eng, out_ap, in_ap, inc):
        em.op(eng, lambda e, o=out_ap, i=in_ap: e.dma_start(out=o, in_=i), inc, 16)

    def dmas(eng, out_ap, in_ap, inc):
        em.op(eng, lambda e, o=out_ap, i=in_ap: e.dma_start(
            out=o, in_=i, allow_slow_non_contiguous=True), inc, 16)

    pro_es = ExitStack()
    sb["wf32b"] = pro_es.enter_context(nc.sbuf_tensor("s_wf32b", [128, MLP], F32))
    sb["wbf"] = pro_es.enter_context(nc.sbuf_tensor("s_wbf", [128, 2 * MLP], BF16))

    for nm in ["identb", "identf", "maskS"]:
        dma("sync", sb[nm][:], dr[nm].ap(), "ldc")
    for j, nm in enumerate(["ln1w", "ln2w"]):
        dmas("sync", bass.AP(sb["lnw_cols"], j * KC, [[2 * KC, 128], [1, KC]]),
            bass.AP(dr[nm], 0, [[1, 128], [128, KC]]), "ldc")
    dmas("sync", bass.AP(sb["convb_col"], 0, [[CC, 128], [1, CC]]),
        bass.AP(dr["convb"], 0, [[1, 128], [128, CC]]), "ldc")
    dmas("gpsimd", sb["bo_rep"][:], bass.AP(dr["bo"], 0, [[0, 128], [1, D]]), "ldc")
    dmas("gpsimd", sb["projb_rep"][:], bass.AP(dr["projb"], 0, [[0, 128], [1, D]]), "ldc")
    dmas("gpsimd", sb["ln1b_col"][:], bass.AP(dr["ln1b"], 0, [[1, 128], [128, KC]]), "ldc")
    dmas("gpsimd", sb["ln2b_col"][:], bass.AP(dr["ln2b"], 0, [[1, 128], [128, KC]]), "ldc")
    em.wait("vector", "ldc")
    vop(lambda e: e.memset(sb["onesb"][:], 1.0), None)
    vop(lambda e: e.memset(sb["onesf"][:], 1.0), None)
    vop(lambda e: e.memset(sb["magici"][:], RSQRT_MAGIC), None)
    vop(lambda e: e.memset(sb["onesrow"][:], 1.0))

    def build_attn_cache(wname, cname, borig, brep):
        dmas("sync", bass.AP(sb["bias_sc"], 0, [[KC, 128], [1, KC]]),
             bass.AP(dr[borig], 0, [[1, 128], [128, KC]]), "wdma")
        em.wait("vector", "wdma")
        for k in range(KC):
            dma("sync", sb["wf32b"][:, 0:D], dr[wname].ap()[128 * k:128 * k + 128, :],
                "wdma")
            em.wait("vector", "wdma")
            lcol = sb["lnw_cols"][:, k:k + 1]
            vop(lambda e, lcol=lcol: e.tensor_scalar(
                sb["wbf"][:, 0:D], sb["wf32b"][:, 0:D], lcol, None, OP.mult), None)
            vopd(lambda e: e.tensor_copy(sb["wbf"][:, D:2 * D], sb["wf32b"][:, 0:D]))
            em.wait("sync", "ev")
            dma("sync", dr[cname].ap()[128 * k:128 * k + 128, :], sb["wbf"][:, 0:D],
                "wdma")
            em.wait("tensor", "ev")
            for c in range(KC):
                top(lambda t, k=k, c=c: t.matmul(
                    B[0][:, c:c + 1], sb["wbf"][:, D + 128 * c:D + 128 * c + 128],
                    sb["ln1b_col"][:, k:k + 1], start=True, stop=True))
            em.wait("vector", "mm")
            vopd(lambda e: e.tensor_tensor(sb["bias_sc"][:], B[0][:, 0:KC],
                                           sb["bias_sc"][:], OP.add))
        em.wait("sync", "ev")
        dmas("sync", bass.AP(dr["b_row"], 0, [[1, 128], [128, KC]]),
            bass.AP(sb["bias_sc"], 0, [[KC, 128], [1, KC]]), "wdma")
        em.wait("gpsimd", "wdma")
        dma("gpsimd", brep[:], bass.AP(dr["b_row"], 0, [[0, 128], [1, D]]), "wdma")

    if "attn" not in skip:
        build_attn_cache("wq", "c_wq", "bq", sb["bq_rep"])
        build_attn_cache("wk", "c_wk", "bk", sb["bk_rep"])
        build_attn_cache("wv", "c_wv", "bv", sb["bv_rep"])

    dmas("sync", bass.AP(sb["fcb_col"], 0, [[CC, 128], [1, CC]]),
         bass.AP(dr["fcb"], 0, [[1, 128], [128, CC]]), "wdma")
    em.wait("vector", "wdma")
    for k in range(KC):
        dma("sync", sb["wf32b"][:], dr["fcw"].ap()[128 * k:128 * k + 128, :], "wdma")
        em.wait("vector", "wdma")
        lcol = sb["lnw_cols"][:, KC + k:KC + k + 1]
        vop(lambda e, lcol=lcol: e.tensor_scalar(
            sb["wbf"][:, 0:MLP], sb["wf32b"][:], lcol, None, OP.mult), None)
        vopd(lambda e: e.tensor_copy(sb["wbf"][:, MLP:2 * MLP], sb["wf32b"][:]))
        em.wait("sync", "ev")
        dma("sync", bass.AP(dr["c_fcw"], k * 16384,
                            [[128, 128], [KC * 16384, CC], [1, 128]]),
            bass.AP(sb["wbf"], 0, [[2 * MLP, 128], [128, CC], [1, 128]]), "wdma")
        em.wait("tensor", "ev")
        for c in range(CC):
            top(lambda t, k=k, c=c: t.matmul(
                B[1][:, c:c + 1], sb["wbf"][:, MLP + 128 * c:MLP + 128 * c + 128],
                sb["ln2b_col"][:, k:k + 1], start=True, stop=True))
        em.wait("vector", "mm")
        vopd(lambda e: e.tensor_tensor(sb["fcb_col"][:], B[1][:, 0:CC],
                                       sb["fcb_col"][:], OP.add))

    if "casts" not in skip:
        for src_, dst_ in [("wo", "c_wo"), ("projw", "c_projw")]:
            nr = dr[src_].shape[0]
            for k in range(0, nr, 128):
                em.wait("sync", "ev")
                dma("sync", sb["wf32b"][:, 0:D],
                    dr[src_].ap()[k:k + 128, :], "wdma")
                em.wait("vector", "wdma")
                vopd(lambda e: e.tensor_copy(sb["wbf"][:, 0:D],
                                             sb["wf32b"][:, 0:D]))
                em.wait("sync", "ev")
                dma("sync", dr[dst_].ap()[k:k + 128, :], sb["wbf"][:, 0:D],
                    "wdma")

    def build_ternary(src, dst, mi, blocked):
        nchunk = dr[src].shape[0] // 128
        width = dr[src].shape[1]
        vop(lambda e: e.memset(sb["sc_a"][:, 0:1], 0.0), None)
        for k in range(nchunk):
            em.wait("sync", "ev")
            dma("sync", sb["wf32b"][:, 0:width],
                dr[src].ap()[128 * k:128 * k + 128, :], "wdma")
            em.wait("vector", "wdma")
            vop(lambda e, w=width: e.tensor_reduce(
                sb["sc_b"][:, 0:1], sb["wf32b"][:, 0:w], AX.X, OP.add,
                apply_absolute_value=True), None)
            vop(lambda e: e.drain(), None)
            vop(lambda e: e.tensor_tensor(sb["sc_a"][:, 0:1], sb["sc_a"][:, 0:1],
                                          sb["sc_b"][:, 0:1], OP.add), "ev")
        em.wait("tensor", "ev")
        top(lambda t: t.matmul(B[2][0:1, 0:1], sb["onesf"][:], sb["sc_a"][:, 0:1],
                               start=True, stop=True))
        em.wait("vector", "mm")
        tot = float(nchunk * 128 * width)
        vopd(lambda e, tot=tot: e.tensor_scalar(
            sb["sc_b"][0:1, 0:1], B[2][0:1, 0:1], 1.0 / tot, 1e-5,
            OP.mult, OP.max), "ev")
        vop(lambda e: e.drain(), None)
        vopd(lambda e: e.reciprocal(sb["sc_b"][0:1, 1:2], sb["sc_b"][0:1, 0:1]))
        em.wait("sync", "ev")
        dma("sync", bass.AP(dr["b_row"], 0, [[1, 1], [1, 2]]), sb["sc_b"][0:1, 0:2],
            "wdma")
        em.wait("gpsimd", "wdma")
        dma("gpsimd", sb["mbar"][:, mi:mi + 1],
            bass.AP(dr["b_row"], 0, [[0, 128], [1, 1]]), "wdma")
        dma("gpsimd", sb["mbar"][:, mi + 2:mi + 3],
            bass.AP(dr["b_row"], 1, [[0, 128], [1, 1]]), "wdma")
        em.wait("vector", "wdma")
        for k in range(nchunk):
            dma("sync", sb["wf32b"][:, 0:width],
                dr[src].ap()[128 * k:128 * k + 128, :], "wdma")
            em.wait("vector", "wdma")
            scol = sb["mbar"][:, mi + 2:mi + 3]
            vop(lambda e, w=width, scol=scol: e.tensor_scalar(
                sb["wf32b"][:, 0:w], sb["wf32b"][:, 0:w], scol, 1.0,
                OP.mult, OP.min), None)
            vop(lambda e, w=width: e.tensor_scalar(
                sb["wf32b"][:, 0:w], sb["wf32b"][:, 0:w], -1.0, MAGIC,
                OP.max, OP.add), None)
            vopd(lambda e, w=width: e.tensor_scalar(
                sb["wbf"][:, 0:w], sb["wf32b"][:, 0:w], MAGIC, None, OP.subtract))
            em.wait("sync", "ev")
            if blocked:
                ncb = width // 128
                dma("sync", bass.AP(dr[dst], k * 16384,
                                    [[128, 128], [KC * 16384, ncb], [1, 128]]),
                    bass.AP(sb["wbf"], 0, [[2 * MLP, 128], [128, ncb], [1, 128]]),
                    "wdma")
            else:
                dma("sync", dr[dst].ap()[128 * k:128 * k + 128, :],
                    sb["wbf"][:, 0:width], "wdma")
            em.wait("vector", "wdma")

    if "ternary" not in skip:
        build_ternary("w1T", "c_w1t", 0, True)
        build_ternary("w2T", "c_w2t", 1, False)

    for j, nm in enumerate(["convw0", "convw1", "convw2"]):
        dmas("sync", bass.AP(sb["convw_col"], j * CC, [[3 * CC, 128], [1, CC]]),
            bass.AP(dr[nm], 0, [[1, 128], [128, CC]]), "wdma")
    em.wait("vector", "wdma")
    vop(lambda e: e.tensor_scalar(sb["convw_col"][:], sb["convw_col"][:],
                                  sb["mbar"][:, 0:1], None, OP.mult))

    for eng in ("sync", "tensor", "vector", "scalar", "gpsimd"):
        for s in ["wdma", "ev", "mm", "gp", "act", "ldc"]:
            em.wait(eng, s)

    pro_es.close()
    HT = TT // 2
    st("fq_g2", [128, 64 * G], BF16)
    st("fk_g2", [128, 64 * G], BF16)
    st("v_g2", [128, 64 * G], BF16)
    st("fqT", [128, 4 * 128], BF16)
    st("fkT", [128, 4 * 128], BF16)
    st("S2", [128, 8 * 128], BF16)
    st("rc_c", [128, G], F32)
    st("attn_g2", [128, 64 * G], BF16)
    st("xq1w_a", [128, KC * WIN], BF16)
    st("xq1w_b", [128, KC * WIN], BF16)
    st("hsil", [128, CC * HT], BF16)
    st("hwin", [128, 2 * 260], BF16)
    st("cv1", [128, 2 * HT], BF16)
    st("cv2", [128, 2 * HT], BF16)
    st("rmax", [128, HT], BF16)
    st("rmin", [128, HT], BF16)
    st("hq_c", [128, 2 * HT], BF16)
    st("gch", [128, 2 * 256], BF16)
    st("s2bc", [128, HT], F32)
    st("s2row", [1, HT], F32)
    vop(lambda e: e.memset(sb["xq1w_a"][:], 0.0), None)
    vop(lambda e: e.memset(sb["xq1w_b"][:], 0.0), "ev")
    for eng in ("sync", "tensor", "scalar", "gpsimd"):
        em.wait(eng, "ev")

    def ln_finish(off):
        ssl = sb["stats"][:, off:off + MC]
        qsl = sb["stats2"][:, off:off + MC]
        a = sb["sc_a"][:, 0:MC]
        b = sb["sc_b"][:, 0:MC]
        vop(lambda e: e.drain(), None)
        vop(lambda e: e.tensor_scalar(ssl, ssl, 1.0 / D, None, OP.mult), None)
        vop(lambda e: e.drain(), None)
        vop(lambda e: e.tensor_tensor(a, ssl, ssl, OP.mult), None)
        vop(lambda e: e.tensor_scalar(qsl, qsl, 1.0 / D, None, OP.mult), None)
        vop(lambda e: e.drain(), None)
        vop(lambda e: e.tensor_tensor(qsl, qsl, a, OP.subtract), None)
        vop(lambda e: e.drain(), None)
        vop(lambda e: e.tensor_scalar(qsl, qsl, 1e-5, None, OP.add), "ev")
        em.wait("scalar", "ev")
        sop(lambda s: s.activation(a, qsl, AF.Ln), "act")
        sop(lambda s: s.activation(qsl, a, AF.Exp, scale=-0.5), "act")
        em.wait("vector", "act")

    def ln_stats(tile, off):
        em.wait("scalar", "ev")
        for m in range(MC):
            sl = tile[:, D * m:D * (m + 1)]
            vop(lambda e, sl=sl, m=m: e.tensor_reduce(
                sb["stats"][:, off + m:off + m + 1], sl, AX.X, OP.add), None)
            sop(lambda s, sl=sl, m=m: s.activation(
                sb["junk"][:], sl, AF.Square,
                accum_out=sb["stats2"][:, off + m:off + m + 1]), "act")
        em.wait("vector", "act")
        ln_finish(off)

    def ln_norm(src, dst, off):
        for m in range(MC):
            ssl = src[:, D * m:D * (m + 1)]
            dsl = dst[:, D * m:D * (m + 1)]
            mcol = sb["stats"][:, off + m:off + m + 1]
            rcol = sb["stats2"][:, off + m:off + m + 1]
            last = (m == MC - 1)
            (vopd if last else vop)(
                lambda e, ssl=ssl, dsl=dsl, mcol=mcol, rcol=rcol: e.tensor_scalar(
                    dsl, ssl, mcol, rcol, OP.subtract, OP.mult))

    def tm_to_fm(tm_tile, fm_tile, fstride=TT, foff=0):
        em.wait("tensor", "ev", em.cnt["ev"])
        fsz = fm_tile.shape[1]
        ngrp = MC * KC // 4
        evmark = [em.cnt["sv"]] * ngrp
        for grp in range(ngrp):
            m, k0 = grp // 2, (grp % 2) * 4
            bank = B[4 + (grp % 4)]
            em.wait("tensor", "sv", evmark[grp - 2] if grp >= 2 else evmark[0])
            for q in range(4):
                k = k0 + q
                pv = bank.ap().bitcast(BF16)[:, 128 * q:128 * q + 128]
                top(lambda t, m=m, k=k, pv=pv, tm=tm_tile: t.transpose(
                    pv, tm[:, D * m + 128 * k:D * m + 128 * k + 128],
                    sb["identb"][:]), "tr")
            trmark = em.cnt["tr"]
            pvfull = bank.ap().bitcast(BF16)[:, 0:512]
            dsl = bass.AP(fm_tile, foff + fstride * k0 + 128 * m,
                          [[fsz, 128], [fstride, 4], [1, 128]])
            em.wait("vector", "tr", trmark)
            vopd(lambda e, dsl=dsl, pvfull=pvfull: e.tensor_copy(dsl, pvfull),
                 "sv")
            evmark[grp] = em.cnt["sv"]

    x_d = dr["x"].ap()
    y_d = dr["y"].ap()
    FS = 64 * G

    def full_barrier():
        for eng in ("sync", "tensor", "vector", "scalar", "gpsimd"):
            for s in ["wdma", "xdma", "ydma", "bnc", "gth", "mm", "tr",
                      "ev", "sv", "gp", "act"]:
                em.wait(eng, s)

    def phase_attn(ti):
        if "attn" in barrier:
            full_barrier()
        x1cur = sb["x1a"] if ti % 2 == 0 else sb["x1b"]
        wcur = sb["xq1w_a"] if ti % 2 == 0 else sb["xq1w_b"]
        wprev = sb["xq1w_b"] if ti % 2 == 0 else sb["xq1w_a"]

        em.wait("sync", "ydma")
        em.wait("sync", "ev")
        for m in range(MC):
            dma("sync", sb["x_tm"][:, D * m:D * (m + 1)],
                x_d[ti * TT + 128 * m:ti * TT + 128 * (m + 1), :], "xdma")
        em.wait("vector", "xdma")
        em.wait("scalar", "xdma")

        ln_stats(sb["x_tm"], 0)
        ln_norm(sb["x_tm"], sb["tm_sc"], 0)
        em.wait("tensor", "ev")
        tm_to_fm(sb["tm_sc"], sb["fm_sc"])

        slot_mark = [em.cnt["mm"]] * 8
        bank_mark = [em.cnt["ev"]] * 4
        tmsc_mark = [em.cnt["bnc"]] * MC
        gath_marks = {}
        for pi, (cname, brep, do_elu, bname) in enumerate([
                ("c_wq", sb["bq_rep"], True, "b_fq"),
                ("c_wk", sb["bk_rep"], True, "b_fk"),
                ("c_wv", sb["bv_rep"], False, "b_v")]):
            ld_mark = []
            for k in range(KC):
                g = pi * KC + k
                em.wait("sync", "mm", slot_mark[g % 8])
                dma("sync", sb["wbuf"][:, D * (g % 8):D * (g % 8 + 1)],
                    dr[cname].ap()[128 * k:128 * k + 128, :], "wdma")
                ld_mark.append(em.cnt["wdma"])
            for k in range(KC):
                g = pi * KC + k
                em.wait("tensor", "wdma", ld_mark[k])
                if k == 0:
                    for b in range(4):
                        em.wait("tensor", "ev", bank_mark[b])
                    if pi == 0:
                        em.wait("tensor", "sv")
                for m in range(MC):
                    for h2 in range(2):
                        lhsT = sb["fm_sc"][:, TT * k + 128 * m:
                                           TT * k + 128 * m + 128]
                        top(lambda t, lhsT=lhsT, m=m, h2=h2, g=g, k=k: t.matmul(
                            B[2 * m + h2][:, :], lhsT,
                            sb["wbuf"][:, D * (g % 8) + 512 * h2:
                                       D * (g % 8) + 512 * (h2 + 1)],
                            start=(k == 0), stop=(k == KC - 1)))
            mm_done = em.cnt["mm"]
            for k in range(KC):
                slot_mark[(pi * KC + k) % 8] = mm_done
            for m in range(MC):
                em.wait("vector", "mm", mm_done)
                em.wait("vector", "bnc", tmsc_mark[m])
                for h2 in range(2):
                    psl = B[2 * m + h2][:, :]
                    osl = sb["tm_sc"][:, D * m + 512 * h2:D * m + 512 * (h2 + 1)]
                    rsl = brep[:, 512 * h2:512 * (h2 + 1)]
                    ju2 = sb["ju2"][:, 512 * h2:512 * h2 + 512]
                    ju3 = sb["ju3"][:, 512 * h2:512 * h2 + 512]
                    jnk = sb["junk"][:, 512 * h2:512 * h2 + 512]
                    if do_elu:
                        vopd(lambda e, psl=psl, rsl=rsl, ju2=ju2:
                             e.scalar_tensor_tensor(ju2, psl, 1.0, rsl,
                                                    OP.mult, OP.add), "ev")
                        em.wait("scalar", "ev", em.cnt["ev"])
                        sop(lambda s, ju2=ju2, jnk=jnk: s.activation(
                            jnk, ju2, AF.Exp), "act")
                        sopd(lambda s, ju2=ju2, ju3=ju3: s.activation(
                            ju3, ju2, AF.Relu), "act")
                        em.wait("vector", "act", em.cnt["act"])
                        vopd(lambda e, osl=osl, jnk=jnk, ju3=ju3:
                             e.scalar_tensor_tensor(osl, jnk, 1.0, ju3,
                                                    OP.min, OP.add), "ev")
                    else:
                        vopd(lambda e, psl=psl, rsl=rsl, osl=osl:
                             e.scalar_tensor_tensor(osl, psl, 1.0, rsl,
                                                    OP.mult, OP.add), "ev")
                bank_mark[m] = em.cnt["ev"]
                em.wait("scalar", "ev", em.cnt["ev"])
                if m == 0:
                    em.wait("scalar", "gth", em.cnt["gth"])
                dma("scalar", dr[bname].ap()[128 * m:128 * (m + 1), :],
                    sb["tm_sc"][:, D * m:D * (m + 1)], "bnc")
                tmsc_mark[m] = em.cnt["bnc"]
            em.wait("scalar", "bnc", em.cnt["bnc"])
            em.wait("scalar", "mm", em.cnt["mm"])
            tname = {"b_fq": "fq_g2", "b_fk": "fk_g2", "b_v": "v_g2"}[bname]
            for h in range(H):
                dma("scalar",
                    bass.AP(sb[tname], h * FS, [[16 * FS, 8], [64, G], [1, 64]]),
                    bass.AP(dr[bname], 64 * h, [[D, 8], [8 * D, G], [1, 64]]),
                    "gth")
            gath_marks[tname] = em.cnt["gth"]

        em.wait("tensor", "gth")
        b01_mark = em.cnt["sv"]
        s2e_mark = [em.cnt["sv"]] * 2
        st1_mark = em.cnt["mm"]
        ave_mark = em.cnt["ev"]
        mb4 = bass.AP(sb["maskS"], 0, [[128, 128], [0, 4], [1, 128]])
        for b8 in range(G // 8):
            sA = B[4] if b8 % 2 == 0 else B[6]
            sB = B[5] if b8 % 2 == 0 else B[7]
            pvq = B[0].ap().bitcast(BF16)
            pvk = B[1].ap().bitcast(BF16)
            em.wait("tensor", "sv", b01_mark)
            for jj in range(4):
                j = b8 * 4 + jj
                top(lambda t, j=j, jj=jj, pvq=pvq: t.transpose(
                    pvq[:, 128 * jj:128 * jj + 128],
                    sb["fq_g2"][:, 128 * j:128 * j + 128], sb["identb"][:]), "tr")
                top(lambda t, j=j, jj=jj, pvk=pvk: t.transpose(
                    pvk[:, 128 * jj:128 * jj + 128],
                    sb["fk_g2"][:, 128 * j:128 * j + 128], sb["identb"][:]), "tr")
            trm = em.cnt["tr"]
            em.wait("vector", "tr", trm)
            em.wait("vector", "mm", st1_mark)
            vopd(lambda e, pvq=pvq: e.tensor_copy(sb["fqT"][:, 0:512],
                                                  pvq[:, 0:512]), "sv")
            vopd(lambda e, pvk=pvk: e.tensor_copy(sb["fkT"][:, 0:512],
                                                  pvk[:, 0:512]), "sv")
            b01_mark = em.cnt["sv"]
            em.wait("tensor", "sv", b01_mark)
            em.wait("tensor", "sv", s2e_mark[b8 % 2])
            for jj in range(4):
                qs = sb["fqT"][:, 128 * jj:128 * jj + 128]
                ks = sb["fkT"][:, 128 * jj:128 * jj + 128]
                for dd in range(2):
                    bank = sA if dd == 0 else sB
                    top(lambda t, qs=qs, ks=ks, dd=dd, bank=bank, jj=jj:
                        t.matmul(bank[:, 128 * jj:128 * jj + 128],
                                 ks[64 * dd:64 * dd + 64, :],
                                 qs[64 * dd:64 * dd + 64, :],
                                 start=True, stop=True))
            st1_mark = em.cnt["mm"]
            em.wait("vector", "mm", st1_mark)
            vop(lambda e, sA=sA: e.tensor_tensor(
                sb["S2"][:, 0:512].rearrange("p (g c) -> p g c", g=4),
                sA[:].rearrange("p (g c) -> p g c", g=4), mb4, OP.mult), None)
            vopd(lambda e, sB=sB: e.tensor_tensor(
                sb["S2"][:, 512:1024].rearrange("p (g c) -> p g c", g=4),
                sB[:].rearrange("p (g c) -> p g c", g=4), mb4, OP.mult), "sv")
            s2e_mark[b8 % 2] = em.cnt["sv"]
            em.wait("tensor", "sv", s2e_mark[b8 % 2])
            em.wait("tensor", "ev", ave_mark)
            for gg in range(8):
                jj, dd = gg // 2, gg % 2
                gi = b8 * 8 + 2 * jj + dd
                s2sl = sb["S2"][:, 512 * dd + 128 * jj:512 * dd + 128 * jj + 128]
                top(lambda t, s2sl=s2sl, gi=gi: t.matmul(
                    B[3][:, gi:gi + 1], s2sl, sb["onesb"][:],
                    start=True, stop=True))
                top(lambda t, s2sl=s2sl, gi=gi, gg=gg: t.matmul(
                    B[2][:, 64 * gg:64 * gg + 64], s2sl,
                    sb["v_g2"][:, 64 * gi:64 * gi + 64], start=True, stop=True))
            st2m = em.cnt["mm"]
            em.wait("vector", "mm", st2m)
            rsl = sb["rc_c"][:, 8 * b8:8 * b8 + 8]
            vop(lambda e, b8=b8: e.tensor_scalar(
                sb["sc_c"][:, 0:8], B[3][:, 8 * b8:8 * b8 + 8], 1e-6, None,
                OP.add), None)
            vop(lambda e: e.drain(), None)
            vop(lambda e, rsl=rsl: e.reciprocal(rsl, sb["sc_c"][:, 0:8]), None)
            vop(lambda e: e.drain(), None)
            rbc = bass.AP(sb["rc_c"], 8 * b8, [[G, 128], [1, 8], [0, 64]])
            vopd(lambda e, b8=b8, rbc=rbc: e.tensor_tensor(
                sb["attn_g2"][:, 512 * b8:512 * (b8 + 1)].rearrange(
                    "p (g e) -> p g e", g=8),
                B[2][:].rearrange("p (g e) -> p g e", g=8), rbc, OP.mult), "ev")
            ave_mark = em.cnt["ev"]

        em.wait("scalar", "ev")
        em.wait("scalar", "gth", em.cnt["gth"])
        dma("scalar", dr["b_at"].ap(), sb["attn_g2"][:], "bnc")
        em.wait("scalar", "bnc")
        for gi in range(G):
            m = (8 * gi) // 128
            r0 = (8 * gi) % 128
            out_ap = bass.AP(sb["tm_sc"], r0 * (MC * D) + D * m,
                             [[MC * D, 8], [64, 16], [1, 64]])
            in_ap = bass.AP(dr["b_at"], 64 * gi, [[16 * FS, 8], [FS, 16], [1, 64]])
            dma("scalar", out_ap, in_ap, "gth")
        em.wait("tensor", "gth")
        em.wait("vector", "gth")
        em.wait("scalar", "gth")
        tm_to_fm(sb["tm_sc"], sb["fm_sc"])

        em.wait("sync", "mm")
        for k in range(KC):
            dma("sync", sb["wbuf"][:, D * k:D * (k + 1)],
                dr["c_wo"].ap()[128 * k:128 * k + 128, :], "wdma")
        em.wait("tensor", "wdma")
        em.wait("tensor", "sv")
        for m in range(MC):
            pb = (B[2 * (m % 2)], B[2 * (m % 2) + 1])
            em.wait("tensor", "ev")
            for k in range(KC):
                lhsT = sb["fm_sc"][:, TT * k + 128 * m:TT * k + 128 * m + 128]
                for h2 in range(2):
                    top(lambda t, lhsT=lhsT, k=k, h2=h2, pb=pb: t.matmul(
                        pb[h2][:, :], lhsT,
                        sb["wbuf"][:, D * k + 512 * h2:D * k + 512 * (h2 + 1)],
                        start=(k == 0), stop=(k == KC - 1)))
            em.wait("vector", "mm")
            for h2 in range(2):
                psl = pb[h2][:, :]
                bsl = sb["bo_rep"][:, 512 * h2:512 * (h2 + 1)]
                xsl = sb["x_tm"][:, D * m + 512 * h2:D * m + 512 * (h2 + 1)]
                osl = sb["xmid_tm"][:, D * m + 512 * h2:D * m + 512 * (h2 + 1)]
                vop(lambda e, psl=psl, bsl=bsl: e.scalar_tensor_tensor(
                    sb["ju2"][:, 0:512], psl, 1.0, bsl, OP.mult, OP.add), None)
                vopd(lambda e, osl=osl, xsl=xsl: e.tensor_tensor(
                    osl, sb["ju2"][:, 0:512], xsl, OP.add), "ev")

        ln_stats(sb["xmid_tm"], 4)
        ln_norm(sb["xmid_tm"], sb["tm_sc"], 4)
        em.wait("tensor", "ev")
        tm_to_fm(sb["tm_sc"], sb["fm_sc"])

        em.wait("tensor", "sv")
        gelu_mark = [em.cnt["act"]] * 2
        fcslice_mark = [em.cnt["mm"]] * 2
        pjslice_mark = [em.cnt["mm"]] * 2
        for half in range(2):
            projmm_marks = []
            for c in range(CC):
                wsl = sb["wslc2"][:, D * (c % 2):D * (c % 2) + D]
                em.wait("sync", "mm", fcslice_mark[c % 2])
                dma("sync", wsl,
                    bass.AP(dr["c_fcw"], c * KC * 16384,
                            [[128, 128], [16384, KC], [1, 128]]), "wdma")
                fcw_ld = em.cnt["wdma"]
                em.wait("sync", "wdma", fcw_ld)
                wpsl = sb["wslc"][:, D * (c % 2):D * (c % 2) + D]
                em.wait("sync", "mm", pjslice_mark[c % 2])
                dma("sync", wpsl, dr["c_projw"].ap()[128 * c:128 * c + 128, :],
                    "wdma")
                em.wait("sync", "wdma", em.cnt["wdma"])
                pjw_ld = em.cnt["wdma"]
                fcbank = B[4 + (c % 2)]
                em.wait("tensor", "wdma", fcw_ld)
                em.wait("tensor", "act", max(gelu_mark))
                for k in range(KC):
                    rhs = sb["fm_sc"][:, TT * k + 256 * half:
                                      TT * k + 256 * half + 256]
                    top(lambda t, wsl=wsl, k=k, rhs=rhs, fcbank=fcbank: t.matmul(
                        fcbank[:, 0:256], wsl[:, 128 * k:128 * k + 128], rhs,
                        start=(k == 0), stop=(k == KC - 1)))
                fcmm = em.cnt["mm"]
                fcslice_mark[c % 2] = fcmm
                gsl = sb["gch"][:, 256 * (c % 2):256 * (c % 2) + 256]
                em.wait("scalar", "mm", fcmm)
                if c >= 2:
                    em.wait("scalar", "mm", projmm_marks[c - 2])
                sop(lambda s, gsl=gsl, fcbank=fcbank, c=c: s.activation(
                    gsl, fcbank[:, 0:256], AF.Gelu,
                    bias=sb["fcb_col"][:, c:c + 1], scale=1.0), None)
                sopd(lambda s: s.copy(sb["junk"][0:1, 0:1],
                                      sb["junk"][0:1, 1:2]), "act")
                gelu_mark[c % 2] = em.cnt["act"]
                em.wait("tensor", "wdma", pjw_ld)
                em.wait("tensor", "act", gelu_mark[c % 2])
                for mloc in range(2):
                    for h2 in range(2):
                        top(lambda t, gsl=gsl, mloc=mloc, h2=h2, wpsl=wpsl, c=c:
                            t.matmul(B[2 * mloc + h2][:, :],
                                     gsl[:, 128 * mloc:128 * mloc + 128],
                                     wpsl[:, 512 * h2:512 * (h2 + 1)],
                                     start=(c == 0), stop=(c == CC - 1)))
                projmm_marks.append(em.cnt["mm"])
                pjslice_mark[c % 2] = em.cnt["mm"]
            em.wait("vector", "mm", em.cnt["mm"])
            for mloc in range(2):
                m = 2 * half + mloc
                for h2 in range(2):
                    psl = B[2 * mloc + h2][:, :]
                    bsl = sb["projb_rep"][:, 512 * h2:512 * (h2 + 1)]
                    midsl = sb["xmid_tm"][:, D * m + 512 * h2:D * m + 512 * (h2 + 1)]
                    xsl = sb["x_tm"][:, D * m + 512 * h2:D * m + 512 * (h2 + 1)]
                    vop(lambda e, psl=psl, bsl=bsl: e.scalar_tensor_tensor(
                        sb["ju2"][:, 0:512], psl, 1.0, bsl, OP.mult, OP.add), None)
                    vop(lambda e, midsl=midsl: e.tensor_tensor(
                        sb["ju3"][:, 0:512], sb["ju2"][:, 0:512], midsl, OP.add),
                        None)
                    vopd(lambda e, xsl=xsl: e.tensor_tensor(
                        xsl, sb["ju3"][:, 0:512], xsl, OP.add), "ev")
            em.wait("tensor", "ev", em.cnt["ev"])

        ln_stats(sb["x_tm"], 0)
        ln_norm(sb["x_tm"], x1cur, 0)

        for m in range(MC):
            sl = x1cur[:, D * m:D * (m + 1)]
            vop(lambda e, sl=sl, m=m: e.tensor_reduce(
                sb["qscale"][:, m:m + 1], sl, AX.X, OP.max,
                apply_absolute_value=True), None)
        vop(lambda e: e.drain(), None)
        vop(lambda e: e.tensor_scalar(sb["qscale"][:, 0:MC], sb["qscale"][:, 0:MC],
                                      1e-5, None, OP.max), None)
        vop(lambda e: e.drain(), None)
        vop(lambda e: e.reciprocal(sb["sc_c"][:, 0:MC], sb["qscale"][:, 0:MC]), None)
        vop(lambda e: e.tensor_scalar(sb["qscale"][:, 4:4 + MC],
                                      sb["qscale"][:, 0:MC], 1.0 / 127.0, None,
                                      OP.mult), None)
        vop(lambda e: e.drain(), None)
        vop(lambda e: e.tensor_scalar(sb["qscale"][:, 0:MC], sb["sc_c"][:, 0:MC],
                                      127.0, None, OP.mult), None)
        vop(lambda e: e.drain(), None)
        for m in range(MC):
            ssl = x1cur[:, D * m:D * (m + 1)]
            scol = sb["qscale"][:, m:m + 1]
            dcol = sb["qscale"][:, 4 + m:5 + m]
            osl = sb["tm_sc"][:, D * m:D * (m + 1)]
            vop(lambda e, ssl=ssl, scol=scol: e.tensor_scalar(
                sb["junk"][:], ssl, scol, MAGIC, OP.mult, OP.add), None)
            vopd(lambda e, osl=osl, dcol=dcol: e.tensor_scalar(
                osl, sb["junk"][:], MAGIC, dcol, OP.subtract, OP.mult), "ev")
        em.wait("tensor", "ev")
        tm_to_fm(sb["tm_sc"], wcur, fstride=WIN, foff=1)
        em.wait("vector", "sv")
        for k in range(KC):
            if ti > 0:
                vop(lambda e, k=k, wc=wcur, wp=wprev: e.tensor_copy(
                    wc[:, WIN * k:WIN * k + 1], wp[:, WIN * k + TT:WIN * k + TT + 1]),
                    None)
            (vopd if k == KC - 1 else vop)(
                lambda e, k=k, wc=wcur, wp=wprev: e.tensor_copy(
                    wp[:, WIN * k + TT + 1:WIN * k + TT + 2],
                    wc[:, WIN * k + 1:WIN * k + 2]), "ev")

    def phase_ffn(tj, last=False):
        if "ffn" in barrier:
            full_barrier()
        x1p = sb["x1a"] if tj % 2 == 0 else sb["x1b"]
        wp = sb["xq1w_a"] if tj % 2 == 0 else sb["xq1w_b"]
        HT = TT // 2
        if last:
            for k in range(KC):
                (vopd if k == KC - 1 else vop)(lambda e, k=k, wp=wp: e.memset(
                    wp[:, WIN * k + TT + 1:WIN * k + TT + 2], 0.0), "ev")
            em.wait("tensor", "ev")
        em.wait("tensor", "sv")
        for half in range(2):
            vop(lambda e: e.memset(sb["rmax"][:], 0.0), None)
            vop(lambda e: e.memset(sb["rmin"][:], 0.0), "ev")
            w1slice_mark = [em.cnt["mm"]] * 2
            hbank_mark = [em.cnt["ev"]] * 2
            convslot_mark = [em.cnt["act"]] * 2
            w1mm_marks = []
            for c in range(CC):
                wsl = sb["wslc2"][:, D * (c % 2):D * (c % 2) + D]
                em.wait("sync", "mm", w1slice_mark[c % 2])
                dma("sync", wsl,
                    bass.AP(dr["c_w1t"], c * KC * 16384,
                            [[128, 128], [16384, KC], [1, 128]]), "wdma")
                ldm = em.cnt["wdma"]
                bA = B[4 + (c % 2)]
                em.wait("tensor", "wdma", ldm)
                em.wait("tensor", "ev", hbank_mark[c % 2])
                for k in range(KC):
                    lh = wsl[:, 128 * k:128 * k + 128]
                    top(lambda t, lh=lh, bA=bA, wp=wp, k=k, half=half: t.matmul(
                        bA[:, 0:258], lh,
                        wp[:, WIN * k + HT * half:WIN * k + HT * half + 258],
                        start=(k == 0), stop=(k == KC - 1)))
                w1mm_marks.append(em.cnt["mm"])
                w1slice_mark[c % 2] = em.cnt["mm"]
                hw = sb["hwin"][:, 260 * (c % 2):260 * (c % 2) + 258]
                cva = sb["cv1"][:, 256 * (c % 2):256 * (c % 2) + 256]
                cvb = sb["cv2"][:, 256 * (c % 2):256 * (c % 2) + 256]
                em.wait("vector", "mm", w1mm_marks[-1])
                em.wait("vector", "act", convslot_mark[c % 2])
                vopd(lambda e, bA=bA, hw=hw: e.tensor_copy(hw, bA[:, 0:258]),
                     "ev")
                hbank_mark[c % 2] = em.cnt["ev"]
                w0c = sb["convw_col"][:, c:c + 1]
                w1c = sb["convw_col"][:, CC + c:CC + c + 1]
                w2c = sb["convw_col"][:, 2 * CC + c:2 * CC + c + 1]
                vop(lambda e, w1c=w1c, hw=hw, cva=cva: e.tensor_scalar(
                    cva, hw[:, 1:257], w1c, None, OP.mult), None)
                vop(lambda e, w0c=w0c, hw=hw, cva=cva, cvb=cvb:
                    e.scalar_tensor_tensor(cvb, hw[:, 0:256], w0c, cva,
                                           OP.mult, OP.add), None)
                vopd(lambda e, w2c=w2c, hw=hw, cva=cva, cvb=cvb:
                     e.scalar_tensor_tensor(cva, hw[:, 2:258], w2c, cvb,
                                            OP.mult, OP.add), "ev")
                cvm = em.cnt["ev"]
                em.wait("scalar", "ev", cvm)
                hs = sb["hsil"][:, HT * c:HT * (c + 1)]
                sopd(lambda s, hs=hs, c=c, cva=cva: s.activation(
                    hs, cva, AF.Silu, bias=sb["convb_col"][:, c:c + 1],
                    scale=1.0), "act")
                convslot_mark[c % 2] = em.cnt["act"]
                em.wait("vector", "act", em.cnt["act"])
                vop(lambda e, hs=hs: e.tensor_tensor(sb["rmax"][:], sb["rmax"][:],
                                                     hs, OP.max), None)
                vop(lambda e, hs=hs: e.tensor_tensor(sb["rmin"][:], sb["rmin"][:],
                                                     hs, OP.min), "ev")

            em.wait("tensor", "ev", em.cnt["ev"])
            for mloc in range(2):
                m = 2 * half + mloc
                pvA = B[6].ap().bitcast(BF16)[:, 0:128]
                pvB = B[6].ap().bitcast(BF16)[:, 128:256]
                top(lambda t, mloc=mloc, pvA=pvA: t.transpose(
                    pvA, sb["rmax"][:, 128 * mloc:128 * mloc + 128],
                    sb["identb"][:]), "tr")
                top(lambda t, mloc=mloc, pvB=pvB: t.transpose(
                    pvB, sb["rmin"][:, 128 * mloc:128 * mloc + 128],
                    sb["identb"][:]), "tr")
                em.wait("vector", "tr")
                vop(lambda e, pvA=pvA: e.tensor_reduce(
                    sb["sc_a"][:, 0:1], pvA, AX.X, OP.max,
                    apply_absolute_value=True), None)
                vopd(lambda e, pvB=pvB: e.tensor_reduce(
                    sb["sc_a"][:, 1:2], pvB, AX.X, OP.max,
                    apply_absolute_value=True), "ev")
                vopd(lambda e, m=m: e.tensor_tensor(
                    sb["q2s"][:, m:m + 1], sb["sc_a"][:, 0:1], sb["sc_a"][:, 1:2],
                    OP.max), "sv")
                em.wait("tensor", "sv")
            c0 = 2 * half
            vop(lambda e: e.drain(), None)
            vop(lambda e, c0=c0: e.tensor_scalar(
                sb["q2s"][:, c0:c0 + 2], sb["q2s"][:, c0:c0 + 2], 1e-5, None,
                OP.max), None)
            vop(lambda e: e.drain(), None)
            vop(lambda e, c0=c0: e.reciprocal(sb["sc_c"][:, 0:2],
                                              sb["q2s"][:, c0:c0 + 2]), None)
            vop(lambda e, c0=c0: e.tensor_scalar(
                sb["q2s"][:, 4 + c0:6 + c0], sb["q2s"][:, c0:c0 + 2],
                1.0 / 127.0, None, OP.mult), None)
            vop(lambda e: e.drain(), None)
            vop(lambda e, c0=c0: e.tensor_tensor(
                sb["q2s"][:, 4 + c0:6 + c0], sb["q2s"][:, 4 + c0:6 + c0],
                sb["mbar"][:, 1:2].broadcast_to([128, 2]), OP.mult), None)
            vopd(lambda e, c0=c0: e.tensor_scalar(
                sb["q2s"][:, c0:c0 + 2], sb["sc_c"][:, 0:2], 127.0, None,
                OP.mult), "ev")
            em.wait("tensor", "ev")
            for mloc in range(2):
                m = 2 * half + mloc
                top(lambda t, m=m, mloc=mloc: t.transpose(
                    B[7][0:1, 128 * mloc:128 * mloc + 128], sb["q2s"][:, m:m + 1],
                    sb["identf"][:]), "tr")
            em.wait("vector", "tr")
            vopd(lambda e: e.tensor_copy(sb["s2row"][:], B[7][0:1, 0:HT]), "sv")
            em.wait("tensor", "sv")
            top(lambda t: t.matmul(B[7][:, 0:HT], sb["onesrow"][:], sb["s2row"][:],
                                   start=True, stop=True))
            em.wait("vector", "mm")
            vopd(lambda e: e.tensor_copy(sb["s2bc"][:], B[7][:, 0:HT]), "ev")

            em.wait("tensor", "ev", em.cnt["ev"])
            w2slice_mark = [em.cnt["mm"]] * 2
            hq_mark = [em.cnt["mm"]] * 2
            for c in range(CC):
                hs = sb["hsil"][:, HT * c:HT * (c + 1)]
                hq = sb["hq_c"][:, HT * (c % 2):HT * (c % 2) + HT]
                em.wait("vector", "mm", hq_mark[c % 2])
                vop(lambda e, hs=hs: e.tensor_tensor(
                    sb["ju2"][:, 0:HT], hs, sb["s2bc"][:], OP.mult), None)
                vopd(lambda e, hq=hq: e.tensor_scalar(
                    hq, sb["ju2"][:, 0:HT], MAGIC, MAGIC, OP.add, OP.subtract),
                    "ev")
                rndm = em.cnt["ev"]
                wsl = sb["wslc"][:, D * (c % 2):D * (c % 2) + D]
                em.wait("sync", "mm", w2slice_mark[c % 2])
                dma("sync", wsl, dr["c_w2t"].ap()[128 * c:128 * c + 128, :], "wdma")
                ldm = em.cnt["wdma"]
                em.wait("tensor", "wdma", ldm)
                em.wait("tensor", "ev", rndm)
                for mloc in range(2):
                    for h2 in range(2):
                        top(lambda t, hq=hq, mloc=mloc, h2=h2, wsl=wsl, c=c:
                            t.matmul(B[2 * mloc + h2][:, :],
                                     hq[:, 128 * mloc:128 * mloc + 128],
                                     wsl[:, 512 * h2:512 * (h2 + 1)],
                                     start=(c == 0), stop=(c == CC - 1)))
                hq_mark[c % 2] = em.cnt["mm"]
                w2slice_mark[c % 2] = em.cnt["mm"]
            em.wait("vector", "mm")
            for mloc in range(2):
                m = 2 * half + mloc
                for h2 in range(2):
                    psl = B[2 * mloc + h2][:, :]
                    x1sl = x1p[:, D * m + 512 * h2:D * m + 512 * (h2 + 1)]
                    osl = sb["xmid_tm"][:, D * m + 512 * h2:D * m + 512 * (h2 + 1)]
                    dcol = sb["q2s"][:, 4 + m:5 + m]
                    vopd(lambda e, psl=psl, x1sl=x1sl, osl=osl, dcol=dcol:
                         e.scalar_tensor_tensor(osl, psl, dcol, x1sl,
                                                OP.mult, OP.add), "ev")
            em.wait("tensor", "ev")

        ln_stats(sb["xmid_tm"], 8)
        ln_norm(sb["xmid_tm"], sb["x_tm"], 8)
        em.wait("sync", "ev")
        for m in range(MC):
            dma("sync", y_d[tj * TT + 128 * m:tj * TT + 128 * (m + 1), :],
                sb["x_tm"][:, D * m:D * (m + 1)], "ydma")

    with nc.Block() as block:
        for ti in range(nt_limit):
            phase_attn(ti)
            if ti > 0 and do_ffn:
                phase_ffn(ti - 1)
        if do_ffn:
            phase_ffn(nt_limit - 1, last=True)
        if dump:
            for eng in ("sync", "tensor", "vector", "scalar", "gpsimd"):
                for s in ["wdma", "xdma", "ydma", "bnc", "gth", "mm", "tr",
                          "ev", "sv", "gp", "act"]:
                    em.wait(eng, s)
            for nm in dump:
                t = sb[nm]
                shp = list(t.shape)
                dbg = nc.dram_tensor("dbg_" + nm, shp, t.dtype,
                                     kind="ExternalOutput")
                dma("sync", dbg.ap(), t[:], "ydma")
        em.wait("sync", "ydma")
        em.wait("gpsimd", "ydma")

        @block.sync
        def _(e):
            for fn in em.lists["sync"]:
                fn(e)

        @block.tensor
        def _(e):
            for fn in em.lists["tensor"]:
                fn(e)

        @block.vector
        def _(e):
            for fn in em.lists["vector"]:
                fn(e)

        @block.scalar
        def _(e):
            for fn in em.lists["scalar"]:
                fn(e)

        @block.gpsimd
        def _(e):
            for fn in em.lists["gpsimd"]:
                fn(e)

    es.close()
    return nc


_NC_CACHE = None


def _get_nc():
    global _NC_CACHE
    if _NC_CACHE is None:
        _NC_CACHE = build_program()
    return _NC_CACHE


def kernel(**inputs):
    ii = {k: np.asarray(v) for k, v in inputs.items()}
    x = np.ascontiguousarray(ii["L0_attn_q_x"].astype(np.float32, copy=False))
    shared = {
        "wq": ii["L0_attn_wq"], "wk": ii["L0_attn_wk"], "wv": ii["L0_attn_wv"],
        "wo": ii["L0_attn_wo"], "fcw": ii["L0_attn_fc_w"],
        "projw": ii["L0_attn_proj_w"],
        "w1T": np.ascontiguousarray(ii["L0_ffn_w1"].T),
        "w2T": np.ascontiguousarray(ii["L0_ffn_w2"].T),
        "ln1w": ii["L0_attn_ln1_w"], "ln1b": ii["L0_attn_ln1_b"],
        "ln2w": ii["L0_attn_ln2_w"], "ln2b": ii["L0_attn_ln2_b"],
        "bq": ii["L0_attn_bq"], "bk": ii["L0_attn_bk"], "bv": ii["L0_attn_bv"],
        "bo": ii["L0_attn_bo"], "fcb": ii["L0_attn_fc_b"],
        "projb": ii["L0_attn_proj_b"],
        "convw0": np.ascontiguousarray(ii["L0_ffn_conv_w"][:, 0, 0]),
        "convw1": np.ascontiguousarray(ii["L0_ffn_conv_w"][:, 0, 1]),
        "convw2": np.ascontiguousarray(ii["L0_ffn_conv_w"][:, 0, 2]),
        "identb": np.eye(128, dtype=ml_dtypes.bfloat16),
        "identf": np.eye(128, dtype=np.float32),
        "maskS": np.kron(np.eye(8, dtype=np.float32), np.ones((16, 16), np.float32)),
        "convb": ii["L0_ffn_conv_b"],
    }
    shared = {k: (np.ascontiguousarray(v.astype(np.float32, copy=False))
                  if v.dtype != ml_dtypes.bfloat16 else v)
              for k, v in shared.items()}
    nc = _get_nc()
    in_maps = [dict(shared, x=np.ascontiguousarray(x[c])) for c in range(NCORES)]
    res = run_bass_kernel_spmd(nc, in_maps, core_ids=list(range(NCORES)))
    out = np.stack([res.results[c]["y"] for c in range(NCORES)], axis=0)
    return out.astype(np.float32, copy=False)


if __name__ == "__main__":
    nc = build_program()
    print("built")


# version 16
# speedup vs baseline: 1.5423x; 1.0077x over previous
SERIAL_FC = False
"""Fused megakernel for mk_seq_postln_d1 (linear-attention block + BitNet FFN).

One raw-Bass program per NeuronCore, launched exactly once via
run_bass_kernel_spmd.  Batch (8) sharded across the 8 cores; each core runs
the entire forward for its own 4096-token sequence.

Structure:
  PROLOGUE  - build per-core bf16 weight caches in HBM (LN gains folded into
              weights, biases pre-folded via on-device matmuls, ternary
              quantization of w1/w2 done in fp32), load constants.
  MAIN LOOP - 8 token tiles of 512.  Per tile: LN1 -> q/k/v (token-major) ->
              elu+1 -> DRAM bounce -> grouped gathers -> 8-token-group
              block-diagonal attention matmuls -> wo -> LN2 -> fc/gelu/proj
              -> post-LN -> act-quant -> transposed window.  The BitNet FFN
              for tile i runs during iteration i+1 (the depthwise conv needs
              one token of lookahead).

Layouts: TM = [128 tokens x feat] (chunk-major), FM = [128 feat x tokens],
g2 = [128 = 16n+h x 64d] 8-token attention groups.
"""
import numpy as np
import ml_dtypes
import concourse.bass as bass
import concourse.mybir as mybir
from concourse.bass_utils import run_bass_kernel_spmd
from contextlib import ExitStack

F32 = mybir.dt.float32
BF16 = mybir.dt.bfloat16
I32 = mybir.dt.int32
OP = mybir.AluOpType
AF = mybir.ActivationFunctionType
AX = mybir.AxisListType

D = 1024
H = 16
MLP = 4096
N = 4096
TT = 512
NT = N // TT
MC = TT // 128
KC = D // 128
CC = MLP // 128
G = TT // 8
WIN = TT + 2
MAGIC = float(1.5 * 2 ** 23)
RSQRT_MAGIC = 0x5f3759df
NCORES = 8


class Emitter:
    def __init__(self):
        self.lists = {e: [] for e in ("sync", "tensor", "vector", "scalar", "gpsimd")}
        self.sems = {}
        self.cnt = {}

    def op(self, eng, fn, inc=None, n=1):
        if inc is not None:
            self.cnt[inc] += n
            sem = self.sems[inc]
            self.lists[eng].append(lambda e, fn=fn, sem=sem, n=n: fn(e).then_inc(sem, n))
        else:
            self.lists[eng].append(lambda e, fn=fn: fn(e))

    def wait(self, eng, name, val=None):
        v = self.cnt[name] if val is None else val
        sem = self.sems[name]
        self.lists[eng].append(lambda e, sem=sem, v=v: e.wait_ge(sem, v))


def build_program(nt_limit=NT, do_ffn=True, dump=(), skip=(), serial_fc=False,
                  barrier=()):
    global SERIAL_FC
    SERIAL_FC = int(serial_fc)
    nc = bass.Bass("TRN2", target_bir_lowering=False)
    es = ExitStack()
    E = es.enter_context

    dr = {}

    def dt(name, shape, dtype, kind="Internal"):
        dr[name] = nc.dram_tensor(name, shape, dtype, kind=kind)
        return dr[name]

    dt("x", [N, D], F32, "ExternalInput")
    for nm, shp in [("wq", [D, D]), ("wk", [D, D]), ("wv", [D, D]), ("wo", [D, D]),
                    ("fcw", [D, MLP]), ("projw", [MLP, D]),
                    ("w1T", [D, MLP]), ("w2T", [MLP, D]),
                    ("ln1w", [D]), ("ln1b", [D]), ("ln2w", [D]), ("ln2b", [D]),
                    ("bq", [D]), ("bk", [D]), ("bv", [D]), ("bo", [D]),
                    ("fcb", [MLP]), ("projb", [D]),
                    ("convw0", [MLP]), ("convw1", [MLP]), ("convw2", [MLP]),
                    ("convb", [MLP])]:
        dt(nm, shp, F32, "ExternalInput")
    dt("identb", [128, 128], BF16, "ExternalInput")
    dt("identf", [128, 128], F32, "ExternalInput")
    dt("maskS", [128, 128], F32, "ExternalInput")
    dt("y", [N, D], F32, "ExternalOutput")

    dt("c_wq", [D, D], BF16)
    dt("c_wk", [D, D], BF16)
    dt("c_wv", [D, D], BF16)
    dt("c_wo", [D, D], BF16)
    dt("c_fcw", [KC * CC * 128 * 128], BF16)
    dt("c_projw", [MLP, D], BF16)
    dt("c_w1t", [KC * CC * 128 * 128], BF16)
    dt("c_w2t", [MLP, D], BF16)
    dt("b_fq", [TT, D], BF16)
    dt("b_fk", [TT, D], BF16)
    dt("b_v", [TT, D], BF16)
    dt("b_at", [128, 64 * G], BF16)
    dt("b_row", [D], F32)

    sb = {}

    def st(name, shape, dtype):
        sb[name] = E(nc.sbuf_tensor("s_" + name, shape, dtype))
        return sb[name]

    st("identb", [128, 128], BF16)
    st("identf", [128, 128], F32)
    st("maskS", [128, 128], F32)
    st("onesb", [128, 1], BF16)
    st("onesf", [128, 1], F32)
    st("onesrow", [1, 128], F32)
    st("magici", [128, 16], I32)
    st("lnw_cols", [128, 2 * KC], F32)
    st("ln1b_col", [128, KC], BF16)
    st("ln2b_col", [128, KC], BF16)
    st("bias_sc", [128, KC], F32)
    st("fcb_col", [128, CC], F32)
    st("convw_col", [128, 3 * CC], F32)
    st("convb_col", [128, CC], F32)
    st("mbar", [128, 4], F32)
    st("bq_rep", [128, D], BF16)
    st("bk_rep", [128, D], BF16)
    st("bv_rep", [128, D], BF16)
    st("bo_rep", [128, D], BF16)
    st("projb_rep", [128, D], BF16)
    st("wbuf", [128, KC * D], BF16)
    st("wslc", [128, 2 * D], BF16)
    st("wslc2", [128, 2 * D], BF16)
    st("x_tm", [128, MC * D], F32)
    st("xmid_tm", [128, MC * D], F32)
    st("x1a", [128, MC * D], F32)
    st("x1b", [128, MC * D], F32)
    st("tm_sc", [128, MC * D], BF16)
    st("fm_sc", [128, KC * TT], BF16)
    st("stats", [128, 16], F32)
    st("stats2", [128, 16], F32)
    st("sc_a", [128, 16], F32)
    st("sc_b", [128, 16], F32)
    st("sc_c", [128, 16], F32)
    st("qscale", [128, 8], F32)
    st("q2s", [128, 8], F32)
    st("dbgk", [128, 256], F32)
    st("junk", [128, D], F32)
    st("ju2", [128, 1024], F32)
    st("ju3", [128, 1024], F32)

    B = {}
    for i in range(8):
        B[i] = E(nc.psum_tensor(f"bank{i}", [128, 512], F32))

    em = Emitter()
    for s in ["ldc", "wdma", "xdma", "ydma", "bnc", "gth", "mm", "tr", "ev",
              "sv", "gp", "act"]:
        em.sems[s] = E(nc.semaphore(s))
        em.cnt[s] = 0

    def vop(fn, inc="ev"):
        em.op("vector", fn, inc)

    def vopd(fn, inc="ev"):
        em.op("vector", fn, None)
        em.op("vector", lambda e: e.drain(), inc)

    def sop(fn, inc="act"):
        em.op("scalar", fn, inc)

    def sopd(fn, inc="act"):
        em.op("scalar", fn, None)
        em.op("scalar", lambda e: e.drain(), inc)

    def gop(fn, inc="gp"):
        em.op("gpsimd", fn, inc)

    def top(fn, inc="mm"):
        em.op("tensor", fn, inc)

    def dma(eng, out_ap, in_ap, inc):
        em.op(eng, lambda e, o=out_ap, i=in_ap: e.dma_start(out=o, in_=i), inc, 16)

    def dmas(eng, out_ap, in_ap, inc):
        em.op(eng, lambda e, o=out_ap, i=in_ap: e.dma_start(
            out=o, in_=i, allow_slow_non_contiguous=True), inc, 16)

    pro_es = ExitStack()
    sb["wf32b"] = pro_es.enter_context(nc.sbuf_tensor("s_wf32b", [128, MLP], F32))
    sb["wbf"] = pro_es.enter_context(nc.sbuf_tensor("s_wbf", [128, 2 * MLP], BF16))

    for nm in ["identb", "identf", "maskS"]:
        dma("sync", sb[nm][:], dr[nm].ap(), "ldc")
    for j, nm in enumerate(["ln1w", "ln2w"]):
        dmas("sync", bass.AP(sb["lnw_cols"], j * KC, [[2 * KC, 128], [1, KC]]),
            bass.AP(dr[nm], 0, [[1, 128], [128, KC]]), "ldc")
    dmas("sync", bass.AP(sb["convb_col"], 0, [[CC, 128], [1, CC]]),
        bass.AP(dr["convb"], 0, [[1, 128], [128, CC]]), "ldc")
    dmas("gpsimd", sb["bo_rep"][:], bass.AP(dr["bo"], 0, [[0, 128], [1, D]]), "ldc")
    dmas("gpsimd", sb["projb_rep"][:], bass.AP(dr["projb"], 0, [[0, 128], [1, D]]), "ldc")
    dmas("gpsimd", sb["ln1b_col"][:], bass.AP(dr["ln1b"], 0, [[1, 128], [128, KC]]), "ldc")
    dmas("gpsimd", sb["ln2b_col"][:], bass.AP(dr["ln2b"], 0, [[1, 128], [128, KC]]), "ldc")
    em.wait("vector", "ldc")
    vop(lambda e: e.memset(sb["onesb"][:], 1.0), None)
    vop(lambda e: e.memset(sb["onesf"][:], 1.0), None)
    vop(lambda e: e.memset(sb["magici"][:], RSQRT_MAGIC), None)
    vop(lambda e: e.memset(sb["onesrow"][:], 1.0))

    def build_attn_cache(wname, cname, borig, brep):
        dmas("sync", bass.AP(sb["bias_sc"], 0, [[KC, 128], [1, KC]]),
             bass.AP(dr[borig], 0, [[1, 128], [128, KC]]), "wdma")
        em.wait("vector", "wdma")
        for k in range(KC):
            dma("sync", sb["wf32b"][:, 0:D], dr[wname].ap()[128 * k:128 * k + 128, :],
                "wdma")
            em.wait("vector", "wdma")
            lcol = sb["lnw_cols"][:, k:k + 1]
            vop(lambda e, lcol=lcol: e.tensor_scalar(
                sb["wbf"][:, 0:D], sb["wf32b"][:, 0:D], lcol, None, OP.mult), None)
            vopd(lambda e: e.tensor_copy(sb["wbf"][:, D:2 * D], sb["wf32b"][:, 0:D]))
            em.wait("sync", "ev")
            dma("sync", dr[cname].ap()[128 * k:128 * k + 128, :], sb["wbf"][:, 0:D],
                "wdma")
            em.wait("tensor", "ev")
            for c in range(KC):
                top(lambda t, k=k, c=c: t.matmul(
                    B[0][:, c:c + 1], sb["wbf"][:, D + 128 * c:D + 128 * c + 128],
                    sb["ln1b_col"][:, k:k + 1], start=True, stop=True))
            em.wait("vector", "mm")
            vopd(lambda e: e.tensor_tensor(sb["bias_sc"][:], B[0][:, 0:KC],
                                           sb["bias_sc"][:], OP.add))
        em.wait("sync", "ev")
        dmas("sync", bass.AP(dr["b_row"], 0, [[1, 128], [128, KC]]),
            bass.AP(sb["bias_sc"], 0, [[KC, 128], [1, KC]]), "wdma")
        em.wait("gpsimd", "wdma")
        dma("gpsimd", brep[:], bass.AP(dr["b_row"], 0, [[0, 128], [1, D]]), "wdma")

    if "attn" not in skip:
        build_attn_cache("wq", "c_wq", "bq", sb["bq_rep"])
        build_attn_cache("wk", "c_wk", "bk", sb["bk_rep"])
        build_attn_cache("wv", "c_wv", "bv", sb["bv_rep"])

    dmas("sync", bass.AP(sb["fcb_col"], 0, [[CC, 128], [1, CC]]),
         bass.AP(dr["fcb"], 0, [[1, 128], [128, CC]]), "wdma")
    em.wait("vector", "wdma")
    for k in range(KC):
        dma("sync", sb["wf32b"][:], dr["fcw"].ap()[128 * k:128 * k + 128, :], "wdma")
        em.wait("vector", "wdma")
        lcol = sb["lnw_cols"][:, KC + k:KC + k + 1]
        vop(lambda e, lcol=lcol: e.tensor_scalar(
            sb["wbf"][:, 0:MLP], sb["wf32b"][:], lcol, None, OP.mult), None)
        vopd(lambda e: e.tensor_copy(sb["wbf"][:, MLP:2 * MLP], sb["wf32b"][:]))
        em.wait("sync", "ev")
        dma("sync", bass.AP(dr["c_fcw"], k * 16384,
                            [[128, 128], [KC * 16384, CC], [1, 128]]),
            bass.AP(sb["wbf"], 0, [[2 * MLP, 128], [128, CC], [1, 128]]), "wdma")
        em.wait("tensor", "ev")
        for c in range(CC):
            top(lambda t, k=k, c=c: t.matmul(
                B[1][:, c:c + 1], sb["wbf"][:, MLP + 128 * c:MLP + 128 * c + 128],
                sb["ln2b_col"][:, k:k + 1], start=True, stop=True))
        em.wait("vector", "mm")
        vopd(lambda e: e.tensor_tensor(sb["fcb_col"][:], B[1][:, 0:CC],
                                       sb["fcb_col"][:], OP.add))

    if "casts" not in skip:
        for src_, dst_ in [("wo", "c_wo"), ("projw", "c_projw")]:
            nr = dr[src_].shape[0]
            for k in range(0, nr, 128):
                em.wait("sync", "ev")
                dma("sync", sb["wf32b"][:, 0:D],
                    dr[src_].ap()[k:k + 128, :], "wdma")
                em.wait("vector", "wdma")
                vopd(lambda e: e.tensor_copy(sb["wbf"][:, 0:D],
                                             sb["wf32b"][:, 0:D]))
                em.wait("sync", "ev")
                dma("sync", dr[dst_].ap()[k:k + 128, :], sb["wbf"][:, 0:D],
                    "wdma")

    def build_ternary(src, dst, mi, blocked):
        nchunk = dr[src].shape[0] // 128
        width = dr[src].shape[1]
        vop(lambda e: e.memset(sb["sc_a"][:, 0:1], 0.0), None)
        for k in range(nchunk):
            em.wait("sync", "ev")
            dma("sync", sb["wf32b"][:, 0:width],
                dr[src].ap()[128 * k:128 * k + 128, :], "wdma")
            em.wait("vector", "wdma")
            vop(lambda e, w=width: e.tensor_reduce(
                sb["sc_b"][:, 0:1], sb["wf32b"][:, 0:w], AX.X, OP.add,
                apply_absolute_value=True), None)
            vop(lambda e: e.drain(), None)
            vop(lambda e: e.tensor_tensor(sb["sc_a"][:, 0:1], sb["sc_a"][:, 0:1],
                                          sb["sc_b"][:, 0:1], OP.add), "ev")
        em.wait("tensor", "ev")
        top(lambda t: t.matmul(B[2][0:1, 0:1], sb["onesf"][:], sb["sc_a"][:, 0:1],
                               start=True, stop=True))
        em.wait("vector", "mm")
        tot = float(nchunk * 128 * width)
        vopd(lambda e, tot=tot: e.tensor_scalar(
            sb["sc_b"][0:1, 0:1], B[2][0:1, 0:1], 1.0 / tot, 1e-5,
            OP.mult, OP.max), "ev")
        vop(lambda e: e.drain(), None)
        vopd(lambda e: e.reciprocal(sb["sc_b"][0:1, 1:2], sb["sc_b"][0:1, 0:1]))
        em.wait("sync", "ev")
        dma("sync", bass.AP(dr["b_row"], 0, [[1, 1], [1, 2]]), sb["sc_b"][0:1, 0:2],
            "wdma")
        em.wait("gpsimd", "wdma")
        dma("gpsimd", sb["mbar"][:, mi:mi + 1],
            bass.AP(dr["b_row"], 0, [[0, 128], [1, 1]]), "wdma")
        dma("gpsimd", sb["mbar"][:, mi + 2:mi + 3],
            bass.AP(dr["b_row"], 1, [[0, 128], [1, 1]]), "wdma")
        em.wait("vector", "wdma")
        for k in range(nchunk):
            dma("sync", sb["wf32b"][:, 0:width],
                dr[src].ap()[128 * k:128 * k + 128, :], "wdma")
            em.wait("vector", "wdma")
            scol = sb["mbar"][:, mi + 2:mi + 3]
            vop(lambda e, w=width, scol=scol: e.tensor_scalar(
                sb["wf32b"][:, 0:w], sb["wf32b"][:, 0:w], scol, 1.0,
                OP.mult, OP.min), None)
            vop(lambda e, w=width: e.tensor_scalar(
                sb["wf32b"][:, 0:w], sb["wf32b"][:, 0:w], -1.0, MAGIC,
                OP.max, OP.add), None)
            vopd(lambda e, w=width: e.tensor_scalar(
                sb["wbf"][:, 0:w], sb["wf32b"][:, 0:w], MAGIC, None, OP.subtract))
            em.wait("sync", "ev")
            if blocked:
                ncb = width // 128
                dma("sync", bass.AP(dr[dst], k * 16384,
                                    [[128, 128], [KC * 16384, ncb], [1, 128]]),
                    bass.AP(sb["wbf"], 0, [[2 * MLP, 128], [128, ncb], [1, 128]]),
                    "wdma")
            else:
                dma("sync", dr[dst].ap()[128 * k:128 * k + 128, :],
                    sb["wbf"][:, 0:width], "wdma")
            em.wait("vector", "wdma")

    if "ternary" not in skip:
        build_ternary("w1T", "c_w1t", 0, True)
        build_ternary("w2T", "c_w2t", 1, False)

    for j, nm in enumerate(["convw0", "convw1", "convw2"]):
        dmas("sync", bass.AP(sb["convw_col"], j * CC, [[3 * CC, 128], [1, CC]]),
            bass.AP(dr[nm], 0, [[1, 128], [128, CC]]), "wdma")
    em.wait("vector", "wdma")
    vop(lambda e: e.tensor_scalar(sb["convw_col"][:], sb["convw_col"][:],
                                  sb["mbar"][:, 0:1], None, OP.mult))

    for eng in ("sync", "tensor", "vector", "scalar", "gpsimd"):
        for s in ["wdma", "ev", "mm", "gp", "act", "ldc"]:
            em.wait(eng, s)

    pro_es.close()
    HT = TT // 2
    st("fq_g2", [128, 64 * G], BF16)
    st("fk_g2", [128, 64 * G], BF16)
    st("v_g2", [128, 64 * G], BF16)
    st("fqT", [128, 4 * 128], BF16)
    st("fkT", [128, 4 * 128], BF16)
    st("S2", [128, 8 * 128], BF16)
    st("rc_c", [128, G], F32)
    st("attn_g2", [128, 64 * G], BF16)
    st("xq1w_a", [128, KC * WIN], BF16)
    st("xq1w_b", [128, KC * WIN], BF16)
    st("hsil", [128, CC * HT], BF16)
    st("hwin", [128, 2 * 260], BF16)
    st("cv1", [128, 2 * HT], BF16)
    st("cv2", [128, 2 * HT], BF16)
    st("rmax", [128, HT], BF16)
    st("rmin", [128, HT], BF16)
    st("hq_c", [128, 2 * HT], BF16)
    st("gch", [128, 2 * 256], BF16)
    st("s2bc", [128, HT], F32)
    st("s2row", [1, HT], F32)
    vop(lambda e: e.memset(sb["xq1w_a"][:], 0.0), None)
    vop(lambda e: e.memset(sb["xq1w_b"][:], 0.0), "ev")
    for eng in ("sync", "tensor", "scalar", "gpsimd"):
        em.wait(eng, "ev")

    def ln_finish(off):
        ssl = sb["stats"][:, off:off + MC]
        qsl = sb["stats2"][:, off:off + MC]
        a = sb["sc_a"][:, 0:MC]
        b = sb["sc_b"][:, 0:MC]
        vop(lambda e: e.drain(), None)
        vop(lambda e: e.tensor_scalar(ssl, ssl, 1.0 / D, None, OP.mult), None)
        vop(lambda e: e.drain(), None)
        vop(lambda e: e.tensor_tensor(a, ssl, ssl, OP.mult), None)
        vop(lambda e: e.tensor_scalar(qsl, qsl, 1.0 / D, None, OP.mult), None)
        vop(lambda e: e.drain(), None)
        vop(lambda e: e.tensor_tensor(qsl, qsl, a, OP.subtract), None)
        vop(lambda e: e.drain(), None)
        vop(lambda e: e.tensor_scalar(qsl, qsl, 1e-5, None, OP.add), "ev")
        em.wait("scalar", "ev")
        sop(lambda s: s.activation(a, qsl, AF.Ln), "act")
        sop(lambda s: s.activation(qsl, a, AF.Exp, scale=-0.5), "act")
        em.wait("vector", "act")

    def ln_stats(tile, off):
        em.wait("scalar", "ev")
        for m in range(MC):
            sl = tile[:, D * m:D * (m + 1)]
            vop(lambda e, sl=sl, m=m: e.tensor_reduce(
                sb["stats"][:, off + m:off + m + 1], sl, AX.X, OP.add), None)
            sop(lambda s, sl=sl, m=m: s.activation(
                sb["junk"][:], sl, AF.Square,
                accum_out=sb["stats2"][:, off + m:off + m + 1]), "act")
        em.wait("vector", "act")
        ln_finish(off)

    def ln_norm(src, dst, off):
        for m in range(MC):
            ssl = src[:, D * m:D * (m + 1)]
            dsl = dst[:, D * m:D * (m + 1)]
            mcol = sb["stats"][:, off + m:off + m + 1]
            rcol = sb["stats2"][:, off + m:off + m + 1]
            last = (m == MC - 1)
            (vopd if last else vop)(
                lambda e, ssl=ssl, dsl=dsl, mcol=mcol, rcol=rcol: e.tensor_scalar(
                    dsl, ssl, mcol, rcol, OP.subtract, OP.mult))

    def tm_to_fm(tm_tile, fm_tile, fstride=TT, foff=0):
        em.wait("tensor", "ev", em.cnt["ev"])
        fsz = fm_tile.shape[1]
        ngrp = MC * KC // 4
        evmark = [em.cnt["sv"]] * ngrp
        for grp in range(ngrp):
            m, k0 = grp // 2, (grp % 2) * 4
            bank = B[4 + (grp % 4)]
            em.wait("tensor", "sv", evmark[grp - 2] if grp >= 2 else evmark[0])
            for q in range(4):
                k = k0 + q
                pv = bank.ap().bitcast(BF16)[:, 128 * q:128 * q + 128]
                top(lambda t, m=m, k=k, pv=pv, tm=tm_tile: t.transpose(
                    pv, tm[:, D * m + 128 * k:D * m + 128 * k + 128],
                    sb["identb"][:]), "tr")
            trmark = em.cnt["tr"]
            pvfull = bank.ap().bitcast(BF16)[:, 0:512]
            dsl = bass.AP(fm_tile, foff + fstride * k0 + 128 * m,
                          [[fsz, 128], [fstride, 4], [1, 128]])
            em.wait("vector", "tr", trmark)
            vopd(lambda e, dsl=dsl, pvfull=pvfull: e.tensor_copy(dsl, pvfull),
                 "sv")
            evmark[grp] = em.cnt["sv"]

    x_d = dr["x"].ap()
    y_d = dr["y"].ap()
    FS = 64 * G

    def full_barrier():
        for eng in ("sync", "tensor", "vector", "scalar", "gpsimd"):
            for s in ["wdma", "xdma", "ydma", "bnc", "gth", "mm", "tr",
                      "ev", "sv", "gp", "act"]:
                em.wait(eng, s)

    def phase_attn(ti):
        if "attn" in barrier:
            full_barrier()
        x1cur = sb["x1a"] if ti % 2 == 0 else sb["x1b"]
        wcur = sb["xq1w_a"] if ti % 2 == 0 else sb["xq1w_b"]
        wprev = sb["xq1w_b"] if ti % 2 == 0 else sb["xq1w_a"]

        em.wait("sync", "ydma")
        em.wait("sync", "ev")
        for m in range(MC):
            dma("sync", sb["x_tm"][:, D * m:D * (m + 1)],
                x_d[ti * TT + 128 * m:ti * TT + 128 * (m + 1), :], "xdma")
        em.wait("vector", "xdma")
        em.wait("scalar", "xdma")

        ln_stats(sb["x_tm"], 0)
        ln_norm(sb["x_tm"], sb["tm_sc"], 0)
        em.wait("tensor", "ev")
        tm_to_fm(sb["tm_sc"], sb["fm_sc"])

        slot_mark = [em.cnt["mm"]] * 8
        bank_mark = [em.cnt["ev"]] * 4
        tmsc_mark = [em.cnt["bnc"]] * MC
        gath_marks = {}
        for pi, (cname, brep, do_elu, bname) in enumerate([
                ("c_wq", sb["bq_rep"], True, "b_fq"),
                ("c_wk", sb["bk_rep"], True, "b_fk"),
                ("c_wv", sb["bv_rep"], False, "b_v")]):
            ld_mark = []
            for k in range(KC):
                g = pi * KC + k
                em.wait("sync", "mm", slot_mark[g % 8])
                dma("sync", sb["wbuf"][:, D * (g % 8):D * (g % 8 + 1)],
                    dr[cname].ap()[128 * k:128 * k + 128, :], "wdma")
                ld_mark.append(em.cnt["wdma"])
            for k in range(KC):
                g = pi * KC + k
                em.wait("tensor", "wdma", ld_mark[k])
                if k == 0:
                    for b in range(4):
                        em.wait("tensor", "ev", bank_mark[b])
                    if pi == 0:
                        em.wait("tensor", "sv")
                for m in range(MC):
                    for h2 in range(2):
                        lhsT = sb["fm_sc"][:, TT * k + 128 * m:
                                           TT * k + 128 * m + 128]
                        top(lambda t, lhsT=lhsT, m=m, h2=h2, g=g, k=k: t.matmul(
                            B[2 * m + h2][:, :], lhsT,
                            sb["wbuf"][:, D * (g % 8) + 512 * h2:
                                       D * (g % 8) + 512 * (h2 + 1)],
                            start=(k == 0), stop=(k == KC - 1)))
            mm_done = em.cnt["mm"]
            for k in range(KC):
                slot_mark[(pi * KC + k) % 8] = mm_done
            for m in range(MC):
                em.wait("vector", "mm", mm_done)
                em.wait("vector", "bnc", tmsc_mark[m])
                for h2 in range(2):
                    psl = B[2 * m + h2][:, :]
                    osl = sb["tm_sc"][:, D * m + 512 * h2:D * m + 512 * (h2 + 1)]
                    rsl = brep[:, 512 * h2:512 * (h2 + 1)]
                    ju2 = sb["ju2"][:, 512 * h2:512 * h2 + 512]
                    ju3 = sb["ju3"][:, 512 * h2:512 * h2 + 512]
                    jnk = sb["junk"][:, 512 * h2:512 * h2 + 512]
                    if do_elu:
                        vopd(lambda e, psl=psl, rsl=rsl, ju2=ju2:
                             e.scalar_tensor_tensor(ju2, psl, 1.0, rsl,
                                                    OP.mult, OP.add), "ev")
                        em.wait("scalar", "ev", em.cnt["ev"])
                        sop(lambda s, ju2=ju2, jnk=jnk: s.activation(
                            jnk, ju2, AF.Exp), "act")
                        sopd(lambda s, ju2=ju2, ju3=ju3: s.activation(
                            ju3, ju2, AF.Relu), "act")
                        em.wait("vector", "act", em.cnt["act"])
                        vopd(lambda e, osl=osl, jnk=jnk, ju3=ju3:
                             e.scalar_tensor_tensor(osl, jnk, 1.0, ju3,
                                                    OP.min, OP.add), "ev")
                    else:
                        vopd(lambda e, psl=psl, rsl=rsl, osl=osl:
                             e.scalar_tensor_tensor(osl, psl, 1.0, rsl,
                                                    OP.mult, OP.add), "ev")
                bank_mark[m] = em.cnt["ev"]
                em.wait("scalar", "ev", em.cnt["ev"])
                if m == 0:
                    em.wait("scalar", "gth", em.cnt["gth"])
                dma("scalar", dr[bname].ap()[128 * m:128 * (m + 1), :],
                    sb["tm_sc"][:, D * m:D * (m + 1)], "bnc")
                tmsc_mark[m] = em.cnt["bnc"]
            em.wait("scalar", "bnc", em.cnt["bnc"])
            em.wait("scalar", "mm", em.cnt["mm"])
            tname = {"b_fq": "fq_g2", "b_fk": "fk_g2", "b_v": "v_g2"}[bname]
            for h in range(H):
                dma("scalar",
                    bass.AP(sb[tname], h * FS, [[16 * FS, 8], [64, G], [1, 64]]),
                    bass.AP(dr[bname], 64 * h, [[D, 8], [8 * D, G], [1, 64]]),
                    "gth")
            gath_marks[tname] = em.cnt["gth"]

        em.wait("tensor", "gth")
        b01_mark = em.cnt["sv"]
        s2e_mark = [em.cnt["sv"]] * 2
        st1_mark = em.cnt["mm"]
        ave_mark = em.cnt["ev"]
        mb4 = bass.AP(sb["maskS"], 0, [[128, 128], [0, 4], [1, 128]])
        for b8 in range(G // 8):
            sA = B[4] if b8 % 2 == 0 else B[6]
            sB = B[5] if b8 % 2 == 0 else B[7]
            pvq = B[0].ap().bitcast(BF16)
            pvk = B[1].ap().bitcast(BF16)
            em.wait("tensor", "sv", b01_mark)
            for jj in range(4):
                j = b8 * 4 + jj
                top(lambda t, j=j, jj=jj, pvq=pvq: t.transpose(
                    pvq[:, 128 * jj:128 * jj + 128],
                    sb["fq_g2"][:, 128 * j:128 * j + 128], sb["identb"][:]), "tr")
                top(lambda t, j=j, jj=jj, pvk=pvk: t.transpose(
                    pvk[:, 128 * jj:128 * jj + 128],
                    sb["fk_g2"][:, 128 * j:128 * j + 128], sb["identb"][:]), "tr")
            trm = em.cnt["tr"]
            em.wait("vector", "tr", trm)
            em.wait("vector", "mm", st1_mark)
            vopd(lambda e, pvq=pvq: e.tensor_copy(sb["fqT"][:, 0:512],
                                                  pvq[:, 0:512]), "sv")
            vopd(lambda e, pvk=pvk: e.tensor_copy(sb["fkT"][:, 0:512],
                                                  pvk[:, 0:512]), "sv")
            b01_mark = em.cnt["sv"]
            em.wait("tensor", "sv", b01_mark)
            em.wait("tensor", "sv", s2e_mark[b8 % 2])
            for jj in range(4):
                qs = sb["fqT"][:, 128 * jj:128 * jj + 128]
                ks = sb["fkT"][:, 128 * jj:128 * jj + 128]
                for dd in range(2):
                    bank = sA if dd == 0 else sB
                    top(lambda t, qs=qs, ks=ks, dd=dd, bank=bank, jj=jj:
                        t.matmul(bank[:, 128 * jj:128 * jj + 128],
                                 ks[64 * dd:64 * dd + 64, :],
                                 qs[64 * dd:64 * dd + 64, :],
                                 start=True, stop=True))
            st1_mark = em.cnt["mm"]
            em.wait("vector", "mm", st1_mark)
            vop(lambda e, sA=sA: e.tensor_tensor(
                sb["S2"][:, 0:512].rearrange("p (g c) -> p g c", g=4),
                sA[:].rearrange("p (g c) -> p g c", g=4), mb4, OP.mult), None)
            vopd(lambda e, sB=sB: e.tensor_tensor(
                sb["S2"][:, 512:1024].rearrange("p (g c) -> p g c", g=4),
                sB[:].rearrange("p (g c) -> p g c", g=4), mb4, OP.mult), "sv")
            s2e_mark[b8 % 2] = em.cnt["sv"]
            em.wait("tensor", "sv", s2e_mark[b8 % 2])
            em.wait("tensor", "ev", ave_mark)
            for gg in range(8):
                jj, dd = gg // 2, gg % 2
                gi = b8 * 8 + 2 * jj + dd
                s2sl = sb["S2"][:, 512 * dd + 128 * jj:512 * dd + 128 * jj + 128]
                top(lambda t, s2sl=s2sl, gi=gi: t.matmul(
                    B[3][:, gi:gi + 1], s2sl, sb["onesb"][:],
                    start=True, stop=True))
                top(lambda t, s2sl=s2sl, gi=gi, gg=gg: t.matmul(
                    B[2][:, 64 * gg:64 * gg + 64], s2sl,
                    sb["v_g2"][:, 64 * gi:64 * gi + 64], start=True, stop=True))
            st2m = em.cnt["mm"]
            em.wait("vector", "mm", st2m)
            rsl = sb["rc_c"][:, 8 * b8:8 * b8 + 8]
            vop(lambda e, b8=b8: e.tensor_scalar(
                sb["sc_c"][:, 0:8], B[3][:, 8 * b8:8 * b8 + 8], 1e-6, None,
                OP.add), None)
            vop(lambda e: e.drain(), None)
            vop(lambda e, rsl=rsl: e.reciprocal(rsl, sb["sc_c"][:, 0:8]), None)
            vop(lambda e: e.drain(), None)
            rbc = bass.AP(sb["rc_c"], 8 * b8, [[G, 128], [1, 8], [0, 64]])
            vopd(lambda e, b8=b8, rbc=rbc: e.tensor_tensor(
                sb["attn_g2"][:, 512 * b8:512 * (b8 + 1)].rearrange(
                    "p (g e) -> p g e", g=8),
                B[2][:].rearrange("p (g e) -> p g e", g=8), rbc, OP.mult), "ev")
            ave_mark = em.cnt["ev"]

        em.wait("scalar", "ev")
        em.wait("scalar", "gth", em.cnt["gth"])
        dma("scalar", dr["b_at"].ap(), sb["attn_g2"][:], "bnc")
        em.wait("scalar", "bnc")
        for gi in range(G):
            m = (8 * gi) // 128
            r0 = (8 * gi) % 128
            out_ap = bass.AP(sb["tm_sc"], r0 * (MC * D) + D * m,
                             [[MC * D, 8], [64, 16], [1, 64]])
            in_ap = bass.AP(dr["b_at"], 64 * gi, [[16 * FS, 8], [FS, 16], [1, 64]])
            dma("scalar", out_ap, in_ap, "gth")
        em.wait("tensor", "gth")
        em.wait("vector", "gth")
        em.wait("scalar", "gth")
        tm_to_fm(sb["tm_sc"], sb["fm_sc"])

        em.wait("sync", "mm")
        for k in range(KC):
            dma("sync", sb["wbuf"][:, D * k:D * (k + 1)],
                dr["c_wo"].ap()[128 * k:128 * k + 128, :], "wdma")
        em.wait("tensor", "wdma")
        em.wait("tensor", "sv")
        for m in range(MC):
            pb = (B[2 * (m % 2)], B[2 * (m % 2) + 1])
            em.wait("tensor", "ev")
            for k in range(KC):
                lhsT = sb["fm_sc"][:, TT * k + 128 * m:TT * k + 128 * m + 128]
                for h2 in range(2):
                    top(lambda t, lhsT=lhsT, k=k, h2=h2, pb=pb: t.matmul(
                        pb[h2][:, :], lhsT,
                        sb["wbuf"][:, D * k + 512 * h2:D * k + 512 * (h2 + 1)],
                        start=(k == 0), stop=(k == KC - 1)))
            em.wait("vector", "mm")
            for h2 in range(2):
                psl = pb[h2][:, :]
                bsl = sb["bo_rep"][:, 512 * h2:512 * (h2 + 1)]
                xsl = sb["x_tm"][:, D * m + 512 * h2:D * m + 512 * (h2 + 1)]
                osl = sb["xmid_tm"][:, D * m + 512 * h2:D * m + 512 * (h2 + 1)]
                vop(lambda e, psl=psl, bsl=bsl: e.scalar_tensor_tensor(
                    sb["ju2"][:, 0:512], psl, 1.0, bsl, OP.mult, OP.add), None)
                vopd(lambda e, osl=osl, xsl=xsl: e.tensor_tensor(
                    osl, sb["ju2"][:, 0:512], xsl, OP.add), "ev")

        ln_stats(sb["xmid_tm"], 4)
        ln_norm(sb["xmid_tm"], sb["tm_sc"], 4)
        em.wait("tensor", "ev")
        tm_to_fm(sb["tm_sc"], sb["fm_sc"])

        em.wait("tensor", "sv")
        gelu_mark = [em.cnt["act"]] * 2
        fcslice_mark = [em.cnt["mm"]] * 2
        pjslice_mark = [em.cnt["mm"]] * 2
        fcev = [em.cnt["ev"]] * 2
        for half in range(2):
            projmm_marks = {}
            pend_proj = None

            def emit_proj(pp):
                gsl, wpsl, gm, pjw_ld, c = pp
                em.wait("tensor", "wdma", pjw_ld)
                em.wait("tensor", "act", gm)
                for mloc in range(2):
                    for h2 in range(2):
                        top(lambda t, gsl=gsl, mloc=mloc, h2=h2, wpsl=wpsl, c=c:
                            t.matmul(B[2 * mloc + h2][:, :],
                                     gsl[:, 128 * mloc:128 * mloc + 128],
                                     wpsl[:, 512 * h2:512 * (h2 + 1)],
                                     start=(c == 0), stop=(c == CC - 1)))
                projmm_marks[c] = em.cnt["mm"]
                pjslice_mark[c % 2] = em.cnt["mm"]

            for c in range(CC):
                wsl = sb["wslc2"][:, D * (c % 2):D * (c % 2) + D]
                em.wait("sync", "mm", fcslice_mark[c % 2])
                dma("sync", wsl,
                    bass.AP(dr["c_fcw"], c * KC * 16384,
                            [[128, 128], [16384, KC], [1, 128]]), "wdma")
                fcw_ld = em.cnt["wdma"]
                em.wait("sync", "wdma", fcw_ld)
                wpsl = sb["wslc"][:, D * (c % 2):D * (c % 2) + D]
                em.wait("sync", "mm", pjslice_mark[c % 2])
                dma("sync", wpsl, dr["c_projw"].ap()[128 * c:128 * c + 128, :],
                    "wdma")
                em.wait("sync", "wdma", em.cnt["wdma"])
                pjw_ld = em.cnt["wdma"]
                fcbank = B[4 + (c % 2)]
                em.wait("tensor", "wdma", fcw_ld)
                em.wait("tensor", "ev", fcev[c % 2])
                for k in range(KC):
                    rhs = sb["fm_sc"][:, TT * k + 256 * half:
                                      TT * k + 256 * half + 256]
                    top(lambda t, wsl=wsl, k=k, rhs=rhs, fcbank=fcbank: t.matmul(
                        fcbank[:, 0:256], wsl[:, 128 * k:128 * k + 128], rhs,
                        start=(k == 0), stop=(k == KC - 1)))
                fcmm = em.cnt["mm"]
                fcslice_mark[c % 2] = fcmm
                gssl = sb["ju3"][:, 256 * (c % 2):256 * (c % 2) + 256]
                em.wait("vector", "mm", fcmm)
                em.wait("vector", "act", gelu_mark[c % 2])
                vopd(lambda e, gssl=gssl, fcbank=fcbank: e.tensor_copy(
                    gssl, fcbank[:, 0:256]), "ev")
                fcev[c % 2] = em.cnt["ev"]
                gsl = sb["gch"][:, 256 * (c % 2):256 * (c % 2) + 256]
                em.wait("scalar", "ev", fcev[c % 2])
                if c >= 2:
                    em.wait("scalar", "mm", projmm_marks[c - 2])
                sopd(lambda s, gsl=gsl, gssl=gssl, c=c: s.activation(
                    gsl, gssl, AF.Gelu,
                    bias=sb["fcb_col"][:, c:c + 1], scale=1.0), "act")
                gelu_mark[c % 2] = em.cnt["act"]
                if pend_proj is not None:
                    emit_proj(pend_proj)
                pend_proj = (gsl, wpsl, gelu_mark[c % 2], pjw_ld, c)
            emit_proj(pend_proj)
            em.wait("vector", "mm", em.cnt["mm"])
            for mloc in range(2):
                m = 2 * half + mloc
                for h2 in range(2):
                    psl = B[2 * mloc + h2][:, :]
                    bsl = sb["projb_rep"][:, 512 * h2:512 * (h2 + 1)]
                    midsl = sb["xmid_tm"][:, D * m + 512 * h2:D * m + 512 * (h2 + 1)]
                    xsl = sb["x_tm"][:, D * m + 512 * h2:D * m + 512 * (h2 + 1)]
                    vop(lambda e, psl=psl, bsl=bsl: e.scalar_tensor_tensor(
                        sb["ju2"][:, 0:512], psl, 1.0, bsl, OP.mult, OP.add), None)
                    vop(lambda e, midsl=midsl: e.tensor_tensor(
                        sb["ju3"][:, 0:512], sb["ju2"][:, 0:512], midsl, OP.add),
                        None)
                    vopd(lambda e, xsl=xsl: e.tensor_tensor(
                        xsl, sb["ju3"][:, 0:512], xsl, OP.add), "ev")
            em.wait("tensor", "ev", em.cnt["ev"])

        ln_stats(sb["x_tm"], 0)
        ln_norm(sb["x_tm"], x1cur, 0)

        for m in range(MC):
            sl = x1cur[:, D * m:D * (m + 1)]
            vop(lambda e, sl=sl, m=m: e.tensor_reduce(
                sb["qscale"][:, m:m + 1], sl, AX.X, OP.max,
                apply_absolute_value=True), None)
        vop(lambda e: e.drain(), None)
        vop(lambda e: e.tensor_scalar(sb["qscale"][:, 0:MC], sb["qscale"][:, 0:MC],
                                      1e-5, None, OP.max), None)
        vop(lambda e: e.drain(), None)
        vop(lambda e: e.reciprocal(sb["sc_c"][:, 0:MC], sb["qscale"][:, 0:MC]), None)
        vop(lambda e: e.tensor_scalar(sb["qscale"][:, 4:4 + MC],
                                      sb["qscale"][:, 0:MC], 1.0 / 127.0, None,
                                      OP.mult), None)
        vop(lambda e: e.drain(), None)
        vop(lambda e: e.tensor_scalar(sb["qscale"][:, 0:MC], sb["sc_c"][:, 0:MC],
                                      127.0, None, OP.mult), None)
        vop(lambda e: e.drain(), None)
        for m in range(MC):
            ssl = x1cur[:, D * m:D * (m + 1)]
            scol = sb["qscale"][:, m:m + 1]
            dcol = sb["qscale"][:, 4 + m:5 + m]
            osl = sb["tm_sc"][:, D * m:D * (m + 1)]
            vop(lambda e, ssl=ssl, scol=scol: e.tensor_scalar(
                sb["junk"][:], ssl, scol, MAGIC, OP.mult, OP.add), None)
            vopd(lambda e, osl=osl, dcol=dcol: e.tensor_scalar(
                osl, sb["junk"][:], MAGIC, dcol, OP.subtract, OP.mult), "ev")
        em.wait("tensor", "ev")
        tm_to_fm(sb["tm_sc"], wcur, fstride=WIN, foff=1)
        em.wait("vector", "sv")
        for k in range(KC):
            if ti > 0:
                vop(lambda e, k=k, wc=wcur, wp=wprev: e.tensor_copy(
                    wc[:, WIN * k:WIN * k + 1], wp[:, WIN * k + TT:WIN * k + TT + 1]),
                    None)
            (vopd if k == KC - 1 else vop)(
                lambda e, k=k, wc=wcur, wp=wprev: e.tensor_copy(
                    wp[:, WIN * k + TT + 1:WIN * k + TT + 2],
                    wc[:, WIN * k + 1:WIN * k + 2]), "ev")

    def phase_ffn(tj, last=False):
        if "ffn" in barrier:
            full_barrier()
        x1p = sb["x1a"] if tj % 2 == 0 else sb["x1b"]
        wp = sb["xq1w_a"] if tj % 2 == 0 else sb["xq1w_b"]
        HT = TT // 2
        if last:
            for k in range(KC):
                (vopd if k == KC - 1 else vop)(lambda e, k=k, wp=wp: e.memset(
                    wp[:, WIN * k + TT + 1:WIN * k + TT + 2], 0.0), "ev")
            em.wait("tensor", "ev")
        em.wait("tensor", "sv")
        for half in range(2):
            vop(lambda e: e.memset(sb["rmax"][:], 0.0), None)
            vop(lambda e: e.memset(sb["rmin"][:], 0.0), "ev")
            w1slice_mark = [em.cnt["mm"]] * 2
            hbank_mark = [em.cnt["ev"]] * 2
            convslot_mark = [em.cnt["act"]] * 2
            w1mm_marks = []
            for c in range(CC):
                wsl = sb["wslc2"][:, D * (c % 2):D * (c % 2) + D]
                em.wait("sync", "mm", w1slice_mark[c % 2])
                dma("sync", wsl,
                    bass.AP(dr["c_w1t"], c * KC * 16384,
                            [[128, 128], [16384, KC], [1, 128]]), "wdma")
                ldm = em.cnt["wdma"]
                bA = B[4 + (c % 2)]
                em.wait("tensor", "wdma", ldm)
                em.wait("tensor", "ev", hbank_mark[c % 2])
                for k in range(KC):
                    lh = wsl[:, 128 * k:128 * k + 128]
                    top(lambda t, lh=lh, bA=bA, wp=wp, k=k, half=half: t.matmul(
                        bA[:, 0:258], lh,
                        wp[:, WIN * k + HT * half:WIN * k + HT * half + 258],
                        start=(k == 0), stop=(k == KC - 1)))
                w1mm_marks.append(em.cnt["mm"])
                w1slice_mark[c % 2] = em.cnt["mm"]
                hw = sb["hwin"][:, 260 * (c % 2):260 * (c % 2) + 258]
                cva = sb["cv1"][:, 256 * (c % 2):256 * (c % 2) + 256]
                cvb = sb["cv2"][:, 256 * (c % 2):256 * (c % 2) + 256]
                em.wait("vector", "mm", w1mm_marks[-1])
                em.wait("vector", "act", convslot_mark[c % 2])
                vopd(lambda e, bA=bA, hw=hw: e.tensor_copy(hw, bA[:, 0:258]),
                     "ev")
                hbank_mark[c % 2] = em.cnt["ev"]
                w0c = sb["convw_col"][:, c:c + 1]
                w1c = sb["convw_col"][:, CC + c:CC + c + 1]
                w2c = sb["convw_col"][:, 2 * CC + c:2 * CC + c + 1]
                vop(lambda e, w1c=w1c, hw=hw, cva=cva: e.tensor_scalar(
                    cva, hw[:, 1:257], w1c, None, OP.mult), None)
                vop(lambda e, w0c=w0c, hw=hw, cva=cva, cvb=cvb:
                    e.scalar_tensor_tensor(cvb, hw[:, 0:256], w0c, cva,
                                           OP.mult, OP.add), None)
                vopd(lambda e, w2c=w2c, hw=hw, cva=cva, cvb=cvb:
                     e.scalar_tensor_tensor(cva, hw[:, 2:258], w2c, cvb,
                                            OP.mult, OP.add), "ev")
                cvm = em.cnt["ev"]
                em.wait("scalar", "ev", cvm)
                hs = sb["hsil"][:, HT * c:HT * (c + 1)]
                sopd(lambda s, hs=hs, c=c, cva=cva: s.activation(
                    hs, cva, AF.Silu, bias=sb["convb_col"][:, c:c + 1],
                    scale=1.0), "act")
                convslot_mark[c % 2] = em.cnt["act"]
                em.wait("vector", "act", em.cnt["act"])
                vop(lambda e, hs=hs: e.tensor_tensor(sb["rmax"][:], sb["rmax"][:],
                                                     hs, OP.max), None)
                vop(lambda e, hs=hs: e.tensor_tensor(sb["rmin"][:], sb["rmin"][:],
                                                     hs, OP.min), "ev")

            em.wait("tensor", "ev", em.cnt["ev"])
            for mloc in range(2):
                m = 2 * half + mloc
                pvA = B[6].ap().bitcast(BF16)[:, 0:128]
                pvB = B[6].ap().bitcast(BF16)[:, 128:256]
                top(lambda t, mloc=mloc, pvA=pvA: t.transpose(
                    pvA, sb["rmax"][:, 128 * mloc:128 * mloc + 128],
                    sb["identb"][:]), "tr")
                top(lambda t, mloc=mloc, pvB=pvB: t.transpose(
                    pvB, sb["rmin"][:, 128 * mloc:128 * mloc + 128],
                    sb["identb"][:]), "tr")
                em.wait("vector", "tr")
                vop(lambda e, pvA=pvA: e.tensor_reduce(
                    sb["sc_a"][:, 0:1], pvA, AX.X, OP.max,
                    apply_absolute_value=True), None)
                vopd(lambda e, pvB=pvB: e.tensor_reduce(
                    sb["sc_a"][:, 1:2], pvB, AX.X, OP.max,
                    apply_absolute_value=True), "ev")
                vopd(lambda e, m=m: e.tensor_tensor(
                    sb["q2s"][:, m:m + 1], sb["sc_a"][:, 0:1], sb["sc_a"][:, 1:2],
                    OP.max), "sv")
                em.wait("tensor", "sv")
            c0 = 2 * half
            vop(lambda e: e.drain(), None)
            vop(lambda e, c0=c0: e.tensor_scalar(
                sb["q2s"][:, c0:c0 + 2], sb["q2s"][:, c0:c0 + 2], 1e-5, None,
                OP.max), None)
            vop(lambda e: e.drain(), None)
            vop(lambda e, c0=c0: e.reciprocal(sb["sc_c"][:, 0:2],
                                              sb["q2s"][:, c0:c0 + 2]), None)
            vop(lambda e, c0=c0: e.tensor_scalar(
                sb["q2s"][:, 4 + c0:6 + c0], sb["q2s"][:, c0:c0 + 2],
                1.0 / 127.0, None, OP.mult), None)
            vop(lambda e: e.drain(), None)
            vop(lambda e, c0=c0: e.tensor_tensor(
                sb["q2s"][:, 4 + c0:6 + c0], sb["q2s"][:, 4 + c0:6 + c0],
                sb["mbar"][:, 1:2].broadcast_to([128, 2]), OP.mult), None)
            vopd(lambda e, c0=c0: e.tensor_scalar(
                sb["q2s"][:, c0:c0 + 2], sb["sc_c"][:, 0:2], 127.0, None,
                OP.mult), "ev")
            em.wait("tensor", "ev")
            for mloc in range(2):
                m = 2 * half + mloc
                top(lambda t, m=m, mloc=mloc: t.transpose(
                    B[7][0:1, 128 * mloc:128 * mloc + 128], sb["q2s"][:, m:m + 1],
                    sb["identf"][:]), "tr")
            em.wait("vector", "tr")
            vopd(lambda e: e.tensor_copy(sb["s2row"][:], B[7][0:1, 0:HT]), "sv")
            em.wait("tensor", "sv")
            top(lambda t: t.matmul(B[7][:, 0:HT], sb["onesrow"][:], sb["s2row"][:],
                                   start=True, stop=True))
            em.wait("vector", "mm")
            vopd(lambda e: e.tensor_copy(sb["s2bc"][:], B[7][:, 0:HT]), "ev")

            em.wait("tensor", "ev", em.cnt["ev"])
            w2slice_mark = [em.cnt["mm"]] * 2
            hq_mark = [em.cnt["mm"]] * 2
            for c in range(CC):
                hs = sb["hsil"][:, HT * c:HT * (c + 1)]
                hq = sb["hq_c"][:, HT * (c % 2):HT * (c % 2) + HT]
                em.wait("vector", "mm", hq_mark[c % 2])
                vop(lambda e, hs=hs: e.tensor_tensor(
                    sb["ju2"][:, 0:HT], hs, sb["s2bc"][:], OP.mult), None)
                vopd(lambda e, hq=hq: e.tensor_scalar(
                    hq, sb["ju2"][:, 0:HT], MAGIC, MAGIC, OP.add, OP.subtract),
                    "ev")
                rndm = em.cnt["ev"]
                wsl = sb["wslc"][:, D * (c % 2):D * (c % 2) + D]
                em.wait("sync", "mm", w2slice_mark[c % 2])
                dma("sync", wsl, dr["c_w2t"].ap()[128 * c:128 * c + 128, :], "wdma")
                ldm = em.cnt["wdma"]
                em.wait("tensor", "wdma", ldm)
                em.wait("tensor", "ev", rndm)
                for mloc in range(2):
                    for h2 in range(2):
                        top(lambda t, hq=hq, mloc=mloc, h2=h2, wsl=wsl, c=c:
                            t.matmul(B[2 * mloc + h2][:, :],
                                     hq[:, 128 * mloc:128 * mloc + 128],
                                     wsl[:, 512 * h2:512 * (h2 + 1)],
                                     start=(c == 0), stop=(c == CC - 1)))
                hq_mark[c % 2] = em.cnt["mm"]
                w2slice_mark[c % 2] = em.cnt["mm"]
            em.wait("vector", "mm")
            for mloc in range(2):
                m = 2 * half + mloc
                for h2 in range(2):
                    psl = B[2 * mloc + h2][:, :]
                    x1sl = x1p[:, D * m + 512 * h2:D * m + 512 * (h2 + 1)]
                    osl = sb["xmid_tm"][:, D * m + 512 * h2:D * m + 512 * (h2 + 1)]
                    dcol = sb["q2s"][:, 4 + m:5 + m]
                    vopd(lambda e, psl=psl, x1sl=x1sl, osl=osl, dcol=dcol:
                         e.scalar_tensor_tensor(osl, psl, dcol, x1sl,
                                                OP.mult, OP.add), "ev")
            em.wait("tensor", "ev")

        ln_stats(sb["xmid_tm"], 8)
        ln_norm(sb["xmid_tm"], sb["x_tm"], 8)
        em.wait("sync", "ev")
        for m in range(MC):
            dma("sync", y_d[tj * TT + 128 * m:tj * TT + 128 * (m + 1), :],
                sb["x_tm"][:, D * m:D * (m + 1)], "ydma")

    with nc.Block() as block:
        for ti in range(nt_limit):
            phase_attn(ti)
            if ti > 0 and do_ffn:
                phase_ffn(ti - 1)
        if do_ffn:
            phase_ffn(nt_limit - 1, last=True)
        if dump:
            for eng in ("sync", "tensor", "vector", "scalar", "gpsimd"):
                for s in ["wdma", "xdma", "ydma", "bnc", "gth", "mm", "tr",
                          "ev", "sv", "gp", "act"]:
                    em.wait(eng, s)
            for nm in dump:
                t = sb[nm]
                shp = list(t.shape)
                dbg = nc.dram_tensor("dbg_" + nm, shp, t.dtype,
                                     kind="ExternalOutput")
                dma("sync", dbg.ap(), t[:], "ydma")
        em.wait("sync", "ydma")
        em.wait("gpsimd", "ydma")

        @block.sync
        def _(e):
            for fn in em.lists["sync"]:
                fn(e)

        @block.tensor
        def _(e):
            for fn in em.lists["tensor"]:
                fn(e)

        @block.vector
        def _(e):
            for fn in em.lists["vector"]:
                fn(e)

        @block.scalar
        def _(e):
            for fn in em.lists["scalar"]:
                fn(e)

        @block.gpsimd
        def _(e):
            for fn in em.lists["gpsimd"]:
                fn(e)

    es.close()
    return nc


_NC_CACHE = None


def _get_nc():
    global _NC_CACHE
    if _NC_CACHE is None:
        _NC_CACHE = build_program()
    return _NC_CACHE


def kernel(**inputs):
    ii = {k: np.asarray(v) for k, v in inputs.items()}
    x = np.ascontiguousarray(ii["L0_attn_q_x"].astype(np.float32, copy=False))
    shared = {
        "wq": ii["L0_attn_wq"], "wk": ii["L0_attn_wk"], "wv": ii["L0_attn_wv"],
        "wo": ii["L0_attn_wo"], "fcw": ii["L0_attn_fc_w"],
        "projw": ii["L0_attn_proj_w"],
        "w1T": np.ascontiguousarray(ii["L0_ffn_w1"].T),
        "w2T": np.ascontiguousarray(ii["L0_ffn_w2"].T),
        "ln1w": ii["L0_attn_ln1_w"], "ln1b": ii["L0_attn_ln1_b"],
        "ln2w": ii["L0_attn_ln2_w"], "ln2b": ii["L0_attn_ln2_b"],
        "bq": ii["L0_attn_bq"], "bk": ii["L0_attn_bk"], "bv": ii["L0_attn_bv"],
        "bo": ii["L0_attn_bo"], "fcb": ii["L0_attn_fc_b"],
        "projb": ii["L0_attn_proj_b"],
        "convw0": np.ascontiguousarray(ii["L0_ffn_conv_w"][:, 0, 0]),
        "convw1": np.ascontiguousarray(ii["L0_ffn_conv_w"][:, 0, 1]),
        "convw2": np.ascontiguousarray(ii["L0_ffn_conv_w"][:, 0, 2]),
        "identb": np.eye(128, dtype=ml_dtypes.bfloat16),
        "identf": np.eye(128, dtype=np.float32),
        "maskS": np.kron(np.eye(8, dtype=np.float32), np.ones((16, 16), np.float32)),
        "convb": ii["L0_ffn_conv_b"],
    }
    shared = {k: (np.ascontiguousarray(v.astype(np.float32, copy=False))
                  if v.dtype != ml_dtypes.bfloat16 else v)
              for k, v in shared.items()}
    nc = _get_nc()
    in_maps = [dict(shared, x=np.ascontiguousarray(x[c])) for c in range(NCORES)]
    res = run_bass_kernel_spmd(nc, in_maps, core_ids=list(range(NCORES)))
    out = np.stack([res.results[c]["y"] for c in range(NCORES)], axis=0)
    return out.astype(np.float32, copy=False)


if __name__ == "__main__":
    nc = build_program()
    print("built")


# version 17
# speedup vs baseline: 1.5761x; 1.0219x over previous
SERIAL_FC = False
"""Fused megakernel for mk_seq_postln_d1 (linear-attention block + BitNet FFN).

One raw-Bass program per NeuronCore, launched exactly once via
run_bass_kernel_spmd.  Batch (8) sharded across the 8 cores; each core runs
the entire forward for its own 4096-token sequence.

Structure:
  PROLOGUE  - build per-core bf16 weight caches in HBM (LN gains folded into
              weights, biases pre-folded via on-device matmuls, ternary
              quantization of w1/w2 done in fp32), load constants.
  MAIN LOOP - 8 token tiles of 512.  Per tile: LN1 -> q/k/v (token-major) ->
              elu+1 -> DRAM bounce -> grouped gathers -> 8-token-group
              block-diagonal attention matmuls -> wo -> LN2 -> fc/gelu/proj
              -> post-LN -> act-quant -> transposed window.  The BitNet FFN
              for tile i runs during iteration i+1 (the depthwise conv needs
              one token of lookahead).

Layouts: TM = [128 tokens x feat] (chunk-major), FM = [128 feat x tokens],
g2 = [128 = 16n+h x 64d] 8-token attention groups.
"""
import numpy as np
import ml_dtypes
import concourse.bass as bass
import concourse.mybir as mybir
from concourse.bass_utils import run_bass_kernel_spmd
from contextlib import ExitStack

F32 = mybir.dt.float32
BF16 = mybir.dt.bfloat16
I32 = mybir.dt.int32
OP = mybir.AluOpType
AF = mybir.ActivationFunctionType
AX = mybir.AxisListType

D = 1024
H = 16
MLP = 4096
N = 4096
TT = 512
NT = N // TT
MC = TT // 128
KC = D // 128
CC = MLP // 128
G = TT // 8
WIN = TT + 2
MAGIC = float(1.5 * 2 ** 23)
RSQRT_MAGIC = 0x5f3759df
NCORES = 8


class Emitter:
    def __init__(self):
        self.lists = {e: [] for e in ("sync", "tensor", "vector", "scalar", "gpsimd")}
        self.sems = {}
        self.cnt = {}

    def op(self, eng, fn, inc=None, n=1):
        if inc is not None:
            self.cnt[inc] += n
            sem = self.sems[inc]
            self.lists[eng].append(lambda e, fn=fn, sem=sem, n=n: fn(e).then_inc(sem, n))
        else:
            self.lists[eng].append(lambda e, fn=fn: fn(e))

    def wait(self, eng, name, val=None):
        v = self.cnt[name] if val is None else val
        sem = self.sems[name]
        self.lists[eng].append(lambda e, sem=sem, v=v: e.wait_ge(sem, v))


def build_program(nt_limit=NT, do_ffn=True, dump=(), skip=(), serial_fc=False,
                  barrier=()):
    global SERIAL_FC
    SERIAL_FC = int(serial_fc)
    nc = bass.Bass("TRN2", target_bir_lowering=False)
    es = ExitStack()
    E = es.enter_context

    dr = {}

    def dt(name, shape, dtype, kind="Internal"):
        dr[name] = nc.dram_tensor(name, shape, dtype, kind=kind)
        return dr[name]

    dt("x", [N, D], F32, "ExternalInput")
    for nm, shp in [("wq", [D, D]), ("wk", [D, D]), ("wv", [D, D]), ("wo", [D, D]),
                    ("fcw", [D, MLP]), ("projw", [MLP, D]),
                    ("w1T", [D, MLP]), ("w2T", [MLP, D]),
                    ("ln1w", [D]), ("ln1b", [D]), ("ln2w", [D]), ("ln2b", [D]),
                    ("bq", [D]), ("bk", [D]), ("bv", [D]), ("bo", [D]),
                    ("fcb", [MLP]), ("projb", [D]),
                    ("convw0", [MLP]), ("convw1", [MLP]), ("convw2", [MLP]),
                    ("convb", [MLP])]:
        dt(nm, shp, F32, "ExternalInput")
    dt("identb", [128, 128], BF16, "ExternalInput")
    dt("identf", [128, 128], F32, "ExternalInput")
    dt("maskS", [128, 128], F32, "ExternalInput")
    dt("y", [N, D], F32, "ExternalOutput")

    dt("c_wq", [D, D], BF16)
    dt("c_wk", [D, D], BF16)
    dt("c_wv", [D, D], BF16)
    dt("c_wo", [D, D], BF16)
    dt("c_fcw", [KC * CC * 128 * 128], BF16)
    dt("c_projw", [MLP, D], BF16)
    dt("c_w1t", [KC * CC * 128 * 128], BF16)
    dt("c_w2t", [MLP, D], BF16)
    dt("b_fq", [TT, D], BF16)
    dt("b_fk", [TT, D], BF16)
    dt("b_v", [TT, D], BF16)
    dt("b_at", [128, 64 * G], BF16)
    dt("b_row", [D], F32)

    sb = {}

    def st(name, shape, dtype):
        sb[name] = E(nc.sbuf_tensor("s_" + name, shape, dtype))
        return sb[name]

    st("identb", [128, 128], BF16)
    st("identf", [128, 128], F32)
    st("maskS", [128, 128], F32)
    st("onesb", [128, 1], BF16)
    st("onesf", [128, 1], F32)
    st("onesrow", [1, 128], F32)
    st("magici", [128, 16], I32)
    st("lnw_cols", [128, 2 * KC], F32)
    st("ln1b_col", [128, KC], BF16)
    st("ln2b_col", [128, KC], BF16)
    st("bias_sc", [128, KC], F32)
    st("fcb_col", [128, CC], F32)
    st("convw_col", [128, 3 * CC], F32)
    st("convb_col", [128, CC], F32)
    st("mbar", [128, 4], F32)
    st("bq_rep", [128, D], BF16)
    st("bk_rep", [128, D], BF16)
    st("bv_rep", [128, D], BF16)
    st("bo_rep", [128, D], BF16)
    st("projb_rep", [128, D], BF16)
    st("wbuf", [128, KC * D], BF16)
    st("wslc", [128, 2 * D], BF16)
    st("wslc2", [128, 2 * D], BF16)
    st("x_tm", [128, MC * D], F32)
    st("xmid_tm", [128, MC * D], F32)
    st("x1a", [128, MC * D], F32)
    st("x1b", [128, MC * D], F32)
    st("tm_sc", [128, MC * D], BF16)
    st("fm_sc", [128, KC * TT], BF16)
    st("stats", [128, 16], F32)
    st("stats2", [128, 16], F32)
    st("sc_a", [128, 16], F32)
    st("sc_b", [128, 16], F32)
    st("sc_c", [128, 16], F32)
    st("qscale", [128, 8], F32)
    st("q2s", [128, 8], F32)
    st("dbgk", [128, 256], F32)
    st("junk", [128, D], F32)
    st("ju2", [128, 1024], F32)
    st("ju3", [128, 1024], F32)

    B = {}
    for i in range(8):
        B[i] = E(nc.psum_tensor(f"bank{i}", [128, 512], F32))

    em = Emitter()
    for s in ["ldc", "wdma", "xdma", "ydma", "bnc", "gth", "mm", "tr", "ev",
              "sv", "gp", "act"]:
        em.sems[s] = E(nc.semaphore(s))
        em.cnt[s] = 0

    def vop(fn, inc="ev"):
        em.op("vector", fn, inc)

    def vopd(fn, inc="ev"):
        em.op("vector", fn, None)
        em.op("vector", lambda e: e.drain(), inc)

    def sop(fn, inc="act"):
        em.op("scalar", fn, inc)

    def sopd(fn, inc="act"):
        em.op("scalar", fn, None)
        em.op("scalar", lambda e: e.drain(), inc)

    def gop(fn, inc="gp"):
        em.op("gpsimd", fn, inc)

    def top(fn, inc="mm"):
        em.op("tensor", fn, inc)

    def dma(eng, out_ap, in_ap, inc):
        em.op(eng, lambda e, o=out_ap, i=in_ap: e.dma_start(out=o, in_=i), inc, 16)

    def dmas(eng, out_ap, in_ap, inc):
        em.op(eng, lambda e, o=out_ap, i=in_ap: e.dma_start(
            out=o, in_=i, allow_slow_non_contiguous=True), inc, 16)

    pro_es = ExitStack()
    sb["wf32b"] = pro_es.enter_context(nc.sbuf_tensor("s_wf32b", [128, MLP], F32))
    sb["wbf"] = pro_es.enter_context(nc.sbuf_tensor("s_wbf", [128, 2 * MLP], BF16))

    for nm in ["identb", "identf", "maskS"]:
        dma("sync", sb[nm][:], dr[nm].ap(), "ldc")
    for j, nm in enumerate(["ln1w", "ln2w"]):
        dmas("sync", bass.AP(sb["lnw_cols"], j * KC, [[2 * KC, 128], [1, KC]]),
            bass.AP(dr[nm], 0, [[1, 128], [128, KC]]), "ldc")
    dmas("sync", bass.AP(sb["convb_col"], 0, [[CC, 128], [1, CC]]),
        bass.AP(dr["convb"], 0, [[1, 128], [128, CC]]), "ldc")
    dmas("gpsimd", sb["bo_rep"][:], bass.AP(dr["bo"], 0, [[0, 128], [1, D]]), "ldc")
    dmas("gpsimd", sb["projb_rep"][:], bass.AP(dr["projb"], 0, [[0, 128], [1, D]]), "ldc")
    dmas("gpsimd", sb["ln1b_col"][:], bass.AP(dr["ln1b"], 0, [[1, 128], [128, KC]]), "ldc")
    dmas("gpsimd", sb["ln2b_col"][:], bass.AP(dr["ln2b"], 0, [[1, 128], [128, KC]]), "ldc")
    em.wait("vector", "ldc")
    vop(lambda e: e.memset(sb["onesb"][:], 1.0), None)
    vop(lambda e: e.memset(sb["onesf"][:], 1.0), None)
    vop(lambda e: e.memset(sb["magici"][:], RSQRT_MAGIC), None)
    vop(lambda e: e.memset(sb["onesrow"][:], 1.0))

    def build_attn_cache(wname, cname, borig, brep):
        dmas("sync", bass.AP(sb["bias_sc"], 0, [[KC, 128], [1, KC]]),
             bass.AP(dr[borig], 0, [[1, 128], [128, KC]]), "wdma")
        em.wait("vector", "wdma")
        for k in range(KC):
            dma("sync", sb["wf32b"][:, 0:D], dr[wname].ap()[128 * k:128 * k + 128, :],
                "wdma")
            em.wait("vector", "wdma")
            lcol = sb["lnw_cols"][:, k:k + 1]
            vop(lambda e, lcol=lcol: e.tensor_scalar(
                sb["wbf"][:, 0:D], sb["wf32b"][:, 0:D], lcol, None, OP.mult), None)
            vopd(lambda e: e.tensor_copy(sb["wbf"][:, D:2 * D], sb["wf32b"][:, 0:D]))
            em.wait("sync", "ev")
            dma("sync", dr[cname].ap()[128 * k:128 * k + 128, :], sb["wbf"][:, 0:D],
                "wdma")
            em.wait("tensor", "ev")
            for c in range(KC):
                top(lambda t, k=k, c=c: t.matmul(
                    B[0][:, c:c + 1], sb["wbf"][:, D + 128 * c:D + 128 * c + 128],
                    sb["ln1b_col"][:, k:k + 1], start=True, stop=True))
            em.wait("vector", "mm")
            vopd(lambda e: e.tensor_tensor(sb["bias_sc"][:], B[0][:, 0:KC],
                                           sb["bias_sc"][:], OP.add))
        em.wait("sync", "ev")
        dmas("sync", bass.AP(dr["b_row"], 0, [[1, 128], [128, KC]]),
            bass.AP(sb["bias_sc"], 0, [[KC, 128], [1, KC]]), "wdma")
        em.wait("gpsimd", "wdma")
        dma("gpsimd", brep[:], bass.AP(dr["b_row"], 0, [[0, 128], [1, D]]), "wdma")

    if "attn" not in skip:
        build_attn_cache("wq", "c_wq", "bq", sb["bq_rep"])
        build_attn_cache("wk", "c_wk", "bk", sb["bk_rep"])
        build_attn_cache("wv", "c_wv", "bv", sb["bv_rep"])

    dmas("sync", bass.AP(sb["fcb_col"], 0, [[CC, 128], [1, CC]]),
         bass.AP(dr["fcb"], 0, [[1, 128], [128, CC]]), "wdma")
    em.wait("vector", "wdma")
    for k in range(KC):
        dma("sync", sb["wf32b"][:], dr["fcw"].ap()[128 * k:128 * k + 128, :], "wdma")
        em.wait("vector", "wdma")
        lcol = sb["lnw_cols"][:, KC + k:KC + k + 1]
        vop(lambda e, lcol=lcol: e.tensor_scalar(
            sb["wbf"][:, 0:MLP], sb["wf32b"][:], lcol, None, OP.mult), None)
        vopd(lambda e: e.tensor_copy(sb["wbf"][:, MLP:2 * MLP], sb["wf32b"][:]))
        em.wait("sync", "ev")
        dma("sync", bass.AP(dr["c_fcw"], k * 16384,
                            [[128, 128], [KC * 16384, CC], [1, 128]]),
            bass.AP(sb["wbf"], 0, [[2 * MLP, 128], [128, CC], [1, 128]]), "wdma")
        em.wait("tensor", "ev")
        for c in range(CC):
            top(lambda t, k=k, c=c: t.matmul(
                B[1][:, c:c + 1], sb["wbf"][:, MLP + 128 * c:MLP + 128 * c + 128],
                sb["ln2b_col"][:, k:k + 1], start=True, stop=True))
        em.wait("vector", "mm")
        vopd(lambda e: e.tensor_tensor(sb["fcb_col"][:], B[1][:, 0:CC],
                                       sb["fcb_col"][:], OP.add))

    if "casts" not in skip:
        for src_, dst_ in [("wo", "c_wo"), ("projw", "c_projw")]:
            nr = dr[src_].shape[0]
            for k in range(0, nr, 128):
                em.wait("sync", "ev")
                dma("sync", sb["wf32b"][:, 0:D],
                    dr[src_].ap()[k:k + 128, :], "wdma")
                em.wait("vector", "wdma")
                vopd(lambda e: e.tensor_copy(sb["wbf"][:, 0:D],
                                             sb["wf32b"][:, 0:D]))
                em.wait("sync", "ev")
                dma("sync", dr[dst_].ap()[k:k + 128, :], sb["wbf"][:, 0:D],
                    "wdma")

    def build_ternary(src, dst, mi, blocked):
        nchunk = dr[src].shape[0] // 128
        width = dr[src].shape[1]
        vop(lambda e: e.memset(sb["sc_a"][:, 0:1], 0.0), None)
        for k in range(nchunk):
            em.wait("sync", "ev")
            dma("sync", sb["wf32b"][:, 0:width],
                dr[src].ap()[128 * k:128 * k + 128, :], "wdma")
            em.wait("vector", "wdma")
            vop(lambda e, w=width: e.tensor_reduce(
                sb["sc_b"][:, 0:1], sb["wf32b"][:, 0:w], AX.X, OP.add,
                apply_absolute_value=True), None)
            vop(lambda e: e.drain(), None)
            vop(lambda e: e.tensor_tensor(sb["sc_a"][:, 0:1], sb["sc_a"][:, 0:1],
                                          sb["sc_b"][:, 0:1], OP.add), "ev")
        em.wait("tensor", "ev")
        top(lambda t: t.matmul(B[2][0:1, 0:1], sb["onesf"][:], sb["sc_a"][:, 0:1],
                               start=True, stop=True))
        em.wait("vector", "mm")
        tot = float(nchunk * 128 * width)
        vopd(lambda e, tot=tot: e.tensor_scalar(
            sb["sc_b"][0:1, 0:1], B[2][0:1, 0:1], 1.0 / tot, 1e-5,
            OP.mult, OP.max), "ev")
        vop(lambda e: e.drain(), None)
        vopd(lambda e: e.reciprocal(sb["sc_b"][0:1, 1:2], sb["sc_b"][0:1, 0:1]))
        em.wait("sync", "ev")
        dma("sync", bass.AP(dr["b_row"], 0, [[1, 1], [1, 2]]), sb["sc_b"][0:1, 0:2],
            "wdma")
        em.wait("gpsimd", "wdma")
        dma("gpsimd", sb["mbar"][:, mi:mi + 1],
            bass.AP(dr["b_row"], 0, [[0, 128], [1, 1]]), "wdma")
        dma("gpsimd", sb["mbar"][:, mi + 2:mi + 3],
            bass.AP(dr["b_row"], 1, [[0, 128], [1, 1]]), "wdma")
        em.wait("vector", "wdma")
        for k in range(nchunk):
            dma("sync", sb["wf32b"][:, 0:width],
                dr[src].ap()[128 * k:128 * k + 128, :], "wdma")
            em.wait("vector", "wdma")
            scol = sb["mbar"][:, mi + 2:mi + 3]
            vop(lambda e, w=width, scol=scol: e.tensor_scalar(
                sb["wf32b"][:, 0:w], sb["wf32b"][:, 0:w], scol, 1.0,
                OP.mult, OP.min), None)
            vop(lambda e, w=width: e.tensor_scalar(
                sb["wf32b"][:, 0:w], sb["wf32b"][:, 0:w], -1.0, MAGIC,
                OP.max, OP.add), None)
            vopd(lambda e, w=width: e.tensor_scalar(
                sb["wbf"][:, 0:w], sb["wf32b"][:, 0:w], MAGIC, None, OP.subtract))
            em.wait("sync", "ev")
            if blocked:
                ncb = width // 128
                dma("sync", bass.AP(dr[dst], k * 16384,
                                    [[128, 128], [KC * 16384, ncb], [1, 128]]),
                    bass.AP(sb["wbf"], 0, [[2 * MLP, 128], [128, ncb], [1, 128]]),
                    "wdma")
            else:
                dma("sync", dr[dst].ap()[128 * k:128 * k + 128, :],
                    sb["wbf"][:, 0:width], "wdma")
            em.wait("vector", "wdma")

    if "ternary" not in skip:
        build_ternary("w1T", "c_w1t", 0, True)
        build_ternary("w2T", "c_w2t", 1, False)

    for j, nm in enumerate(["convw0", "convw1", "convw2"]):
        dmas("sync", bass.AP(sb["convw_col"], j * CC, [[3 * CC, 128], [1, CC]]),
            bass.AP(dr[nm], 0, [[1, 128], [128, CC]]), "wdma")
    em.wait("vector", "wdma")
    vop(lambda e: e.tensor_scalar(sb["convw_col"][:], sb["convw_col"][:],
                                  sb["mbar"][:, 0:1], None, OP.mult))

    for eng in ("sync", "tensor", "vector", "scalar", "gpsimd"):
        for s in ["wdma", "ev", "mm", "gp", "act", "ldc"]:
            em.wait(eng, s)

    pro_es.close()
    HT = TT // 2
    st("fq_g2", [128, 64 * G], BF16)
    st("fk_g2", [128, 64 * G], BF16)
    st("v_g2", [128, 64 * G], BF16)
    st("fqT", [128, 4 * 128], BF16)
    st("fkT", [128, 4 * 128], BF16)
    st("S2", [128, 8 * 128], BF16)
    st("rc_c", [128, G], F32)
    st("attn_g2", [128, 64 * G], BF16)
    st("xq1w_a", [128, KC * WIN], BF16)
    st("xq1w_b", [128, KC * WIN], BF16)
    st("hsil", [128, CC * HT], BF16)
    st("hwin", [128, 2 * 260], BF16)
    st("cv1", [128, 2 * HT], BF16)
    st("cv2", [128, 2 * HT], BF16)
    st("rmax", [128, HT], BF16)
    st("rmin", [128, HT], BF16)
    st("hq_c", [128, 2 * HT], BF16)
    st("gch", [128, 2 * 256], BF16)
    st("s2bc", [128, HT], F32)
    st("s2row", [1, HT], F32)
    vop(lambda e: e.memset(sb["xq1w_a"][:], 0.0), None)
    vop(lambda e: e.memset(sb["xq1w_b"][:], 0.0), "ev")
    for eng in ("sync", "tensor", "scalar", "gpsimd"):
        em.wait(eng, "ev")

    def ln_finish(off):
        ssl = sb["stats"][:, off:off + MC]
        qsl = sb["stats2"][:, off:off + MC]
        a = sb["sc_a"][:, 0:MC]
        b = sb["sc_b"][:, 0:MC]
        vop(lambda e: e.drain(), None)
        vop(lambda e: e.tensor_scalar(ssl, ssl, 1.0 / D, None, OP.mult), None)
        vop(lambda e: e.drain(), None)
        vop(lambda e: e.tensor_tensor(a, ssl, ssl, OP.mult), None)
        vop(lambda e: e.tensor_scalar(qsl, qsl, 1.0 / D, None, OP.mult), None)
        vop(lambda e: e.drain(), None)
        vop(lambda e: e.tensor_tensor(qsl, qsl, a, OP.subtract), None)
        vop(lambda e: e.drain(), None)
        vop(lambda e: e.tensor_scalar(qsl, qsl, 1e-5, None, OP.add), "ev")
        em.wait("scalar", "ev")
        sop(lambda s: s.activation(a, qsl, AF.Ln), "act")
        sop(lambda s: s.activation(qsl, a, AF.Exp, scale=-0.5), "act")
        em.wait("vector", "act")

    def ln_stats(tile, off):
        em.wait("scalar", "ev")
        for m in range(MC):
            sl = tile[:, D * m:D * (m + 1)]
            vop(lambda e, sl=sl, m=m: e.tensor_reduce(
                sb["stats"][:, off + m:off + m + 1], sl, AX.X, OP.add), None)
            sop(lambda s, sl=sl, m=m: s.activation(
                sb["junk"][:], sl, AF.Square,
                accum_out=sb["stats2"][:, off + m:off + m + 1]), "act")
        em.wait("vector", "act")
        ln_finish(off)

    def ln_norm(src, dst, off):
        for m in range(MC):
            ssl = src[:, D * m:D * (m + 1)]
            dsl = dst[:, D * m:D * (m + 1)]
            mcol = sb["stats"][:, off + m:off + m + 1]
            rcol = sb["stats2"][:, off + m:off + m + 1]
            last = (m == MC - 1)
            (vopd if last else vop)(
                lambda e, ssl=ssl, dsl=dsl, mcol=mcol, rcol=rcol: e.tensor_scalar(
                    dsl, ssl, mcol, rcol, OP.subtract, OP.mult))

    def tm_to_fm(tm_tile, fm_tile, fstride=TT, foff=0):
        em.wait("tensor", "ev", em.cnt["ev"])
        fsz = fm_tile.shape[1]
        ngrp = MC * KC // 4
        evmark = [em.cnt["sv"]] * ngrp
        for grp in range(ngrp):
            m, k0 = grp // 2, (grp % 2) * 4
            bank = B[4 + (grp % 4)]
            em.wait("tensor", "sv", evmark[grp - 2] if grp >= 2 else evmark[0])
            for q in range(4):
                k = k0 + q
                pv = bank.ap().bitcast(BF16)[:, 128 * q:128 * q + 128]
                top(lambda t, m=m, k=k, pv=pv, tm=tm_tile: t.transpose(
                    pv, tm[:, D * m + 128 * k:D * m + 128 * k + 128],
                    sb["identb"][:]), "tr")
            trmark = em.cnt["tr"]
            pvfull = bank.ap().bitcast(BF16)[:, 0:512]
            dsl = bass.AP(fm_tile, foff + fstride * k0 + 128 * m,
                          [[fsz, 128], [fstride, 4], [1, 128]])
            em.wait("vector", "tr", trmark)
            vopd(lambda e, dsl=dsl, pvfull=pvfull: e.tensor_copy(dsl, pvfull),
                 "sv")
            evmark[grp] = em.cnt["sv"]

    x_d = dr["x"].ap()
    y_d = dr["y"].ap()
    FS = 64 * G

    def full_barrier():
        for eng in ("sync", "tensor", "vector", "scalar", "gpsimd"):
            for s in ["wdma", "xdma", "ydma", "bnc", "gth", "mm", "tr",
                      "ev", "sv", "gp", "act"]:
                em.wait(eng, s)

    def phase_attn(ti):
        if "attn" in barrier:
            full_barrier()
        x1cur = sb["x1a"] if ti % 2 == 0 else sb["x1b"]
        wcur = sb["xq1w_a"] if ti % 2 == 0 else sb["xq1w_b"]
        wprev = sb["xq1w_b"] if ti % 2 == 0 else sb["xq1w_a"]

        em.wait("sync", "ydma")
        em.wait("sync", "ev")
        for m in range(MC):
            dma("sync", sb["x_tm"][:, D * m:D * (m + 1)],
                x_d[ti * TT + 128 * m:ti * TT + 128 * (m + 1), :], "xdma")
        em.wait("vector", "xdma")
        em.wait("scalar", "xdma")

        ln_stats(sb["x_tm"], 0)
        ln_norm(sb["x_tm"], sb["tm_sc"], 0)
        em.wait("tensor", "ev")
        tm_to_fm(sb["tm_sc"], sb["fm_sc"])

        slot_mark = [em.cnt["mm"]] * 8
        bank_mark = [em.cnt["ev"]] * 4
        tmsc_mark = [em.cnt["bnc"]] * MC
        gath_marks = {}
        for pi, (cname, brep, do_elu, bname) in enumerate([
                ("c_wq", sb["bq_rep"], True, "b_fq"),
                ("c_wk", sb["bk_rep"], True, "b_fk"),
                ("c_wv", sb["bv_rep"], False, "b_v")]):
            ld_mark = []
            for k in range(KC):
                g = pi * KC + k
                em.wait("sync", "mm", slot_mark[g % 8])
                dma("sync", sb["wbuf"][:, D * (g % 8):D * (g % 8 + 1)],
                    dr[cname].ap()[128 * k:128 * k + 128, :], "wdma")
                ld_mark.append(em.cnt["wdma"])
            for k in range(KC):
                g = pi * KC + k
                em.wait("tensor", "wdma", ld_mark[k])
                if k == 0:
                    for b in range(4):
                        em.wait("tensor", "ev", bank_mark[b])
                    if pi == 0:
                        em.wait("tensor", "sv")
                for m in range(MC):
                    for h2 in range(2):
                        lhsT = sb["fm_sc"][:, TT * k + 128 * m:
                                           TT * k + 128 * m + 128]
                        top(lambda t, lhsT=lhsT, m=m, h2=h2, g=g, k=k: t.matmul(
                            B[2 * m + h2][:, :], lhsT,
                            sb["wbuf"][:, D * (g % 8) + 512 * h2:
                                       D * (g % 8) + 512 * (h2 + 1)],
                            start=(k == 0), stop=(k == KC - 1)))
            mm_done = em.cnt["mm"]
            for k in range(KC):
                slot_mark[(pi * KC + k) % 8] = mm_done
            for m in range(MC):
                em.wait("vector", "mm", mm_done)
                em.wait("vector", "bnc", tmsc_mark[m])
                for h2 in range(2):
                    psl = B[2 * m + h2][:, :]
                    osl = sb["tm_sc"][:, D * m + 512 * h2:D * m + 512 * (h2 + 1)]
                    rsl = brep[:, 512 * h2:512 * (h2 + 1)]
                    ju2 = sb["ju2"][:, 512 * h2:512 * h2 + 512]
                    ju3 = sb["ju3"][:, 512 * h2:512 * h2 + 512]
                    jnk = sb["junk"][:, 512 * h2:512 * h2 + 512]
                    if do_elu:
                        vopd(lambda e, psl=psl, rsl=rsl, ju2=ju2:
                             e.scalar_tensor_tensor(ju2, psl, 1.0, rsl,
                                                    OP.mult, OP.add), "ev")
                        em.wait("scalar", "ev", em.cnt["ev"])
                        sop(lambda s, ju2=ju2, jnk=jnk: s.activation(
                            jnk, ju2, AF.Exp), "act")
                        sopd(lambda s, ju2=ju2, ju3=ju3: s.activation(
                            ju3, ju2, AF.Relu), "act")
                        em.wait("vector", "act", em.cnt["act"])
                        vopd(lambda e, osl=osl, jnk=jnk, ju3=ju3:
                             e.scalar_tensor_tensor(osl, jnk, 1.0, ju3,
                                                    OP.min, OP.add), "ev")
                    else:
                        vopd(lambda e, psl=psl, rsl=rsl, osl=osl:
                             e.scalar_tensor_tensor(osl, psl, 1.0, rsl,
                                                    OP.mult, OP.add), "ev")
                bank_mark[m] = em.cnt["ev"]
                em.wait("scalar", "ev", em.cnt["ev"])
                if m == 0:
                    em.wait("scalar", "gth", em.cnt["gth"])
                dma("scalar", dr[bname].ap()[128 * m:128 * (m + 1), :],
                    sb["tm_sc"][:, D * m:D * (m + 1)], "bnc")
                tmsc_mark[m] = em.cnt["bnc"]
            qeng = ("gpsimd",) if pi < 2 else ("gpsimd", "scalar")
            for eng in qeng:
                em.wait(eng, "bnc", em.cnt["bnc"])
                em.wait(eng, "mm", em.cnt["mm"])
            tname = {"b_fq": "fq_g2", "b_fk": "fk_g2", "b_v": "v_g2"}[bname]
            for h in range(H):
                dma(qeng[h % len(qeng)],
                    bass.AP(sb[tname], h * FS, [[16 * FS, 8], [64, G], [1, 64]]),
                    bass.AP(dr[bname], 64 * h, [[D, 8], [8 * D, G], [1, 64]]),
                    "gth")
            gath_marks[tname] = em.cnt["gth"]

        em.wait("tensor", "gth")
        b01_mark = em.cnt["sv"]
        s2e_mark = [em.cnt["sv"]] * 2
        st1_mark = em.cnt["mm"]
        ave_mark = em.cnt["ev"]
        mb4 = bass.AP(sb["maskS"], 0, [[128, 128], [0, 4], [1, 128]])
        for b8 in range(G // 8):
            sA = B[4] if b8 % 2 == 0 else B[6]
            sB = B[5] if b8 % 2 == 0 else B[7]
            pvq = B[0].ap().bitcast(BF16)
            pvk = B[1].ap().bitcast(BF16)
            em.wait("tensor", "sv", b01_mark)
            for jj in range(4):
                j = b8 * 4 + jj
                top(lambda t, j=j, jj=jj, pvq=pvq: t.transpose(
                    pvq[:, 128 * jj:128 * jj + 128],
                    sb["fq_g2"][:, 128 * j:128 * j + 128], sb["identb"][:]), "tr")
                top(lambda t, j=j, jj=jj, pvk=pvk: t.transpose(
                    pvk[:, 128 * jj:128 * jj + 128],
                    sb["fk_g2"][:, 128 * j:128 * j + 128], sb["identb"][:]), "tr")
            trm = em.cnt["tr"]
            em.wait("vector", "tr", trm)
            em.wait("vector", "mm", st1_mark)
            vopd(lambda e, pvq=pvq: e.tensor_copy(sb["fqT"][:, 0:512],
                                                  pvq[:, 0:512]), "sv")
            vopd(lambda e, pvk=pvk: e.tensor_copy(sb["fkT"][:, 0:512],
                                                  pvk[:, 0:512]), "sv")
            b01_mark = em.cnt["sv"]
            em.wait("tensor", "sv", b01_mark)
            em.wait("tensor", "sv", s2e_mark[b8 % 2])
            for jj in range(4):
                qs = sb["fqT"][:, 128 * jj:128 * jj + 128]
                ks = sb["fkT"][:, 128 * jj:128 * jj + 128]
                for dd in range(2):
                    bank = sA if dd == 0 else sB
                    top(lambda t, qs=qs, ks=ks, dd=dd, bank=bank, jj=jj:
                        t.matmul(bank[:, 128 * jj:128 * jj + 128],
                                 ks[64 * dd:64 * dd + 64, :],
                                 qs[64 * dd:64 * dd + 64, :],
                                 start=True, stop=True))
            st1_mark = em.cnt["mm"]
            em.wait("vector", "mm", st1_mark)
            vop(lambda e, sA=sA: e.tensor_tensor(
                sb["S2"][:, 0:512].rearrange("p (g c) -> p g c", g=4),
                sA[:].rearrange("p (g c) -> p g c", g=4), mb4, OP.mult), None)
            vopd(lambda e, sB=sB: e.tensor_tensor(
                sb["S2"][:, 512:1024].rearrange("p (g c) -> p g c", g=4),
                sB[:].rearrange("p (g c) -> p g c", g=4), mb4, OP.mult), "sv")
            s2e_mark[b8 % 2] = em.cnt["sv"]
            em.wait("tensor", "sv", s2e_mark[b8 % 2])
            em.wait("tensor", "ev", ave_mark)
            for gg in range(8):
                jj, dd = gg // 2, gg % 2
                gi = b8 * 8 + 2 * jj + dd
                s2sl = sb["S2"][:, 512 * dd + 128 * jj:512 * dd + 128 * jj + 128]
                top(lambda t, s2sl=s2sl, gi=gi: t.matmul(
                    B[3][:, gi:gi + 1], s2sl, sb["onesb"][:],
                    start=True, stop=True))
                top(lambda t, s2sl=s2sl, gi=gi, gg=gg: t.matmul(
                    B[2][:, 64 * gg:64 * gg + 64], s2sl,
                    sb["v_g2"][:, 64 * gi:64 * gi + 64], start=True, stop=True))
            st2m = em.cnt["mm"]
            em.wait("vector", "mm", st2m)
            rsl = sb["rc_c"][:, 8 * b8:8 * b8 + 8]
            vop(lambda e, b8=b8: e.tensor_scalar(
                sb["sc_c"][:, 0:8], B[3][:, 8 * b8:8 * b8 + 8], 1e-6, None,
                OP.add), None)
            vop(lambda e: e.drain(), None)
            vop(lambda e, rsl=rsl: e.reciprocal(rsl, sb["sc_c"][:, 0:8]), None)
            vop(lambda e: e.drain(), None)
            rbc = bass.AP(sb["rc_c"], 8 * b8, [[G, 128], [1, 8], [0, 64]])
            vopd(lambda e, b8=b8, rbc=rbc: e.tensor_tensor(
                sb["attn_g2"][:, 512 * b8:512 * (b8 + 1)].rearrange(
                    "p (g e) -> p g e", g=8),
                B[2][:].rearrange("p (g e) -> p g e", g=8), rbc, OP.mult), "ev")
            ave_mark = em.cnt["ev"]

        em.wait("scalar", "ev")
        em.wait("scalar", "gth", em.cnt["gth"])
        dma("scalar", dr["b_at"].ap(), sb["attn_g2"][:], "bnc")
        gengs = ("scalar", "gpsimd", "sync")
        for eng in gengs:
            em.wait(eng, "bnc", em.cnt["bnc"])
        for gi in range(G):
            m = (8 * gi) // 128
            r0 = (8 * gi) % 128
            out_ap = bass.AP(sb["tm_sc"], r0 * (MC * D) + D * m,
                             [[MC * D, 8], [64, 16], [1, 64]])
            in_ap = bass.AP(dr["b_at"], 64 * gi, [[16 * FS, 8], [FS, 16], [1, 64]])
            dma(gengs[gi % 3], out_ap, in_ap, "gth")
        em.wait("tensor", "gth")
        em.wait("vector", "gth")
        em.wait("scalar", "gth")
        tm_to_fm(sb["tm_sc"], sb["fm_sc"])

        em.wait("sync", "mm")
        for k in range(KC):
            dma("sync", sb["wbuf"][:, D * k:D * (k + 1)],
                dr["c_wo"].ap()[128 * k:128 * k + 128, :], "wdma")
        em.wait("tensor", "wdma")
        em.wait("tensor", "sv")
        for m in range(MC):
            pb = (B[2 * (m % 2)], B[2 * (m % 2) + 1])
            em.wait("tensor", "ev")
            for k in range(KC):
                lhsT = sb["fm_sc"][:, TT * k + 128 * m:TT * k + 128 * m + 128]
                for h2 in range(2):
                    top(lambda t, lhsT=lhsT, k=k, h2=h2, pb=pb: t.matmul(
                        pb[h2][:, :], lhsT,
                        sb["wbuf"][:, D * k + 512 * h2:D * k + 512 * (h2 + 1)],
                        start=(k == 0), stop=(k == KC - 1)))
            em.wait("vector", "mm")
            for h2 in range(2):
                psl = pb[h2][:, :]
                bsl = sb["bo_rep"][:, 512 * h2:512 * (h2 + 1)]
                xsl = sb["x_tm"][:, D * m + 512 * h2:D * m + 512 * (h2 + 1)]
                osl = sb["xmid_tm"][:, D * m + 512 * h2:D * m + 512 * (h2 + 1)]
                vop(lambda e, psl=psl, bsl=bsl: e.scalar_tensor_tensor(
                    sb["ju2"][:, 0:512], psl, 1.0, bsl, OP.mult, OP.add), None)
                vopd(lambda e, osl=osl, xsl=xsl: e.tensor_tensor(
                    osl, sb["ju2"][:, 0:512], xsl, OP.add), "ev")

        ln_stats(sb["xmid_tm"], 4)
        ln_norm(sb["xmid_tm"], sb["tm_sc"], 4)
        em.wait("tensor", "ev")
        tm_to_fm(sb["tm_sc"], sb["fm_sc"])

        em.wait("tensor", "sv")
        gelu_mark = [em.cnt["act"]] * 2
        fcslice_mark = [em.cnt["mm"]] * 2
        pjslice_mark = [em.cnt["mm"]] * 2
        fcev = [em.cnt["ev"]] * 2
        for half in range(2):
            projmm_marks = {}
            pend_proj = None

            def emit_proj(pp):
                gsl, wpsl, gm, pjw_ld, c = pp
                em.wait("tensor", "wdma", pjw_ld)
                em.wait("tensor", "act", gm)
                for mloc in range(2):
                    for h2 in range(2):
                        top(lambda t, gsl=gsl, mloc=mloc, h2=h2, wpsl=wpsl, c=c:
                            t.matmul(B[2 * mloc + h2][:, :],
                                     gsl[:, 128 * mloc:128 * mloc + 128],
                                     wpsl[:, 512 * h2:512 * (h2 + 1)],
                                     start=(c == 0), stop=(c == CC - 1)))
                projmm_marks[c] = em.cnt["mm"]
                pjslice_mark[c % 2] = em.cnt["mm"]

            for c in range(CC):
                wsl = sb["wslc2"][:, D * (c % 2):D * (c % 2) + D]
                em.wait("sync", "mm", fcslice_mark[c % 2])
                dma("sync", wsl,
                    bass.AP(dr["c_fcw"], c * KC * 16384,
                            [[128, 128], [16384, KC], [1, 128]]), "wdma")
                fcw_ld = em.cnt["wdma"]
                em.wait("sync", "wdma", fcw_ld)
                wpsl = sb["wslc"][:, D * (c % 2):D * (c % 2) + D]
                em.wait("sync", "mm", pjslice_mark[c % 2])
                dma("sync", wpsl, dr["c_projw"].ap()[128 * c:128 * c + 128, :],
                    "wdma")
                em.wait("sync", "wdma", em.cnt["wdma"])
                pjw_ld = em.cnt["wdma"]
                fcbank = B[4 + (c % 2)]
                em.wait("tensor", "wdma", fcw_ld)
                em.wait("tensor", "ev", fcev[c % 2])
                for k in range(KC):
                    rhs = sb["fm_sc"][:, TT * k + 256 * half:
                                      TT * k + 256 * half + 256]
                    top(lambda t, wsl=wsl, k=k, rhs=rhs, fcbank=fcbank: t.matmul(
                        fcbank[:, 0:256], wsl[:, 128 * k:128 * k + 128], rhs,
                        start=(k == 0), stop=(k == KC - 1)))
                fcmm = em.cnt["mm"]
                fcslice_mark[c % 2] = fcmm
                gssl = sb["ju3"][:, 256 * (c % 2):256 * (c % 2) + 256]
                em.wait("vector", "mm", fcmm)
                em.wait("vector", "act", gelu_mark[c % 2])
                vopd(lambda e, gssl=gssl, fcbank=fcbank: e.tensor_copy(
                    gssl, fcbank[:, 0:256]), "ev")
                fcev[c % 2] = em.cnt["ev"]
                gsl = sb["gch"][:, 256 * (c % 2):256 * (c % 2) + 256]
                em.wait("scalar", "ev", fcev[c % 2])
                if c >= 2:
                    em.wait("scalar", "mm", projmm_marks[c - 2])
                sopd(lambda s, gsl=gsl, gssl=gssl, c=c: s.activation(
                    gsl, gssl, AF.Gelu,
                    bias=sb["fcb_col"][:, c:c + 1], scale=1.0), "act")
                gelu_mark[c % 2] = em.cnt["act"]
                if pend_proj is not None:
                    emit_proj(pend_proj)
                pend_proj = (gsl, wpsl, gelu_mark[c % 2], pjw_ld, c)
            emit_proj(pend_proj)
            em.wait("vector", "mm", em.cnt["mm"])
            for mloc in range(2):
                m = 2 * half + mloc
                for h2 in range(2):
                    psl = B[2 * mloc + h2][:, :]
                    bsl = sb["projb_rep"][:, 512 * h2:512 * (h2 + 1)]
                    midsl = sb["xmid_tm"][:, D * m + 512 * h2:D * m + 512 * (h2 + 1)]
                    xsl = sb["x_tm"][:, D * m + 512 * h2:D * m + 512 * (h2 + 1)]
                    vop(lambda e, psl=psl, bsl=bsl: e.scalar_tensor_tensor(
                        sb["ju2"][:, 0:512], psl, 1.0, bsl, OP.mult, OP.add), None)
                    vop(lambda e, midsl=midsl: e.tensor_tensor(
                        sb["ju3"][:, 0:512], sb["ju2"][:, 0:512], midsl, OP.add),
                        None)
                    vopd(lambda e, xsl=xsl: e.tensor_tensor(
                        xsl, sb["ju3"][:, 0:512], xsl, OP.add), "ev")
            em.wait("tensor", "ev", em.cnt["ev"])

        ln_stats(sb["x_tm"], 0)
        ln_norm(sb["x_tm"], x1cur, 0)

        for m in range(MC):
            sl = x1cur[:, D * m:D * (m + 1)]
            vop(lambda e, sl=sl, m=m: e.tensor_reduce(
                sb["qscale"][:, m:m + 1], sl, AX.X, OP.max,
                apply_absolute_value=True), None)
        vop(lambda e: e.drain(), None)
        vop(lambda e: e.tensor_scalar(sb["qscale"][:, 0:MC], sb["qscale"][:, 0:MC],
                                      1e-5, None, OP.max), None)
        vop(lambda e: e.drain(), None)
        vop(lambda e: e.reciprocal(sb["sc_c"][:, 0:MC], sb["qscale"][:, 0:MC]), None)
        vop(lambda e: e.tensor_scalar(sb["qscale"][:, 4:4 + MC],
                                      sb["qscale"][:, 0:MC], 1.0 / 127.0, None,
                                      OP.mult), None)
        vop(lambda e: e.drain(), None)
        vop(lambda e: e.tensor_scalar(sb["qscale"][:, 0:MC], sb["sc_c"][:, 0:MC],
                                      127.0, None, OP.mult), None)
        vop(lambda e: e.drain(), None)
        for m in range(MC):
            ssl = x1cur[:, D * m:D * (m + 1)]
            scol = sb["qscale"][:, m:m + 1]
            dcol = sb["qscale"][:, 4 + m:5 + m]
            osl = sb["tm_sc"][:, D * m:D * (m + 1)]
            vop(lambda e, ssl=ssl, scol=scol: e.tensor_scalar(
                sb["junk"][:], ssl, scol, MAGIC, OP.mult, OP.add), None)
            vopd(lambda e, osl=osl, dcol=dcol: e.tensor_scalar(
                osl, sb["junk"][:], MAGIC, dcol, OP.subtract, OP.mult), "ev")
        em.wait("tensor", "ev")
        tm_to_fm(sb["tm_sc"], wcur, fstride=WIN, foff=1)
        em.wait("vector", "sv")
        for k in range(KC):
            if ti > 0:
                vop(lambda e, k=k, wc=wcur, wp=wprev: e.tensor_copy(
                    wc[:, WIN * k:WIN * k + 1], wp[:, WIN * k + TT:WIN * k + TT + 1]),
                    None)
            (vopd if k == KC - 1 else vop)(
                lambda e, k=k, wc=wcur, wp=wprev: e.tensor_copy(
                    wp[:, WIN * k + TT + 1:WIN * k + TT + 2],
                    wc[:, WIN * k + 1:WIN * k + 2]), "ev")

    def phase_ffn(tj, last=False):
        if "ffn" in barrier:
            full_barrier()
        x1p = sb["x1a"] if tj % 2 == 0 else sb["x1b"]
        wp = sb["xq1w_a"] if tj % 2 == 0 else sb["xq1w_b"]
        HT = TT // 2
        if last:
            for k in range(KC):
                (vopd if k == KC - 1 else vop)(lambda e, k=k, wp=wp: e.memset(
                    wp[:, WIN * k + TT + 1:WIN * k + TT + 2], 0.0), "ev")
            em.wait("tensor", "ev")
        em.wait("tensor", "sv")
        for half in range(2):
            vop(lambda e: e.memset(sb["rmax"][:], 0.0), None)
            vop(lambda e: e.memset(sb["rmin"][:], 0.0), "ev")
            w1slice_mark = [em.cnt["mm"]] * 2
            hbank_mark = [em.cnt["ev"]] * 2
            convslot_mark = [em.cnt["act"]] * 2
            w1mm_marks = []
            for c in range(CC):
                wsl = sb["wslc2"][:, D * (c % 2):D * (c % 2) + D]
                em.wait("sync", "mm", w1slice_mark[c % 2])
                dma("sync", wsl,
                    bass.AP(dr["c_w1t"], c * KC * 16384,
                            [[128, 128], [16384, KC], [1, 128]]), "wdma")
                ldm = em.cnt["wdma"]
                bA = B[4 + (c % 2)]
                em.wait("tensor", "wdma", ldm)
                em.wait("tensor", "ev", hbank_mark[c % 2])
                for k in range(KC):
                    lh = wsl[:, 128 * k:128 * k + 128]
                    top(lambda t, lh=lh, bA=bA, wp=wp, k=k, half=half: t.matmul(
                        bA[:, 0:258], lh,
                        wp[:, WIN * k + HT * half:WIN * k + HT * half + 258],
                        start=(k == 0), stop=(k == KC - 1)))
                w1mm_marks.append(em.cnt["mm"])
                w1slice_mark[c % 2] = em.cnt["mm"]
                hw = sb["hwin"][:, 260 * (c % 2):260 * (c % 2) + 258]
                cva = sb["cv1"][:, 256 * (c % 2):256 * (c % 2) + 256]
                cvb = sb["cv2"][:, 256 * (c % 2):256 * (c % 2) + 256]
                em.wait("vector", "mm", w1mm_marks[-1])
                em.wait("vector", "act", convslot_mark[c % 2])
                vopd(lambda e, bA=bA, hw=hw: e.tensor_copy(hw, bA[:, 0:258]),
                     "ev")
                hbank_mark[c % 2] = em.cnt["ev"]
                w0c = sb["convw_col"][:, c:c + 1]
                w1c = sb["convw_col"][:, CC + c:CC + c + 1]
                w2c = sb["convw_col"][:, 2 * CC + c:2 * CC + c + 1]
                vop(lambda e, w1c=w1c, hw=hw, cva=cva: e.tensor_scalar(
                    cva, hw[:, 1:257], w1c, None, OP.mult), None)
                vop(lambda e, w0c=w0c, hw=hw, cva=cva, cvb=cvb:
                    e.scalar_tensor_tensor(cvb, hw[:, 0:256], w0c, cva,
                                           OP.mult, OP.add), None)
                vopd(lambda e, w2c=w2c, hw=hw, cva=cva, cvb=cvb:
                     e.scalar_tensor_tensor(cva, hw[:, 2:258], w2c, cvb,
                                            OP.mult, OP.add), "ev")
                cvm = em.cnt["ev"]
                em.wait("scalar", "ev", cvm)
                hs = sb["hsil"][:, HT * c:HT * (c + 1)]
                sopd(lambda s, hs=hs, c=c, cva=cva: s.activation(
                    hs, cva, AF.Silu, bias=sb["convb_col"][:, c:c + 1],
                    scale=1.0), "act")
                convslot_mark[c % 2] = em.cnt["act"]
                em.wait("vector", "act", em.cnt["act"])
                vop(lambda e, hs=hs: e.tensor_tensor(sb["rmax"][:], sb["rmax"][:],
                                                     hs, OP.max), None)
                vop(lambda e, hs=hs: e.tensor_tensor(sb["rmin"][:], sb["rmin"][:],
                                                     hs, OP.min), "ev")

            em.wait("tensor", "ev", em.cnt["ev"])
            for mloc in range(2):
                m = 2 * half + mloc
                pvA = B[6].ap().bitcast(BF16)[:, 0:128]
                pvB = B[6].ap().bitcast(BF16)[:, 128:256]
                top(lambda t, mloc=mloc, pvA=pvA: t.transpose(
                    pvA, sb["rmax"][:, 128 * mloc:128 * mloc + 128],
                    sb["identb"][:]), "tr")
                top(lambda t, mloc=mloc, pvB=pvB: t.transpose(
                    pvB, sb["rmin"][:, 128 * mloc:128 * mloc + 128],
                    sb["identb"][:]), "tr")
                em.wait("vector", "tr")
                vop(lambda e, pvA=pvA: e.tensor_reduce(
                    sb["sc_a"][:, 0:1], pvA, AX.X, OP.max,
                    apply_absolute_value=True), None)
                vopd(lambda e, pvB=pvB: e.tensor_reduce(
                    sb["sc_a"][:, 1:2], pvB, AX.X, OP.max,
                    apply_absolute_value=True), "ev")
                vopd(lambda e, m=m: e.tensor_tensor(
                    sb["q2s"][:, m:m + 1], sb["sc_a"][:, 0:1], sb["sc_a"][:, 1:2],
                    OP.max), "sv")
                em.wait("tensor", "sv")
            c0 = 2 * half
            vop(lambda e: e.drain(), None)
            vop(lambda e, c0=c0: e.tensor_scalar(
                sb["q2s"][:, c0:c0 + 2], sb["q2s"][:, c0:c0 + 2], 1e-5, None,
                OP.max), None)
            vop(lambda e: e.drain(), None)
            vop(lambda e, c0=c0: e.reciprocal(sb["sc_c"][:, 0:2],
                                              sb["q2s"][:, c0:c0 + 2]), None)
            vop(lambda e, c0=c0: e.tensor_scalar(
                sb["q2s"][:, 4 + c0:6 + c0], sb["q2s"][:, c0:c0 + 2],
                1.0 / 127.0, None, OP.mult), None)
            vop(lambda e: e.drain(), None)
            vop(lambda e, c0=c0: e.tensor_tensor(
                sb["q2s"][:, 4 + c0:6 + c0], sb["q2s"][:, 4 + c0:6 + c0],
                sb["mbar"][:, 1:2].broadcast_to([128, 2]), OP.mult), None)
            vopd(lambda e, c0=c0: e.tensor_scalar(
                sb["q2s"][:, c0:c0 + 2], sb["sc_c"][:, 0:2], 127.0, None,
                OP.mult), "ev")
            em.wait("tensor", "ev")
            for mloc in range(2):
                m = 2 * half + mloc
                top(lambda t, m=m, mloc=mloc: t.transpose(
                    B[7][0:1, 128 * mloc:128 * mloc + 128], sb["q2s"][:, m:m + 1],
                    sb["identf"][:]), "tr")
            em.wait("vector", "tr")
            vopd(lambda e: e.tensor_copy(sb["s2row"][:], B[7][0:1, 0:HT]), "sv")
            em.wait("tensor", "sv")
            top(lambda t: t.matmul(B[7][:, 0:HT], sb["onesrow"][:], sb["s2row"][:],
                                   start=True, stop=True))
            em.wait("vector", "mm")
            vopd(lambda e: e.tensor_copy(sb["s2bc"][:], B[7][:, 0:HT]), "ev")

            em.wait("tensor", "ev", em.cnt["ev"])
            w2slice_mark = [em.cnt["mm"]] * 2
            hq_mark = [em.cnt["mm"]] * 2
            for c in range(CC):
                hs = sb["hsil"][:, HT * c:HT * (c + 1)]
                hq = sb["hq_c"][:, HT * (c % 2):HT * (c % 2) + HT]
                em.wait("vector", "mm", hq_mark[c % 2])
                vop(lambda e, hs=hs: e.tensor_tensor(
                    sb["ju2"][:, 0:HT], hs, sb["s2bc"][:], OP.mult), None)
                vopd(lambda e, hq=hq: e.tensor_scalar(
                    hq, sb["ju2"][:, 0:HT], MAGIC, MAGIC, OP.add, OP.subtract),
                    "ev")
                rndm = em.cnt["ev"]
                wsl = sb["wslc"][:, D * (c % 2):D * (c % 2) + D]
                em.wait("sync", "mm", w2slice_mark[c % 2])
                dma("sync", wsl, dr["c_w2t"].ap()[128 * c:128 * c + 128, :], "wdma")
                ldm = em.cnt["wdma"]
                em.wait("tensor", "wdma", ldm)
                em.wait("tensor", "ev", rndm)
                for mloc in range(2):
                    for h2 in range(2):
                        top(lambda t, hq=hq, mloc=mloc, h2=h2, wsl=wsl, c=c:
                            t.matmul(B[2 * mloc + h2][:, :],
                                     hq[:, 128 * mloc:128 * mloc + 128],
                                     wsl[:, 512 * h2:512 * (h2 + 1)],
                                     start=(c == 0), stop=(c == CC - 1)))
                hq_mark[c % 2] = em.cnt["mm"]
                w2slice_mark[c % 2] = em.cnt["mm"]
            em.wait("vector", "mm")
            for mloc in range(2):
                m = 2 * half + mloc
                for h2 in range(2):
                    psl = B[2 * mloc + h2][:, :]
                    x1sl = x1p[:, D * m + 512 * h2:D * m + 512 * (h2 + 1)]
                    osl = sb["xmid_tm"][:, D * m + 512 * h2:D * m + 512 * (h2 + 1)]
                    dcol = sb["q2s"][:, 4 + m:5 + m]
                    vopd(lambda e, psl=psl, x1sl=x1sl, osl=osl, dcol=dcol:
                         e.scalar_tensor_tensor(osl, psl, dcol, x1sl,
                                                OP.mult, OP.add), "ev")
            em.wait("tensor", "ev")

        ln_stats(sb["xmid_tm"], 8)
        ln_norm(sb["xmid_tm"], sb["x_tm"], 8)
        em.wait("sync", "ev")
        for m in range(MC):
            dma("sync", y_d[tj * TT + 128 * m:tj * TT + 128 * (m + 1), :],
                sb["x_tm"][:, D * m:D * (m + 1)], "ydma")

    with nc.Block() as block:
        for ti in range(nt_limit):
            phase_attn(ti)
            if ti > 0 and do_ffn:
                phase_ffn(ti - 1)
        if do_ffn:
            phase_ffn(nt_limit - 1, last=True)
        if dump:
            for eng in ("sync", "tensor", "vector", "scalar", "gpsimd"):
                for s in ["wdma", "xdma", "ydma", "bnc", "gth", "mm", "tr",
                          "ev", "sv", "gp", "act"]:
                    em.wait(eng, s)
            for nm in dump:
                t = sb[nm]
                shp = list(t.shape)
                dbg = nc.dram_tensor("dbg_" + nm, shp, t.dtype,
                                     kind="ExternalOutput")
                dma("sync", dbg.ap(), t[:], "ydma")
        em.wait("sync", "ydma")
        em.wait("gpsimd", "ydma")

        @block.sync
        def _(e):
            for fn in em.lists["sync"]:
                fn(e)

        @block.tensor
        def _(e):
            for fn in em.lists["tensor"]:
                fn(e)

        @block.vector
        def _(e):
            for fn in em.lists["vector"]:
                fn(e)

        @block.scalar
        def _(e):
            for fn in em.lists["scalar"]:
                fn(e)

        @block.gpsimd
        def _(e):
            for fn in em.lists["gpsimd"]:
                fn(e)

    es.close()
    return nc


_NC_CACHE = None


def _get_nc():
    global _NC_CACHE
    if _NC_CACHE is None:
        _NC_CACHE = build_program()
    return _NC_CACHE


def kernel(**inputs):
    ii = {k: np.asarray(v) for k, v in inputs.items()}
    x = np.ascontiguousarray(ii["L0_attn_q_x"].astype(np.float32, copy=False))
    shared = {
        "wq": ii["L0_attn_wq"], "wk": ii["L0_attn_wk"], "wv": ii["L0_attn_wv"],
        "wo": ii["L0_attn_wo"], "fcw": ii["L0_attn_fc_w"],
        "projw": ii["L0_attn_proj_w"],
        "w1T": np.ascontiguousarray(ii["L0_ffn_w1"].T),
        "w2T": np.ascontiguousarray(ii["L0_ffn_w2"].T),
        "ln1w": ii["L0_attn_ln1_w"], "ln1b": ii["L0_attn_ln1_b"],
        "ln2w": ii["L0_attn_ln2_w"], "ln2b": ii["L0_attn_ln2_b"],
        "bq": ii["L0_attn_bq"], "bk": ii["L0_attn_bk"], "bv": ii["L0_attn_bv"],
        "bo": ii["L0_attn_bo"], "fcb": ii["L0_attn_fc_b"],
        "projb": ii["L0_attn_proj_b"],
        "convw0": np.ascontiguousarray(ii["L0_ffn_conv_w"][:, 0, 0]),
        "convw1": np.ascontiguousarray(ii["L0_ffn_conv_w"][:, 0, 1]),
        "convw2": np.ascontiguousarray(ii["L0_ffn_conv_w"][:, 0, 2]),
        "identb": np.eye(128, dtype=ml_dtypes.bfloat16),
        "identf": np.eye(128, dtype=np.float32),
        "maskS": np.kron(np.eye(8, dtype=np.float32), np.ones((16, 16), np.float32)),
        "convb": ii["L0_ffn_conv_b"],
    }
    shared = {k: (np.ascontiguousarray(v.astype(np.float32, copy=False))
                  if v.dtype != ml_dtypes.bfloat16 else v)
              for k, v in shared.items()}
    nc = _get_nc()
    in_maps = [dict(shared, x=np.ascontiguousarray(x[c])) for c in range(NCORES)]
    res = run_bass_kernel_spmd(nc, in_maps, core_ids=list(range(NCORES)))
    out = np.stack([res.results[c]["y"] for c in range(NCORES)], axis=0)
    return out.astype(np.float32, copy=False)


if __name__ == "__main__":
    nc = build_program()
    print("built")


# version 18 (final)
# speedup vs baseline: 1.5794x; 1.0021x over previous
SERIAL_FC = False
"""Fused megakernel for mk_seq_postln_d1 (linear-attention block + BitNet FFN).

One raw-Bass program per NeuronCore, launched exactly once via
run_bass_kernel_spmd.  Batch (8) sharded across the 8 cores; each core runs
the entire forward for its own 4096-token sequence.

Structure:
  PROLOGUE  - build per-core bf16 weight caches in HBM (LN gains folded into
              weights, biases pre-folded via on-device matmuls, ternary
              quantization of w1/w2 done in fp32), load constants.
  MAIN LOOP - 8 token tiles of 512.  Per tile: LN1 -> q/k/v (token-major) ->
              elu+1 -> DRAM bounce -> grouped gathers -> 8-token-group
              block-diagonal attention matmuls -> wo -> LN2 -> fc/gelu/proj
              -> post-LN -> act-quant -> transposed window.  The BitNet FFN
              for tile i runs during iteration i+1 (the depthwise conv needs
              one token of lookahead).

Layouts: TM = [128 tokens x feat] (chunk-major), FM = [128 feat x tokens],
g2 = [128 = 16n+h x 64d] 8-token attention groups.
"""
import numpy as np
import ml_dtypes
import concourse.bass as bass
import concourse.mybir as mybir
from concourse.bass_utils import run_bass_kernel_spmd
from contextlib import ExitStack

F32 = mybir.dt.float32
BF16 = mybir.dt.bfloat16
I32 = mybir.dt.int32
OP = mybir.AluOpType
AF = mybir.ActivationFunctionType
AX = mybir.AxisListType

D = 1024
H = 16
MLP = 4096
N = 4096
TT = 512
NT = N // TT
MC = TT // 128
KC = D // 128
CC = MLP // 128
G = TT // 8
WIN = TT + 2
MAGIC = float(1.5 * 2 ** 23)
RSQRT_MAGIC = 0x5f3759df
NCORES = 8


class Emitter:
    def __init__(self):
        self.lists = {e: [] for e in ("sync", "tensor", "vector", "scalar", "gpsimd")}
        self.sems = {}
        self.cnt = {}

    def op(self, eng, fn, inc=None, n=1):
        if inc is not None:
            self.cnt[inc] += n
            sem = self.sems[inc]
            self.lists[eng].append(lambda e, fn=fn, sem=sem, n=n: fn(e).then_inc(sem, n))
        else:
            self.lists[eng].append(lambda e, fn=fn: fn(e))

    def wait(self, eng, name, val=None):
        v = self.cnt[name] if val is None else val
        sem = self.sems[name]
        self.lists[eng].append(lambda e, sem=sem, v=v: e.wait_ge(sem, v))


def build_program(nt_limit=NT, do_ffn=True, dump=(), skip=(), serial_fc=False,
                  barrier=()):
    global SERIAL_FC
    SERIAL_FC = int(serial_fc)
    nc = bass.Bass("TRN2", target_bir_lowering=False)
    es = ExitStack()
    E = es.enter_context

    dr = {}

    def dt(name, shape, dtype, kind="Internal"):
        dr[name] = nc.dram_tensor(name, shape, dtype, kind=kind)
        return dr[name]

    dt("x", [N, D], F32, "ExternalInput")
    for nm, shp in [("wq", [D, D]), ("wk", [D, D]), ("wv", [D, D]), ("wo", [D, D]),
                    ("fcw", [D, MLP]), ("projw", [MLP, D]),
                    ("w1T", [D, MLP]), ("w2T", [MLP, D]),
                    ("ln1w", [D]), ("ln1b", [D]), ("ln2w", [D]), ("ln2b", [D]),
                    ("bq", [D]), ("bk", [D]), ("bv", [D]), ("bo", [D]),
                    ("fcb", [MLP]), ("projb", [D]),
                    ("convw0", [MLP]), ("convw1", [MLP]), ("convw2", [MLP]),
                    ("convb", [MLP])]:
        dt(nm, shp, F32, "ExternalInput")
    dt("identb", [128, 128], BF16, "ExternalInput")
    dt("identf", [128, 128], F32, "ExternalInput")
    dt("maskS", [128, 128], F32, "ExternalInput")
    dt("y", [N, D], F32, "ExternalOutput")

    dt("c_wq", [D, D], BF16)
    dt("c_wk", [D, D], BF16)
    dt("c_wv", [D, D], BF16)
    dt("c_wo", [D, D], BF16)
    dt("c_fcw", [KC * CC * 128 * 128], BF16)
    dt("c_projw", [MLP, D], BF16)
    dt("c_w1t", [KC * CC * 128 * 128], BF16)
    dt("c_w2t", [MLP, D], BF16)
    dt("b_fq", [TT, D], BF16)
    dt("b_fk", [TT, D], BF16)
    dt("b_v", [TT, D], BF16)
    dt("b_at", [128, 64 * G], BF16)
    dt("b_row", [D], F32)

    sb = {}

    def st(name, shape, dtype):
        sb[name] = E(nc.sbuf_tensor("s_" + name, shape, dtype))
        return sb[name]

    st("identb", [128, 128], BF16)
    st("identf", [128, 128], F32)
    st("maskS", [128, 128], F32)
    st("onesb", [128, 1], BF16)
    st("onesf", [128, 1], F32)
    st("onesrow", [1, 128], F32)
    st("magici", [128, 16], I32)
    st("lnw_cols", [128, 2 * KC], F32)
    st("ln1b_col", [128, KC], BF16)
    st("ln2b_col", [128, KC], BF16)
    st("bias_sc", [128, KC], F32)
    st("fcb_col", [128, CC], F32)
    st("convw_col", [128, 3 * CC], F32)
    st("convb_col", [128, CC], F32)
    st("mbar", [128, 4], F32)
    st("bq_rep", [128, D], BF16)
    st("bk_rep", [128, D], BF16)
    st("bv_rep", [128, D], BF16)
    st("bo_rep", [128, D], BF16)
    st("projb_rep", [128, D], BF16)
    st("wbuf", [128, KC * D], BF16)
    st("wslc", [128, 2 * D], BF16)
    st("wslc2", [128, 2 * D], BF16)
    st("x_tm", [128, MC * D], F32)
    st("xmid_tm", [128, MC * D], F32)
    st("x1a", [128, MC * D], F32)
    st("x1b", [128, MC * D], F32)
    st("tm_sc", [128, MC * D], BF16)
    st("fm_sc", [128, KC * TT], BF16)
    st("stats", [128, 16], F32)
    st("stats2", [128, 16], F32)
    st("sc_a", [128, 16], F32)
    st("sc_b", [128, 16], F32)
    st("sc_c", [128, 16], F32)
    st("qscale", [128, 8], F32)
    st("q2s", [128, 8], F32)
    st("dbgk", [128, 256], F32)
    st("junk", [128, D], F32)
    st("ju2", [128, 1024], F32)
    st("ju3", [128, 1024], F32)

    B = {}
    for i in range(8):
        B[i] = E(nc.psum_tensor(f"bank{i}", [128, 512], F32))

    em = Emitter()
    for s in ["ldc", "wdma", "xdma", "ydma", "bnc", "gth", "mm", "tr", "ev",
              "sv", "gp", "act"]:
        em.sems[s] = E(nc.semaphore(s))
        em.cnt[s] = 0

    def vop(fn, inc="ev"):
        em.op("vector", fn, inc)

    def vopd(fn, inc="ev"):
        em.op("vector", fn, None)
        em.op("vector", lambda e: e.drain(), inc)

    def sop(fn, inc="act"):
        em.op("scalar", fn, inc)

    def sopd(fn, inc="act"):
        em.op("scalar", fn, None)
        em.op("scalar", lambda e: e.drain(), inc)

    def gop(fn, inc="gp"):
        em.op("gpsimd", fn, inc)

    def top(fn, inc="mm"):
        em.op("tensor", fn, inc)

    def dma(eng, out_ap, in_ap, inc):
        em.op(eng, lambda e, o=out_ap, i=in_ap: e.dma_start(out=o, in_=i), inc, 16)

    def dmas(eng, out_ap, in_ap, inc):
        em.op(eng, lambda e, o=out_ap, i=in_ap: e.dma_start(
            out=o, in_=i, allow_slow_non_contiguous=True), inc, 16)

    pro_es = ExitStack()
    sb["wf32b"] = pro_es.enter_context(nc.sbuf_tensor("s_wf32b", [128, MLP], F32))
    sb["wbf"] = pro_es.enter_context(nc.sbuf_tensor("s_wbf", [128, 2 * MLP], BF16))

    for nm in ["identb", "identf", "maskS"]:
        dma("sync", sb[nm][:], dr[nm].ap(), "ldc")
    for j, nm in enumerate(["ln1w", "ln2w"]):
        dmas("sync", bass.AP(sb["lnw_cols"], j * KC, [[2 * KC, 128], [1, KC]]),
            bass.AP(dr[nm], 0, [[1, 128], [128, KC]]), "ldc")
    dmas("sync", bass.AP(sb["convb_col"], 0, [[CC, 128], [1, CC]]),
        bass.AP(dr["convb"], 0, [[1, 128], [128, CC]]), "ldc")
    dmas("gpsimd", sb["bo_rep"][:], bass.AP(dr["bo"], 0, [[0, 128], [1, D]]), "ldc")
    dmas("gpsimd", sb["projb_rep"][:], bass.AP(dr["projb"], 0, [[0, 128], [1, D]]), "ldc")
    dmas("gpsimd", sb["ln1b_col"][:], bass.AP(dr["ln1b"], 0, [[1, 128], [128, KC]]), "ldc")
    dmas("gpsimd", sb["ln2b_col"][:], bass.AP(dr["ln2b"], 0, [[1, 128], [128, KC]]), "ldc")
    em.wait("vector", "ldc")
    vop(lambda e: e.memset(sb["onesb"][:], 1.0), None)
    vop(lambda e: e.memset(sb["onesf"][:], 1.0), None)
    vop(lambda e: e.memset(sb["magici"][:], RSQRT_MAGIC), None)
    vop(lambda e: e.memset(sb["onesrow"][:], 1.0))

    def build_attn_cache(wname, cname, borig, brep):
        dmas("sync", bass.AP(sb["bias_sc"], 0, [[KC, 128], [1, KC]]),
             bass.AP(dr[borig], 0, [[1, 128], [128, KC]]), "wdma")
        em.wait("vector", "wdma")
        ldm, stm, evm, mmm, bacc = {}, {}, {}, {}, {}
        ent_bnc, ent_mm = em.cnt["bnc"], em.cnt["mm"]

        def emit_load(k):
            if k >= 2:
                em.wait("sync", "ev", evm[k - 2])
            dma("sync", sb["wf32b"][:, (k % 2) * D:(k % 2) * D + D],
                dr[wname].ap()[128 * k:128 * k + 128, :], "wdma")
            ldm[k] = em.cnt["wdma"]
            em.wait("sync", "wdma", ldm[k])

        emit_load(0)
        emit_load(1)
        for k in range(KC):
            src = sb["wf32b"][:, (k % 2) * D:(k % 2) * D + D]
            w0 = sb["wbf"][:, (2 * (k % 2)) * D:(2 * (k % 2)) * D + D]
            w1 = sb["wbf"][:, (2 * (k % 2) + 1) * D:(2 * (k % 2) + 1) * D + D]
            em.wait("vector", "wdma", ldm[k])
            em.wait("vector", "bnc", stm[k - 2] if k >= 2 else ent_bnc)
            em.wait("vector", "mm", mmm[k - 2] if k >= 2 else ent_mm)
            lcol = sb["lnw_cols"][:, k:k + 1]
            vop(lambda e, lcol=lcol, src=src, w0=w0: e.tensor_scalar(
                w0, src, lcol, None, OP.mult), None)
            vopd(lambda e, src=src, w1=w1: e.tensor_copy(w1, src))
            evm[k] = em.cnt["ev"]
            if k + 2 < KC:
                emit_load(k + 2)
            em.wait("gpsimd", "ev", evm[k])
            dma("gpsimd", dr[cname].ap()[128 * k:128 * k + 128, :], w0, "bnc")
            stm[k] = em.cnt["bnc"]
            em.wait("gpsimd", "bnc", stm[k])
            em.wait("tensor", "ev", evm[k])
            if k >= 1:
                em.wait("tensor", "ev", bacc[k - 1])
            for c in range(KC):
                top(lambda t, w1=w1, k=k, c=c: t.matmul(
                    B[0][:, c:c + 1], w1[:, 128 * c:128 * c + 128],
                    sb["ln1b_col"][:, k:k + 1], start=True, stop=True))
            mmm[k] = em.cnt["mm"]
            em.wait("vector", "mm", mmm[k])
            vopd(lambda e: e.tensor_tensor(sb["bias_sc"][:], B[0][:, 0:KC],
                                           sb["bias_sc"][:], OP.add))
            bacc[k] = em.cnt["ev"]
        em.wait("sync", "ev")
        dmas("sync", bass.AP(dr["b_row"], 0, [[1, 128], [128, KC]]),
            bass.AP(sb["bias_sc"], 0, [[KC, 128], [1, KC]]), "wdma")
        em.wait("gpsimd", "wdma")
        dma("gpsimd", brep[:], bass.AP(dr["b_row"], 0, [[0, 128], [1, D]]), "wdma")

    if "attn" not in skip:
        build_attn_cache("wq", "c_wq", "bq", sb["bq_rep"])
        build_attn_cache("wk", "c_wk", "bk", sb["bk_rep"])
        build_attn_cache("wv", "c_wv", "bv", sb["bv_rep"])

    dmas("sync", bass.AP(sb["fcb_col"], 0, [[CC, 128], [1, CC]]),
         bass.AP(dr["fcb"], 0, [[1, 128], [128, CC]]), "wdma")
    em.wait("vector", "wdma")
    for k in range(KC):
        dma("sync", sb["wf32b"][:], dr["fcw"].ap()[128 * k:128 * k + 128, :], "wdma")
        em.wait("vector", "wdma")
        lcol = sb["lnw_cols"][:, KC + k:KC + k + 1]
        vop(lambda e, lcol=lcol: e.tensor_scalar(
            sb["wbf"][:, 0:MLP], sb["wf32b"][:], lcol, None, OP.mult), None)
        vopd(lambda e: e.tensor_copy(sb["wbf"][:, MLP:2 * MLP], sb["wf32b"][:]))
        em.wait("sync", "ev")
        dma("sync", bass.AP(dr["c_fcw"], k * 16384,
                            [[128, 128], [KC * 16384, CC], [1, 128]]),
            bass.AP(sb["wbf"], 0, [[2 * MLP, 128], [128, CC], [1, 128]]), "wdma")
        em.wait("tensor", "ev")
        for c in range(CC):
            top(lambda t, k=k, c=c: t.matmul(
                B[1][:, c:c + 1], sb["wbf"][:, MLP + 128 * c:MLP + 128 * c + 128],
                sb["ln2b_col"][:, k:k + 1], start=True, stop=True))
        em.wait("vector", "mm")
        vopd(lambda e: e.tensor_tensor(sb["fcb_col"][:], B[1][:, 0:CC],
                                       sb["fcb_col"][:], OP.add))

    if "casts" not in skip:
        for src_, dst_ in [("wo", "c_wo"), ("projw", "c_projw")]:
            nr = dr[src_].shape[0]
            for k in range(0, nr, 128):
                em.wait("sync", "ev")
                dma("sync", sb["wf32b"][:, 0:D],
                    dr[src_].ap()[k:k + 128, :], "wdma")
                em.wait("vector", "wdma")
                vopd(lambda e: e.tensor_copy(sb["wbf"][:, 0:D],
                                             sb["wf32b"][:, 0:D]))
                em.wait("sync", "ev")
                dma("sync", dr[dst_].ap()[k:k + 128, :], sb["wbf"][:, 0:D],
                    "wdma")

    def build_ternary(src, dst, mi, blocked):
        nchunk = dr[src].shape[0] // 128
        width = dr[src].shape[1]
        vop(lambda e: e.memset(sb["sc_a"][:, 0:1], 0.0), None)
        for k in range(nchunk):
            em.wait("sync", "ev")
            dma("sync", sb["wf32b"][:, 0:width],
                dr[src].ap()[128 * k:128 * k + 128, :], "wdma")
            em.wait("vector", "wdma")
            vop(lambda e, w=width: e.tensor_reduce(
                sb["sc_b"][:, 0:1], sb["wf32b"][:, 0:w], AX.X, OP.add,
                apply_absolute_value=True), None)
            vop(lambda e: e.drain(), None)
            vop(lambda e: e.tensor_tensor(sb["sc_a"][:, 0:1], sb["sc_a"][:, 0:1],
                                          sb["sc_b"][:, 0:1], OP.add), "ev")
        em.wait("tensor", "ev")
        top(lambda t: t.matmul(B[2][0:1, 0:1], sb["onesf"][:], sb["sc_a"][:, 0:1],
                               start=True, stop=True))
        em.wait("vector", "mm")
        tot = float(nchunk * 128 * width)
        vopd(lambda e, tot=tot: e.tensor_scalar(
            sb["sc_b"][0:1, 0:1], B[2][0:1, 0:1], 1.0 / tot, 1e-5,
            OP.mult, OP.max), "ev")
        vop(lambda e: e.drain(), None)
        vopd(lambda e: e.reciprocal(sb["sc_b"][0:1, 1:2], sb["sc_b"][0:1, 0:1]))
        em.wait("sync", "ev")
        dma("sync", bass.AP(dr["b_row"], 0, [[1, 1], [1, 2]]), sb["sc_b"][0:1, 0:2],
            "wdma")
        em.wait("gpsimd", "wdma")
        dma("gpsimd", sb["mbar"][:, mi:mi + 1],
            bass.AP(dr["b_row"], 0, [[0, 128], [1, 1]]), "wdma")
        dma("gpsimd", sb["mbar"][:, mi + 2:mi + 3],
            bass.AP(dr["b_row"], 1, [[0, 128], [1, 1]]), "wdma")
        em.wait("vector", "wdma")
        for k in range(nchunk):
            dma("sync", sb["wf32b"][:, 0:width],
                dr[src].ap()[128 * k:128 * k + 128, :], "wdma")
            em.wait("vector", "wdma")
            scol = sb["mbar"][:, mi + 2:mi + 3]
            vop(lambda e, w=width, scol=scol: e.tensor_scalar(
                sb["wf32b"][:, 0:w], sb["wf32b"][:, 0:w], scol, 1.0,
                OP.mult, OP.min), None)
            vop(lambda e, w=width: e.tensor_scalar(
                sb["wf32b"][:, 0:w], sb["wf32b"][:, 0:w], -1.0, MAGIC,
                OP.max, OP.add), None)
            vopd(lambda e, w=width: e.tensor_scalar(
                sb["wbf"][:, 0:w], sb["wf32b"][:, 0:w], MAGIC, None, OP.subtract))
            em.wait("sync", "ev")
            if blocked:
                ncb = width // 128
                dma("sync", bass.AP(dr[dst], k * 16384,
                                    [[128, 128], [KC * 16384, ncb], [1, 128]]),
                    bass.AP(sb["wbf"], 0, [[2 * MLP, 128], [128, ncb], [1, 128]]),
                    "wdma")
            else:
                dma("sync", dr[dst].ap()[128 * k:128 * k + 128, :],
                    sb["wbf"][:, 0:width], "wdma")
            em.wait("vector", "wdma")

    if "ternary" not in skip:
        build_ternary("w1T", "c_w1t", 0, True)
        build_ternary("w2T", "c_w2t", 1, False)

    for j, nm in enumerate(["convw0", "convw1", "convw2"]):
        dmas("sync", bass.AP(sb["convw_col"], j * CC, [[3 * CC, 128], [1, CC]]),
            bass.AP(dr[nm], 0, [[1, 128], [128, CC]]), "wdma")
    em.wait("vector", "wdma")
    vop(lambda e: e.tensor_scalar(sb["convw_col"][:], sb["convw_col"][:],
                                  sb["mbar"][:, 0:1], None, OP.mult))

    for eng in ("sync", "tensor", "vector", "scalar", "gpsimd"):
        for s in ["wdma", "bnc", "ev", "mm", "gp", "act", "ldc"]:
            em.wait(eng, s)

    pro_es.close()
    HT = TT // 2
    st("fq_g2", [128, 64 * G], BF16)
    st("fk_g2", [128, 64 * G], BF16)
    st("v_g2", [128, 64 * G], BF16)
    st("fqT", [128, 4 * 128], BF16)
    st("fkT", [128, 4 * 128], BF16)
    st("S2", [128, 8 * 128], BF16)
    st("rc_c", [128, G], F32)
    st("attn_g2", [128, 64 * G], BF16)
    st("xq1w_a", [128, KC * WIN], BF16)
    st("xq1w_b", [128, KC * WIN], BF16)
    st("hsil", [128, CC * HT], BF16)
    st("hwin", [128, 2 * 260], BF16)
    st("cv1", [128, 2 * HT], BF16)
    st("cv2", [128, 2 * HT], BF16)
    st("rmax", [128, HT], BF16)
    st("rmin", [128, HT], BF16)
    st("hq_c", [128, 2 * HT], BF16)
    st("gch", [128, 2 * 256], BF16)
    st("s2bc", [128, HT], F32)
    st("s2row", [1, HT], F32)
    vop(lambda e: e.memset(sb["xq1w_a"][:], 0.0), None)
    vop(lambda e: e.memset(sb["xq1w_b"][:], 0.0), "ev")
    for eng in ("sync", "tensor", "scalar", "gpsimd"):
        em.wait(eng, "ev")

    def ln_finish(off):
        ssl = sb["stats"][:, off:off + MC]
        qsl = sb["stats2"][:, off:off + MC]
        a = sb["sc_a"][:, 0:MC]
        b = sb["sc_b"][:, 0:MC]
        vop(lambda e: e.drain(), None)
        vop(lambda e: e.tensor_scalar(ssl, ssl, 1.0 / D, None, OP.mult), None)
        vop(lambda e: e.drain(), None)
        vop(lambda e: e.tensor_tensor(a, ssl, ssl, OP.mult), None)
        vop(lambda e: e.tensor_scalar(qsl, qsl, 1.0 / D, None, OP.mult), None)
        vop(lambda e: e.drain(), None)
        vop(lambda e: e.tensor_tensor(qsl, qsl, a, OP.subtract), None)
        vop(lambda e: e.drain(), None)
        vop(lambda e: e.tensor_scalar(qsl, qsl, 1e-5, None, OP.add), "ev")
        em.wait("scalar", "ev")
        sop(lambda s: s.activation(a, qsl, AF.Ln), "act")
        sop(lambda s: s.activation(qsl, a, AF.Exp, scale=-0.5), "act")
        em.wait("vector", "act")

    def ln_stats(tile, off):
        em.wait("scalar", "ev")
        for m in range(MC):
            sl = tile[:, D * m:D * (m + 1)]
            vop(lambda e, sl=sl, m=m: e.tensor_reduce(
                sb["stats"][:, off + m:off + m + 1], sl, AX.X, OP.add), None)
            sop(lambda s, sl=sl, m=m: s.activation(
                sb["junk"][:], sl, AF.Square,
                accum_out=sb["stats2"][:, off + m:off + m + 1]), "act")
        em.wait("vector", "act")
        ln_finish(off)

    def ln_norm(src, dst, off):
        for m in range(MC):
            ssl = src[:, D * m:D * (m + 1)]
            dsl = dst[:, D * m:D * (m + 1)]
            mcol = sb["stats"][:, off + m:off + m + 1]
            rcol = sb["stats2"][:, off + m:off + m + 1]
            last = (m == MC - 1)
            (vopd if last else vop)(
                lambda e, ssl=ssl, dsl=dsl, mcol=mcol, rcol=rcol: e.tensor_scalar(
                    dsl, ssl, mcol, rcol, OP.subtract, OP.mult))

    def tm_to_fm(tm_tile, fm_tile, fstride=TT, foff=0):
        em.wait("tensor", "ev", em.cnt["ev"])
        fsz = fm_tile.shape[1]
        ngrp = MC * KC // 4
        evmark = [em.cnt["sv"]] * ngrp
        for grp in range(ngrp):
            m, k0 = grp // 2, (grp % 2) * 4
            bank = B[4 + (grp % 4)]
            em.wait("tensor", "sv", evmark[grp - 2] if grp >= 2 else evmark[0])
            for q in range(4):
                k = k0 + q
                pv = bank.ap().bitcast(BF16)[:, 128 * q:128 * q + 128]
                top(lambda t, m=m, k=k, pv=pv, tm=tm_tile: t.transpose(
                    pv, tm[:, D * m + 128 * k:D * m + 128 * k + 128],
                    sb["identb"][:]), "tr")
            trmark = em.cnt["tr"]
            pvfull = bank.ap().bitcast(BF16)[:, 0:512]
            dsl = bass.AP(fm_tile, foff + fstride * k0 + 128 * m,
                          [[fsz, 128], [fstride, 4], [1, 128]])
            em.wait("vector", "tr", trmark)
            vopd(lambda e, dsl=dsl, pvfull=pvfull: e.tensor_copy(dsl, pvfull),
                 "sv")
            evmark[grp] = em.cnt["sv"]

    x_d = dr["x"].ap()
    y_d = dr["y"].ap()
    FS = 64 * G

    def full_barrier():
        for eng in ("sync", "tensor", "vector", "scalar", "gpsimd"):
            for s in ["wdma", "xdma", "ydma", "bnc", "gth", "mm", "tr",
                      "ev", "sv", "gp", "act"]:
                em.wait(eng, s)

    def phase_attn(ti):
        if "attn" in barrier:
            full_barrier()
        x1cur = sb["x1a"] if ti % 2 == 0 else sb["x1b"]
        wcur = sb["xq1w_a"] if ti % 2 == 0 else sb["xq1w_b"]
        wprev = sb["xq1w_b"] if ti % 2 == 0 else sb["xq1w_a"]

        em.wait("sync", "ydma")
        em.wait("sync", "ev")
        for m in range(MC):
            dma("sync", sb["x_tm"][:, D * m:D * (m + 1)],
                x_d[ti * TT + 128 * m:ti * TT + 128 * (m + 1), :], "xdma")
        em.wait("vector", "xdma")
        em.wait("scalar", "xdma")

        ln_stats(sb["x_tm"], 0)
        ln_norm(sb["x_tm"], sb["tm_sc"], 0)
        em.wait("tensor", "ev")
        tm_to_fm(sb["tm_sc"], sb["fm_sc"])

        slot_mark = [em.cnt["mm"]] * 8
        bank_mark = [em.cnt["ev"]] * 4
        tmsc_mark = [em.cnt["bnc"]] * MC
        gath_marks = {}
        for pi, (cname, brep, do_elu, bname) in enumerate([
                ("c_wq", sb["bq_rep"], True, "b_fq"),
                ("c_wk", sb["bk_rep"], True, "b_fk"),
                ("c_wv", sb["bv_rep"], False, "b_v")]):
            ld_mark = []
            for k in range(KC):
                g = pi * KC + k
                em.wait("sync", "mm", slot_mark[g % 8])
                dma("sync", sb["wbuf"][:, D * (g % 8):D * (g % 8 + 1)],
                    dr[cname].ap()[128 * k:128 * k + 128, :], "wdma")
                ld_mark.append(em.cnt["wdma"])
            for k in range(KC):
                g = pi * KC + k
                em.wait("tensor", "wdma", ld_mark[k])
                if k == 0:
                    for b in range(4):
                        em.wait("tensor", "ev", bank_mark[b])
                    if pi == 0:
                        em.wait("tensor", "sv")
                for m in range(MC):
                    for h2 in range(2):
                        lhsT = sb["fm_sc"][:, TT * k + 128 * m:
                                           TT * k + 128 * m + 128]
                        top(lambda t, lhsT=lhsT, m=m, h2=h2, g=g, k=k: t.matmul(
                            B[2 * m + h2][:, :], lhsT,
                            sb["wbuf"][:, D * (g % 8) + 512 * h2:
                                       D * (g % 8) + 512 * (h2 + 1)],
                            start=(k == 0), stop=(k == KC - 1)))
            mm_done = em.cnt["mm"]
            for k in range(KC):
                slot_mark[(pi * KC + k) % 8] = mm_done
            for m in range(MC):
                em.wait("vector", "mm", mm_done)
                em.wait("vector", "bnc", tmsc_mark[m])
                for h2 in range(2):
                    psl = B[2 * m + h2][:, :]
                    osl = sb["tm_sc"][:, D * m + 512 * h2:D * m + 512 * (h2 + 1)]
                    rsl = brep[:, 512 * h2:512 * (h2 + 1)]
                    ju2 = sb["ju2"][:, 512 * h2:512 * h2 + 512]
                    ju3 = sb["ju3"][:, 512 * h2:512 * h2 + 512]
                    jnk = sb["junk"][:, 512 * h2:512 * h2 + 512]
                    if do_elu:
                        vopd(lambda e, psl=psl, rsl=rsl, ju2=ju2:
                             e.scalar_tensor_tensor(ju2, psl, 1.0, rsl,
                                                    OP.mult, OP.add), "ev")
                        em.wait("scalar", "ev", em.cnt["ev"])
                        sop(lambda s, ju2=ju2, jnk=jnk: s.activation(
                            jnk, ju2, AF.Exp), "act")
                        sopd(lambda s, ju2=ju2, ju3=ju3: s.activation(
                            ju3, ju2, AF.Relu), "act")
                        em.wait("vector", "act", em.cnt["act"])
                        vopd(lambda e, osl=osl, jnk=jnk, ju3=ju3:
                             e.scalar_tensor_tensor(osl, jnk, 1.0, ju3,
                                                    OP.min, OP.add), "ev")
                    else:
                        vopd(lambda e, psl=psl, rsl=rsl, osl=osl:
                             e.scalar_tensor_tensor(osl, psl, 1.0, rsl,
                                                    OP.mult, OP.add), "ev")
                bank_mark[m] = em.cnt["ev"]
                em.wait("scalar", "ev", em.cnt["ev"])
                if m == 0:
                    em.wait("scalar", "gth", em.cnt["gth"])
                dma("scalar", dr[bname].ap()[128 * m:128 * (m + 1), :],
                    sb["tm_sc"][:, D * m:D * (m + 1)], "bnc")
                tmsc_mark[m] = em.cnt["bnc"]
            qeng = ("gpsimd",) if pi < 2 else ("gpsimd", "scalar")
            for eng in qeng:
                em.wait(eng, "bnc", em.cnt["bnc"])
                em.wait(eng, "mm", em.cnt["mm"])
            tname = {"b_fq": "fq_g2", "b_fk": "fk_g2", "b_v": "v_g2"}[bname]
            for h in range(H):
                dma(qeng[h % len(qeng)],
                    bass.AP(sb[tname], h * FS, [[16 * FS, 8], [64, G], [1, 64]]),
                    bass.AP(dr[bname], 64 * h, [[D, 8], [8 * D, G], [1, 64]]),
                    "gth")
            gath_marks[tname] = em.cnt["gth"]

        em.wait("tensor", "gth")
        b01_mark = em.cnt["sv"]
        s2e_mark = [em.cnt["sv"]] * 2
        st1_mark = em.cnt["mm"]
        ave_mark = em.cnt["ev"]
        mb4 = bass.AP(sb["maskS"], 0, [[128, 128], [0, 4], [1, 128]])
        for b8 in range(G // 8):
            sA = B[4] if b8 % 2 == 0 else B[6]
            sB = B[5] if b8 % 2 == 0 else B[7]
            pvq = B[0].ap().bitcast(BF16)
            pvk = B[1].ap().bitcast(BF16)
            em.wait("tensor", "sv", b01_mark)
            for jj in range(4):
                j = b8 * 4 + jj
                top(lambda t, j=j, jj=jj, pvq=pvq: t.transpose(
                    pvq[:, 128 * jj:128 * jj + 128],
                    sb["fq_g2"][:, 128 * j:128 * j + 128], sb["identb"][:]), "tr")
                top(lambda t, j=j, jj=jj, pvk=pvk: t.transpose(
                    pvk[:, 128 * jj:128 * jj + 128],
                    sb["fk_g2"][:, 128 * j:128 * j + 128], sb["identb"][:]), "tr")
            trm = em.cnt["tr"]
            em.wait("vector", "tr", trm)
            em.wait("vector", "mm", st1_mark)
            vopd(lambda e, pvq=pvq: e.tensor_copy(sb["fqT"][:, 0:512],
                                                  pvq[:, 0:512]), "sv")
            vopd(lambda e, pvk=pvk: e.tensor_copy(sb["fkT"][:, 0:512],
                                                  pvk[:, 0:512]), "sv")
            b01_mark = em.cnt["sv"]
            em.wait("tensor", "sv", b01_mark)
            em.wait("tensor", "sv", s2e_mark[b8 % 2])
            for jj in range(4):
                qs = sb["fqT"][:, 128 * jj:128 * jj + 128]
                ks = sb["fkT"][:, 128 * jj:128 * jj + 128]
                for dd in range(2):
                    bank = sA if dd == 0 else sB
                    top(lambda t, qs=qs, ks=ks, dd=dd, bank=bank, jj=jj:
                        t.matmul(bank[:, 128 * jj:128 * jj + 128],
                                 ks[64 * dd:64 * dd + 64, :],
                                 qs[64 * dd:64 * dd + 64, :],
                                 start=True, stop=True))
            st1_mark = em.cnt["mm"]
            em.wait("vector", "mm", st1_mark)
            vop(lambda e, sA=sA: e.tensor_tensor(
                sb["S2"][:, 0:512].rearrange("p (g c) -> p g c", g=4),
                sA[:].rearrange("p (g c) -> p g c", g=4), mb4, OP.mult), None)
            vopd(lambda e, sB=sB: e.tensor_tensor(
                sb["S2"][:, 512:1024].rearrange("p (g c) -> p g c", g=4),
                sB[:].rearrange("p (g c) -> p g c", g=4), mb4, OP.mult), "sv")
            s2e_mark[b8 % 2] = em.cnt["sv"]
            em.wait("tensor", "sv", s2e_mark[b8 % 2])
            em.wait("tensor", "ev", ave_mark)
            for gg in range(8):
                jj, dd = gg // 2, gg % 2
                gi = b8 * 8 + 2 * jj + dd
                s2sl = sb["S2"][:, 512 * dd + 128 * jj:512 * dd + 128 * jj + 128]
                top(lambda t, s2sl=s2sl, gi=gi: t.matmul(
                    B[3][:, gi:gi + 1], s2sl, sb["onesb"][:],
                    start=True, stop=True))
                top(lambda t, s2sl=s2sl, gi=gi, gg=gg: t.matmul(
                    B[2][:, 64 * gg:64 * gg + 64], s2sl,
                    sb["v_g2"][:, 64 * gi:64 * gi + 64], start=True, stop=True))
            st2m = em.cnt["mm"]
            em.wait("vector", "mm", st2m)
            rsl = sb["rc_c"][:, 8 * b8:8 * b8 + 8]
            vop(lambda e, b8=b8: e.tensor_scalar(
                sb["sc_c"][:, 0:8], B[3][:, 8 * b8:8 * b8 + 8], 1e-6, None,
                OP.add), None)
            vop(lambda e: e.drain(), None)
            vop(lambda e, rsl=rsl: e.reciprocal(rsl, sb["sc_c"][:, 0:8]), None)
            vop(lambda e: e.drain(), None)
            rbc = bass.AP(sb["rc_c"], 8 * b8, [[G, 128], [1, 8], [0, 64]])
            vopd(lambda e, b8=b8, rbc=rbc: e.tensor_tensor(
                sb["attn_g2"][:, 512 * b8:512 * (b8 + 1)].rearrange(
                    "p (g e) -> p g e", g=8),
                B[2][:].rearrange("p (g e) -> p g e", g=8), rbc, OP.mult), "ev")
            ave_mark = em.cnt["ev"]

        em.wait("scalar", "ev")
        em.wait("scalar", "gth", em.cnt["gth"])
        dma("scalar", dr["b_at"].ap(), sb["attn_g2"][:], "bnc")
        gengs = ("scalar", "gpsimd", "sync")
        for eng in gengs:
            em.wait(eng, "bnc", em.cnt["bnc"])
        for gi in range(G):
            m = (8 * gi) // 128
            r0 = (8 * gi) % 128
            out_ap = bass.AP(sb["tm_sc"], r0 * (MC * D) + D * m,
                             [[MC * D, 8], [64, 16], [1, 64]])
            in_ap = bass.AP(dr["b_at"], 64 * gi, [[16 * FS, 8], [FS, 16], [1, 64]])
            dma(gengs[gi % 3], out_ap, in_ap, "gth")
        em.wait("tensor", "gth")
        em.wait("vector", "gth")
        em.wait("scalar", "gth")
        tm_to_fm(sb["tm_sc"], sb["fm_sc"])

        em.wait("sync", "mm")
        for k in range(KC):
            dma("sync", sb["wbuf"][:, D * k:D * (k + 1)],
                dr["c_wo"].ap()[128 * k:128 * k + 128, :], "wdma")
        em.wait("tensor", "wdma")
        em.wait("tensor", "sv")
        for m in range(MC):
            pb = (B[2 * (m % 2)], B[2 * (m % 2) + 1])
            em.wait("tensor", "ev")
            for k in range(KC):
                lhsT = sb["fm_sc"][:, TT * k + 128 * m:TT * k + 128 * m + 128]
                for h2 in range(2):
                    top(lambda t, lhsT=lhsT, k=k, h2=h2, pb=pb: t.matmul(
                        pb[h2][:, :], lhsT,
                        sb["wbuf"][:, D * k + 512 * h2:D * k + 512 * (h2 + 1)],
                        start=(k == 0), stop=(k == KC - 1)))
            em.wait("vector", "mm")
            for h2 in range(2):
                psl = pb[h2][:, :]
                bsl = sb["bo_rep"][:, 512 * h2:512 * (h2 + 1)]
                xsl = sb["x_tm"][:, D * m + 512 * h2:D * m + 512 * (h2 + 1)]
                osl = sb["xmid_tm"][:, D * m + 512 * h2:D * m + 512 * (h2 + 1)]
                vop(lambda e, psl=psl, bsl=bsl: e.scalar_tensor_tensor(
                    sb["ju2"][:, 0:512], psl, 1.0, bsl, OP.mult, OP.add), None)
                vopd(lambda e, osl=osl, xsl=xsl: e.tensor_tensor(
                    osl, sb["ju2"][:, 0:512], xsl, OP.add), "ev")

        ln_stats(sb["xmid_tm"], 4)
        ln_norm(sb["xmid_tm"], sb["tm_sc"], 4)
        em.wait("tensor", "ev")
        tm_to_fm(sb["tm_sc"], sb["fm_sc"])

        em.wait("tensor", "sv")
        gelu_mark = [em.cnt["act"]] * 2
        fcslice_mark = [em.cnt["mm"]] * 2
        pjslice_mark = [em.cnt["mm"]] * 2
        fcev = [em.cnt["ev"]] * 2
        for half in range(2):
            projmm_marks = {}
            pend_proj = None

            def emit_proj(pp):
                gsl, wpsl, gm, pjw_ld, c = pp
                em.wait("tensor", "wdma", pjw_ld)
                em.wait("tensor", "act", gm)
                for mloc in range(2):
                    for h2 in range(2):
                        top(lambda t, gsl=gsl, mloc=mloc, h2=h2, wpsl=wpsl, c=c:
                            t.matmul(B[2 * mloc + h2][:, :],
                                     gsl[:, 128 * mloc:128 * mloc + 128],
                                     wpsl[:, 512 * h2:512 * (h2 + 1)],
                                     start=(c == 0), stop=(c == CC - 1)))
                projmm_marks[c] = em.cnt["mm"]
                pjslice_mark[c % 2] = em.cnt["mm"]

            for c in range(CC):
                wsl = sb["wslc2"][:, D * (c % 2):D * (c % 2) + D]
                em.wait("sync", "mm", fcslice_mark[c % 2])
                dma("sync", wsl,
                    bass.AP(dr["c_fcw"], c * KC * 16384,
                            [[128, 128], [16384, KC], [1, 128]]), "wdma")
                fcw_ld = em.cnt["wdma"]
                em.wait("sync", "wdma", fcw_ld)
                wpsl = sb["wslc"][:, D * (c % 2):D * (c % 2) + D]
                em.wait("sync", "mm", pjslice_mark[c % 2])
                dma("sync", wpsl, dr["c_projw"].ap()[128 * c:128 * c + 128, :],
                    "wdma")
                em.wait("sync", "wdma", em.cnt["wdma"])
                pjw_ld = em.cnt["wdma"]
                fcbank = B[4 + (c % 2)]
                em.wait("tensor", "wdma", fcw_ld)
                em.wait("tensor", "ev", fcev[c % 2])
                for k in range(KC):
                    rhs = sb["fm_sc"][:, TT * k + 256 * half:
                                      TT * k + 256 * half + 256]
                    top(lambda t, wsl=wsl, k=k, rhs=rhs, fcbank=fcbank: t.matmul(
                        fcbank[:, 0:256], wsl[:, 128 * k:128 * k + 128], rhs,
                        start=(k == 0), stop=(k == KC - 1)))
                fcmm = em.cnt["mm"]
                fcslice_mark[c % 2] = fcmm
                gssl = sb["ju3"][:, 256 * (c % 2):256 * (c % 2) + 256]
                em.wait("vector", "mm", fcmm)
                em.wait("vector", "act", gelu_mark[c % 2])
                vopd(lambda e, gssl=gssl, fcbank=fcbank: e.tensor_copy(
                    gssl, fcbank[:, 0:256]), "ev")
                fcev[c % 2] = em.cnt["ev"]
                gsl = sb["gch"][:, 256 * (c % 2):256 * (c % 2) + 256]
                em.wait("scalar", "ev", fcev[c % 2])
                if c >= 2:
                    em.wait("scalar", "mm", projmm_marks[c - 2])
                sopd(lambda s, gsl=gsl, gssl=gssl, c=c: s.activation(
                    gsl, gssl, AF.Gelu,
                    bias=sb["fcb_col"][:, c:c + 1], scale=1.0), "act")
                gelu_mark[c % 2] = em.cnt["act"]
                if pend_proj is not None:
                    emit_proj(pend_proj)
                pend_proj = (gsl, wpsl, gelu_mark[c % 2], pjw_ld, c)
            emit_proj(pend_proj)
            em.wait("vector", "mm", em.cnt["mm"])
            for mloc in range(2):
                m = 2 * half + mloc
                for h2 in range(2):
                    psl = B[2 * mloc + h2][:, :]
                    bsl = sb["projb_rep"][:, 512 * h2:512 * (h2 + 1)]
                    midsl = sb["xmid_tm"][:, D * m + 512 * h2:D * m + 512 * (h2 + 1)]
                    xsl = sb["x_tm"][:, D * m + 512 * h2:D * m + 512 * (h2 + 1)]
                    vop(lambda e, psl=psl, bsl=bsl: e.scalar_tensor_tensor(
                        sb["ju2"][:, 0:512], psl, 1.0, bsl, OP.mult, OP.add), None)
                    vop(lambda e, midsl=midsl: e.tensor_tensor(
                        sb["ju3"][:, 0:512], sb["ju2"][:, 0:512], midsl, OP.add),
                        None)
                    vopd(lambda e, xsl=xsl: e.tensor_tensor(
                        xsl, sb["ju3"][:, 0:512], xsl, OP.add), "ev")
            em.wait("tensor", "ev", em.cnt["ev"])

        ln_stats(sb["x_tm"], 0)
        ln_norm(sb["x_tm"], x1cur, 0)

        for m in range(MC):
            sl = x1cur[:, D * m:D * (m + 1)]
            vop(lambda e, sl=sl, m=m: e.tensor_reduce(
                sb["qscale"][:, m:m + 1], sl, AX.X, OP.max,
                apply_absolute_value=True), None)
        vop(lambda e: e.drain(), None)
        vop(lambda e: e.tensor_scalar(sb["qscale"][:, 0:MC], sb["qscale"][:, 0:MC],
                                      1e-5, None, OP.max), None)
        vop(lambda e: e.drain(), None)
        vop(lambda e: e.reciprocal(sb["sc_c"][:, 0:MC], sb["qscale"][:, 0:MC]), None)
        vop(lambda e: e.tensor_scalar(sb["qscale"][:, 4:4 + MC],
                                      sb["qscale"][:, 0:MC], 1.0 / 127.0, None,
                                      OP.mult), None)
        vop(lambda e: e.drain(), None)
        vop(lambda e: e.tensor_scalar(sb["qscale"][:, 0:MC], sb["sc_c"][:, 0:MC],
                                      127.0, None, OP.mult), None)
        vop(lambda e: e.drain(), None)
        for m in range(MC):
            ssl = x1cur[:, D * m:D * (m + 1)]
            scol = sb["qscale"][:, m:m + 1]
            dcol = sb["qscale"][:, 4 + m:5 + m]
            osl = sb["tm_sc"][:, D * m:D * (m + 1)]
            vop(lambda e, ssl=ssl, scol=scol: e.tensor_scalar(
                sb["junk"][:], ssl, scol, MAGIC, OP.mult, OP.add), None)
            vopd(lambda e, osl=osl, dcol=dcol: e.tensor_scalar(
                osl, sb["junk"][:], MAGIC, dcol, OP.subtract, OP.mult), "ev")
        em.wait("tensor", "ev")
        tm_to_fm(sb["tm_sc"], wcur, fstride=WIN, foff=1)
        em.wait("vector", "sv")
        for k in range(KC):
            if ti > 0:
                vop(lambda e, k=k, wc=wcur, wp=wprev: e.tensor_copy(
                    wc[:, WIN * k:WIN * k + 1], wp[:, WIN * k + TT:WIN * k + TT + 1]),
                    None)
            (vopd if k == KC - 1 else vop)(
                lambda e, k=k, wc=wcur, wp=wprev: e.tensor_copy(
                    wp[:, WIN * k + TT + 1:WIN * k + TT + 2],
                    wc[:, WIN * k + 1:WIN * k + 2]), "ev")

    def phase_ffn(tj, last=False):
        if "ffn" in barrier:
            full_barrier()
        x1p = sb["x1a"] if tj % 2 == 0 else sb["x1b"]
        wp = sb["xq1w_a"] if tj % 2 == 0 else sb["xq1w_b"]
        HT = TT // 2
        if last:
            for k in range(KC):
                (vopd if k == KC - 1 else vop)(lambda e, k=k, wp=wp: e.memset(
                    wp[:, WIN * k + TT + 1:WIN * k + TT + 2], 0.0), "ev")
            em.wait("tensor", "ev")
        em.wait("tensor", "sv")
        for half in range(2):
            vop(lambda e: e.memset(sb["rmax"][:], 0.0), None)
            vop(lambda e: e.memset(sb["rmin"][:], 0.0), "ev")
            w1slice_mark = [em.cnt["mm"]] * 2
            hbank_mark = [em.cnt["ev"]] * 2
            convslot_mark = [em.cnt["act"]] * 2
            w1mm_marks = []
            for c in range(CC):
                wsl = sb["wslc2"][:, D * (c % 2):D * (c % 2) + D]
                em.wait("sync", "mm", w1slice_mark[c % 2])
                dma("sync", wsl,
                    bass.AP(dr["c_w1t"], c * KC * 16384,
                            [[128, 128], [16384, KC], [1, 128]]), "wdma")
                ldm = em.cnt["wdma"]
                bA = B[4 + (c % 2)]
                em.wait("tensor", "wdma", ldm)
                em.wait("tensor", "ev", hbank_mark[c % 2])
                for k in range(KC):
                    lh = wsl[:, 128 * k:128 * k + 128]
                    top(lambda t, lh=lh, bA=bA, wp=wp, k=k, half=half: t.matmul(
                        bA[:, 0:258], lh,
                        wp[:, WIN * k + HT * half:WIN * k + HT * half + 258],
                        start=(k == 0), stop=(k == KC - 1)))
                w1mm_marks.append(em.cnt["mm"])
                w1slice_mark[c % 2] = em.cnt["mm"]
                hw = sb["hwin"][:, 260 * (c % 2):260 * (c % 2) + 258]
                cva = sb["cv1"][:, 256 * (c % 2):256 * (c % 2) + 256]
                cvb = sb["cv2"][:, 256 * (c % 2):256 * (c % 2) + 256]
                em.wait("vector", "mm", w1mm_marks[-1])
                em.wait("vector", "act", convslot_mark[c % 2])
                vopd(lambda e, bA=bA, hw=hw: e.tensor_copy(hw, bA[:, 0:258]),
                     "ev")
                hbank_mark[c % 2] = em.cnt["ev"]
                w0c = sb["convw_col"][:, c:c + 1]
                w1c = sb["convw_col"][:, CC + c:CC + c + 1]
                w2c = sb["convw_col"][:, 2 * CC + c:2 * CC + c + 1]
                vop(lambda e, w1c=w1c, hw=hw, cva=cva: e.tensor_scalar(
                    cva, hw[:, 1:257], w1c, None, OP.mult), None)
                vop(lambda e, w0c=w0c, hw=hw, cva=cva, cvb=cvb:
                    e.scalar_tensor_tensor(cvb, hw[:, 0:256], w0c, cva,
                                           OP.mult, OP.add), None)
                vopd(lambda e, w2c=w2c, hw=hw, cva=cva, cvb=cvb:
                     e.scalar_tensor_tensor(cva, hw[:, 2:258], w2c, cvb,
                                            OP.mult, OP.add), "ev")
                cvm = em.cnt["ev"]
                em.wait("scalar", "ev", cvm)
                hs = sb["hsil"][:, HT * c:HT * (c + 1)]
                sopd(lambda s, hs=hs, c=c, cva=cva: s.activation(
                    hs, cva, AF.Silu, bias=sb["convb_col"][:, c:c + 1],
                    scale=1.0), "act")
                convslot_mark[c % 2] = em.cnt["act"]
                em.wait("vector", "act", em.cnt["act"])
                vop(lambda e, hs=hs: e.tensor_tensor(sb["rmax"][:], sb["rmax"][:],
                                                     hs, OP.max), None)
                vop(lambda e, hs=hs: e.tensor_tensor(sb["rmin"][:], sb["rmin"][:],
                                                     hs, OP.min), "ev")

            em.wait("tensor", "ev", em.cnt["ev"])
            for mloc in range(2):
                m = 2 * half + mloc
                pvA = B[6].ap().bitcast(BF16)[:, 0:128]
                pvB = B[6].ap().bitcast(BF16)[:, 128:256]
                top(lambda t, mloc=mloc, pvA=pvA: t.transpose(
                    pvA, sb["rmax"][:, 128 * mloc:128 * mloc + 128],
                    sb["identb"][:]), "tr")
                top(lambda t, mloc=mloc, pvB=pvB: t.transpose(
                    pvB, sb["rmin"][:, 128 * mloc:128 * mloc + 128],
                    sb["identb"][:]), "tr")
                em.wait("vector", "tr")
                vop(lambda e, pvA=pvA: e.tensor_reduce(
                    sb["sc_a"][:, 0:1], pvA, AX.X, OP.max,
                    apply_absolute_value=True), None)
                vopd(lambda e, pvB=pvB: e.tensor_reduce(
                    sb["sc_a"][:, 1:2], pvB, AX.X, OP.max,
                    apply_absolute_value=True), "ev")
                vopd(lambda e, m=m: e.tensor_tensor(
                    sb["q2s"][:, m:m + 1], sb["sc_a"][:, 0:1], sb["sc_a"][:, 1:2],
                    OP.max), "sv")
                em.wait("tensor", "sv")
            c0 = 2 * half
            vop(lambda e: e.drain(), None)
            vop(lambda e, c0=c0: e.tensor_scalar(
                sb["q2s"][:, c0:c0 + 2], sb["q2s"][:, c0:c0 + 2], 1e-5, None,
                OP.max), None)
            vop(lambda e: e.drain(), None)
            vop(lambda e, c0=c0: e.reciprocal(sb["sc_c"][:, 0:2],
                                              sb["q2s"][:, c0:c0 + 2]), None)
            vop(lambda e, c0=c0: e.tensor_scalar(
                sb["q2s"][:, 4 + c0:6 + c0], sb["q2s"][:, c0:c0 + 2],
                1.0 / 127.0, None, OP.mult), None)
            vop(lambda e: e.drain(), None)
            vop(lambda e, c0=c0: e.tensor_tensor(
                sb["q2s"][:, 4 + c0:6 + c0], sb["q2s"][:, 4 + c0:6 + c0],
                sb["mbar"][:, 1:2].broadcast_to([128, 2]), OP.mult), None)
            vopd(lambda e, c0=c0: e.tensor_scalar(
                sb["q2s"][:, c0:c0 + 2], sb["sc_c"][:, 0:2], 127.0, None,
                OP.mult), "ev")
            em.wait("tensor", "ev")
            for mloc in range(2):
                m = 2 * half + mloc
                top(lambda t, m=m, mloc=mloc: t.transpose(
                    B[7][0:1, 128 * mloc:128 * mloc + 128], sb["q2s"][:, m:m + 1],
                    sb["identf"][:]), "tr")
            em.wait("vector", "tr")
            vopd(lambda e: e.tensor_copy(sb["s2row"][:], B[7][0:1, 0:HT]), "sv")
            em.wait("tensor", "sv")
            top(lambda t: t.matmul(B[7][:, 0:HT], sb["onesrow"][:], sb["s2row"][:],
                                   start=True, stop=True))
            em.wait("vector", "mm")
            vopd(lambda e: e.tensor_copy(sb["s2bc"][:], B[7][:, 0:HT]), "ev")

            em.wait("tensor", "ev", em.cnt["ev"])
            w2slice_mark = [em.cnt["mm"]] * 2
            hq_mark = [em.cnt["mm"]] * 2
            for c in range(CC):
                hs = sb["hsil"][:, HT * c:HT * (c + 1)]
                hq = sb["hq_c"][:, HT * (c % 2):HT * (c % 2) + HT]
                em.wait("vector", "mm", hq_mark[c % 2])
                vop(lambda e, hs=hs: e.tensor_tensor(
                    sb["ju2"][:, 0:HT], hs, sb["s2bc"][:], OP.mult), None)
                vopd(lambda e, hq=hq: e.tensor_scalar(
                    hq, sb["ju2"][:, 0:HT], MAGIC, MAGIC, OP.add, OP.subtract),
                    "ev")
                rndm = em.cnt["ev"]
                wsl = sb["wslc"][:, D * (c % 2):D * (c % 2) + D]
                em.wait("sync", "mm", w2slice_mark[c % 2])
                dma("sync", wsl, dr["c_w2t"].ap()[128 * c:128 * c + 128, :], "wdma")
                ldm = em.cnt["wdma"]
                em.wait("tensor", "wdma", ldm)
                em.wait("tensor", "ev", rndm)
                for mloc in range(2):
                    for h2 in range(2):
                        top(lambda t, hq=hq, mloc=mloc, h2=h2, wsl=wsl, c=c:
                            t.matmul(B[2 * mloc + h2][:, :],
                                     hq[:, 128 * mloc:128 * mloc + 128],
                                     wsl[:, 512 * h2:512 * (h2 + 1)],
                                     start=(c == 0), stop=(c == CC - 1)))
                hq_mark[c % 2] = em.cnt["mm"]
                w2slice_mark[c % 2] = em.cnt["mm"]
            em.wait("vector", "mm")
            for mloc in range(2):
                m = 2 * half + mloc
                for h2 in range(2):
                    psl = B[2 * mloc + h2][:, :]
                    x1sl = x1p[:, D * m + 512 * h2:D * m + 512 * (h2 + 1)]
                    osl = sb["xmid_tm"][:, D * m + 512 * h2:D * m + 512 * (h2 + 1)]
                    dcol = sb["q2s"][:, 4 + m:5 + m]
                    vopd(lambda e, psl=psl, x1sl=x1sl, osl=osl, dcol=dcol:
                         e.scalar_tensor_tensor(osl, psl, dcol, x1sl,
                                                OP.mult, OP.add), "ev")
            em.wait("tensor", "ev")

        ln_stats(sb["xmid_tm"], 8)
        ln_norm(sb["xmid_tm"], sb["x_tm"], 8)
        em.wait("sync", "ev")
        for m in range(MC):
            dma("sync", y_d[tj * TT + 128 * m:tj * TT + 128 * (m + 1), :],
                sb["x_tm"][:, D * m:D * (m + 1)], "ydma")

    with nc.Block() as block:
        for ti in range(nt_limit):
            phase_attn(ti)
            if ti > 0 and do_ffn:
                phase_ffn(ti - 1)
        if do_ffn:
            phase_ffn(nt_limit - 1, last=True)
        if dump:
            for eng in ("sync", "tensor", "vector", "scalar", "gpsimd"):
                for s in ["wdma", "xdma", "ydma", "bnc", "gth", "mm", "tr",
                          "ev", "sv", "gp", "act"]:
                    em.wait(eng, s)
            for nm in dump:
                t = sb[nm]
                shp = list(t.shape)
                dbg = nc.dram_tensor("dbg_" + nm, shp, t.dtype,
                                     kind="ExternalOutput")
                dma("sync", dbg.ap(), t[:], "ydma")
        em.wait("sync", "ydma")
        em.wait("gpsimd", "ydma")

        @block.sync
        def _(e):
            for fn in em.lists["sync"]:
                fn(e)

        @block.tensor
        def _(e):
            for fn in em.lists["tensor"]:
                fn(e)

        @block.vector
        def _(e):
            for fn in em.lists["vector"]:
                fn(e)

        @block.scalar
        def _(e):
            for fn in em.lists["scalar"]:
                fn(e)

        @block.gpsimd
        def _(e):
            for fn in em.lists["gpsimd"]:
                fn(e)

    es.close()
    return nc


_NC_CACHE = None


def _get_nc():
    global _NC_CACHE
    if _NC_CACHE is None:
        _NC_CACHE = build_program()
    return _NC_CACHE


def kernel(**inputs):
    ii = {k: np.asarray(v) for k, v in inputs.items()}
    x = np.ascontiguousarray(ii["L0_attn_q_x"].astype(np.float32, copy=False))
    shared = {
        "wq": ii["L0_attn_wq"], "wk": ii["L0_attn_wk"], "wv": ii["L0_attn_wv"],
        "wo": ii["L0_attn_wo"], "fcw": ii["L0_attn_fc_w"],
        "projw": ii["L0_attn_proj_w"],
        "w1T": np.ascontiguousarray(ii["L0_ffn_w1"].T),
        "w2T": np.ascontiguousarray(ii["L0_ffn_w2"].T),
        "ln1w": ii["L0_attn_ln1_w"], "ln1b": ii["L0_attn_ln1_b"],
        "ln2w": ii["L0_attn_ln2_w"], "ln2b": ii["L0_attn_ln2_b"],
        "bq": ii["L0_attn_bq"], "bk": ii["L0_attn_bk"], "bv": ii["L0_attn_bv"],
        "bo": ii["L0_attn_bo"], "fcb": ii["L0_attn_fc_b"],
        "projb": ii["L0_attn_proj_b"],
        "convw0": np.ascontiguousarray(ii["L0_ffn_conv_w"][:, 0, 0]),
        "convw1": np.ascontiguousarray(ii["L0_ffn_conv_w"][:, 0, 1]),
        "convw2": np.ascontiguousarray(ii["L0_ffn_conv_w"][:, 0, 2]),
        "identb": np.eye(128, dtype=ml_dtypes.bfloat16),
        "identf": np.eye(128, dtype=np.float32),
        "maskS": np.kron(np.eye(8, dtype=np.float32), np.ones((16, 16), np.float32)),
        "convb": ii["L0_ffn_conv_b"],
    }
    shared = {k: (np.ascontiguousarray(v.astype(np.float32, copy=False))
                  if v.dtype != ml_dtypes.bfloat16 else v)
              for k, v in shared.items()}
    nc = _get_nc()
    in_maps = [dict(shared, x=np.ascontiguousarray(x[c])) for c in range(NCORES)]
    res = run_bass_kernel_spmd(nc, in_maps, core_ids=list(range(NCORES)))
    out = np.stack([res.results[c]["y"] for c in range(NCORES)], axis=0)
    return out.astype(np.float32, copy=False)


if __name__ == "__main__":
    nc = build_program()
    print("built")
